# version 12
# baseline (speedup 1.0000x reference)
import numpy as np
from contextlib import ExitStack
import concourse.bass as bass
import concourse.mybir as mybir
from concourse.bass_utils import run_bass_kernel_spmd

F32 = mybir.dt.float32
AL = mybir.AluOpType
ACT = mybir.ActivationFunctionType
AX = mybir.AxisListType.X

MAGIC = float(3 * 2.0 ** 22)
C7 = 1.0 / 7.0
CQ = float(1.0 / 32767.0)
EPS = 1e-5
QEPS = 1e-12

D = 1024
I3 = 3072
TOK = 4096
S = 2048
NH = 16
DH = 64
NT = TOK // 128
KC = D // 128
IT = I3 // 128
TB = TOK // 512
ENGS = ("sync", "tensor", "vector", "scalar", "gpsimd")


class Fine:

    def __init__(self, sem):
        self.sem = sem
        self.n = 0

    def step(self, eng, instr):
        self.n += 1
        instr.then_inc(self.sem, 1)
        eng.wait_ge(self.sem, self.n)
        return instr


def build_nc(n_cores=8):
    nc = bass.Bass("TRN2", target_bir_lowering=False)
    x0 = nc.dram_tensor("x0", [TOK, D], F32, kind="ExternalInput").ap()
    wq = nc.dram_tensor("wq", [D, D], F32, kind="ExternalInput").ap()
    wkv = nc.dram_tensor("wkv", [D, 2 * D], F32, kind="ExternalInput").ap()
    wo = nc.dram_tensor("wo", [D, D], F32, kind="ExternalInput").ap()
    bo = nc.dram_tensor("bo", [1, D], F32, kind="ExternalInput").ap()
    wup = nc.dram_tensor("wup", [I3, D], F32, kind="ExternalInput").ap()
    wgate = nc.dram_tensor("wgate", [I3, D], F32, kind="ExternalInput").ap()
    wdown = nc.dram_tensor("wdown", [D, I3], F32, kind="ExternalInput").ap()
    identity = nc.dram_tensor("identity", [128, 128], F32, kind="ExternalInput").ap()
    onesrow = nc.dram_tensor("onesrow", [1, 128], F32, kind="ExternalInput").ap()
    negrow = nc.dram_tensor("negrow", [1, 128], F32, kind="ExternalInput").ap()
    slotm = nc.dram_tensor("slotm", [1, 16], F32, kind="ExternalInput").ap()
    out = nc.dram_tensor("out", [TOK, D], F32, kind="ExternalOutput").ap()

    qfm = nc.dram_tensor("qfm", [D, TOK], F32).ap()
    kfm = nc.dram_tensor("kfm", [D, TOK], F32).ap()
    vscr = nc.dram_tensor("vscr", [NT, 128, D], F32).ap()
    attn_fm = nc.dram_tensor("attn_fm", [2, 4, KC, 128, 512], F32).ap()
    x1scr = nc.dram_tensor("x1scr", [TOK, D], F32).ap()
    h2scr = nc.dram_tensor("h2scr", [TOK, D], F32).ap()
    BF16 = mybir.dt.bfloat16
    wupT = [nc.dram_tensor(f"wupT{j}", [IT, KC, 128, 128], BF16).ap() for j in range(2)]
    wgT = [nc.dram_tensor(f"wgT{j}", [IT, KC, 128, 128], BF16).ap() for j in range(2)]
    wdT = [nc.dram_tensor(f"wdT{j}", [KC, IT, 128, 128], BF16).ap() for j in range(2)]
    u0scr = nc.dram_tensor("u0scr", [IT, 128, TOK], F32).ap()
    g0scr = nc.dram_tensor("g0scr", [IT, 128, TOK], F32).ap()
    opscr = nc.dram_tensor("opscr", [IT, 128, TOK], F32).ap()
    ccin = [nc.dram_tensor(f"ccin{i}", [1, 16], F32) for i in range(4)]
    ccout = [nc.dram_tensor(f"ccout{i}", [1, 16], F32) for i in range(4)]

    es = ExitStack()
    with es:
        RBASE = 16512
        es.enter_context(nc.sbuf_tensor("reserve_arena", [128, 196000], mybir.dt.uint8))

        def sb(name, shape, start, dt_=F32):
            per = int(np.prod(shape[1:])) * (2 if dt_ == mybir.dt.bfloat16 else 4)
            start = start + RBASE
            assert start + per <= 212512, (name, start, per)
            return nc.alloc_sbuf_tensor_at(name, shape, dt_, offset=start)
        KB = 1024
        A0, B0, S0 = 0, 128 * KB, 182 * KB
        hfm = sb("hfm", [128, KC * TOK], A0)
        VB = sb("VB", [128, 16 * D], A0)
        ESd = [sb(f"ESd{i}", [128, 8192], A0 + (64 + 32 * i) * KB) for i in range(2)]
        woS = sb("woS", [128, KC * D], A0)
        bobc = sb("bobc", [128, D], A0 + 32 * KB)
        afm = [sb(f"afm{i}", [128, D], A0 + 36 * KB + i * 4 * KB) for i in range(2)]
        BF = mybir.dt.bfloat16
        xfmh = sb("xfmh", [128, KC * TOK], A0, BF)
        xfml = sb("xfml", [128, KC * TOK], A0 + 64 * KB, BF)
        jb = [sb(f"jb{i}", [128, TOK], A0 + i * 16 * KB) for i in range(8)]
        wdSh = sb("wdSh", [128, KC * IT * 128], A0, BF)
        wdSl = sb("wdSl", [128, KC * IT * 128], A0 + 48 * KB, BF)
        ffm = sb("ffm", [128, KC * 512], A0 + 96 * KB)
        ffnt = sb("ffnt", [128, D], A0 + 112 * KB)
        x1r = [sb(f"x1r{i}", [128, D], (A0 + 116 * KB + i * 4 * KB) if i < 2 else
                  (B0 + 24 * KB + (i - 2) * 4 * KB)) for i in range(4)]
        outt = sb("outt", [128, D], A0 + 124 * KB)
        ln_x = [sb(f"ln_x{i}", [128, D], B0 + i * 4 * KB) for i in range(2)]
        ln_h = sb("ln_h", [128, D], B0 + 8 * KB)
        sqscr = sb("sqscr", [128, D], B0 + 12 * KB)
        wst = [sb(f"wst{i}", [128, KC * 128], B0 + 4 * KB + i * 4 * KB) for i in range(2)]
        wmv = [sb(f"wmv{i}", [128, KC * 512], B0 + 12 * KB + i * 16 * KB) for i in range(2)]
        evst = sb("evst", [128, 1024], B0)
        dqin = [sb(f"dqin{i}", [128, I3], B0 + i * 12 * KB) for i in range(2)]
        dqscr = sb("dqscr", [128, I3], B0 + 24 * KB)
        dqh = sb("dqh", [128, I3], B0 + 36 * KB, BF)
        dql = sb("dql", [128, I3], B0 + 42 * KB, BF)
        dqsth = sb("dqsth", [128, I3], B0 + 24 * KB, BF)
        dqstl = sb("dqstl", [128, I3], B0 + 30 * KB, BF)
        KH = [sb(f"KH{i}", [64, S], B0 + i * 8 * KB) for i in range(2)]
        QH = [sb(f"QH{i}", [64, 512], B0 + 16 * KB + i * 2 * KB) for i in range(2)]
        zw = sb("zw", [128, 4096], B0 + 20 * KB)
        ast = [sb(f"ast{i}", [128, 512], B0 + 36 * KB + i * 2 * KB) for i in range(2)]
        rzb = sb("rzb", [64, 512], B0 + 40 * KB)
        mt16 = sb("mt16", [128, 16], B0 + 42 * KB)
        rzrow = sb("rzrow", [1, 512], B0 + 42 * KB + 512)
        zq4 = sb("zq4", [128, 4], B0 + 42 * KB + 2560)
        att_in = [sb(f"att_in{i}", [128, D], B0 + 16 * KB + i * 4 * KB) for i in range(2)]
        x0r = [sb(f"x0r{i}", [128, D], B0 + 24 * KB + i * 4 * KB) for i in range(2)]
        x1t = sb("x1t", [128, D], B0 + 32 * KB)
        h2t = sb("h2t", [128, D], B0 + 36 * KB)
        h2in = [sb(f"h2in{i}", [128, D], B0 + i * 4 * KB) for i in range(2)]
        xh = sb("xh", [128, D], B0 + 8 * KB, BF)
        xl = sb("xl", [128, D], B0 + 10 * KB, BF)
        u0j = [sb(f"u0j{i}", [128, TOK], B0 + i * 16 * KB) for i in range(2)]
        jbx = [sb(f"jbx{i}", [128, TOK], B0 + i * 16 * KB) for i in range(2)]
        wT7h = [sb(f"wT7h{i}", [128, KC * 128], B0 + 32 * KB + i * 2 * KB, BF) for i in range(2)]
        wT7l = [sb(f"wT7l{i}", [128, KC * 128], B0 + 36 * KB + i * 2 * KB, BF) for i in range(2)]
        ev7 = [sb(f"ev7_{i}", [128, 512], B0 + 40 * KB + i * 2 * KB) for i in range(2)]
        oslh = sb("oslh", [128, 8 * 512], B0, BF)
        osll = sb("osll", [128, 8 * 512], B0 + 8 * KB, BF)
        opin = [sb(f"opin{i}", [128, 512], B0 + 16 * KB + i * 2 * KB) for i in range(2)]
        sqd = sb("sqd", [128, D], B0 + 20 * KB)
        ident_s = sb("ident_s", [128, 128], S0)
        ident_b = sb("ident_b", [128, 128], S0 + 8320, BF)
        ones_s = sb("ones_s", [1, 128], S0 + 512)
        negr_s = sb("negr_s", [1, 128], S0 + 1024)
        slot_s = sb("slot_s", [1, 16], S0 + 1536)
        arv_s = sb("arv_s", [1, 16], S0 + 1600)
        arr_s = sb("arr_s", [1, 16], S0 + 1664)
        epsv = sb("epsv", [128, 1], S0 + 1728)
        stats = sb("stats", [128, 16], S0 + 1792)
        bc = sb("bc", [128, 16], S0 + 1856)
        sc = sb("sc", [1, 16], S0 + 1920)
        macc = sb("macc", [128, 4], S0 + 1984)
        mtmp = sb("mtmp", [128, 2], S0 + 2048)
        mrow = sb("mrow", [1, 128], S0 + 2112)
        negm = sb("negm", [128, 1], S0 + 2624)
        zred = sb("zred", [128, 8], S0 + 2688)
        bo_s = sb("bo_s", [1, D], S0 + 2752)
        ones_c = sb("ones_c", [128, 1], S0 + 6912)
        dqs = sb("dqs", [128, 96], S0 + 7 * KB)
        dqr = sb("dqr", [128, 96], S0 + 7 * KB + 512)

        ps = [es.enter_context(nc.psum_tensor(f"ps{i}", [128, 512], F32)) for i in range(8)]

        sems = {}
        for s_ in ("lda", "ldb", "ldg", "spa", "spb", "spg", "sha", "shb", "vv", "ss", "tt", "gg", "bar", "cc", "vf", "sf", "gf"):
            sems[s_] = es.enter_context(nc.semaphore(s_))
        block = es.enter_context(nc.Block())

        C = {k: 0 for k in sems}
        VF = Fine(sems["vf"])
        SF = Fine(sems["sf"])
        GFN = [0]
        DCTX = {}
        phases = []

        def barrier():
            C["bar"] += 5
            tgt = C["bar"]

            def mk():
                def f(e):
                    e.nop().then_inc(sems["bar"], 1)
                    e.wait_ge(sems["bar"], tgt)
                return f
            phases.append({n_: mk() for n_ in ENGS})

        def add(ph):
            phases.append(ph)
            barrier()

        def ln_stats_vec(e, xt, vv_var_tgt, ss_sq_tgt, ss_sqrt_tgt, mcol=1, sumcol=0):
            VF.step(e, e.reduce_sum(stats[:, sumcol:sumcol + 1], xt, axis=AX))
            VF.step(e, e.tensor_scalar(stats[:, mcol:mcol + 1], stats[:, sumcol:sumcol + 1],
                                       1.0 / D, None, AL.mult, AL.bypass))
            e.wait_ge(sems["ss"], ss_sq_tgt)
            VF.step(e, e.tensor_scalar(stats[:, 3:4], stats[:, 2:3], 1.0 / D, None, AL.mult, AL.bypass))
            VF.step(e, e.tensor_tensor(stats[:, 4:5], stats[:, mcol:mcol + 1], stats[:, mcol:mcol + 1], AL.mult))
            VF.step(e, e.tensor_tensor(stats[:, 5:6], stats[:, 3:4], stats[:, 4:5], AL.subtract))
            e.nop().then_inc(sems["vv"], 1)
            e.wait_ge(sems["ss"], ss_sqrt_tgt)
            VF.step(e, e.reciprocal(stats[:, 7:8], stats[:, 6:7]))
            VF.step(e, e.tensor_scalar(stats[:, 8:9], stats[:, 5:6], EPS, None, AL.add, AL.bypass))
            for _ in range(2):
                VF.step(e, e.tensor_tensor(stats[:, 9:10], stats[:, 7:8], stats[:, 7:8], AL.mult))
                VF.step(e, e.tensor_tensor(stats[:, 9:10], stats[:, 9:10], stats[:, 8:9], AL.mult))
                VF.step(e, e.tensor_scalar(stats[:, 9:10], stats[:, 9:10], -0.5, 1.5, AL.mult, AL.add))
                VF.step(e, e.tensor_tensor(stats[:, 7:8], stats[:, 7:8], stats[:, 9:10], AL.mult))

        def ln_stats_scal(e, xt, vv_xt_tgt, vv_var_tgt):
            e.wait_ge(sems["vv"], vv_xt_tgt)
            SF.step(e, e.activation(sqscr[:], xt, ACT.Square, accum_out=stats[:, 2:3]))
            e.nop().then_inc(sems["ss"], 1)
            e.wait_ge(sems["vv"], vv_var_tgt)
            SF.step(e, e.activation(stats[:, 6:7], stats[:, 5:6], ACT.Sqrt, bias=epsv[:]))
            e.nop().then_inc(sems["ss"], 1)

        def ph_consts():
            g0 = C["ldg"]
            C["ldg"] += 16 * 5
            C["vv"] += 1

            def f_sync(e):
                e.dma_start(out=ident_s[:], in_=identity).then_inc(sems["ldg"], 16)
                e.dma_start(out=ones_s[:], in_=onesrow).then_inc(sems["ldg"], 16)
                e.dma_start(out=negr_s[:], in_=negrow).then_inc(sems["ldg"], 16)
                e.dma_start(out=slot_s[:], in_=slotm).then_inc(sems["ldg"], 16)
                e.dma_start(out=bo_s[:], in_=bo).then_inc(sems["ldg"], 16)

            def f_vector(e):
                VF.step(e, e.memset(epsv[:], EPS))
                VF.step(e, e.memset(macc[:], 0.0))
                VF.step(e, e.memset(mtmp[:], 0.0))
                VF.step(e, e.memset(ones_c[:], 1.0))
                e.wait_ge(sems["ldg"], g0 + 80)
                VF.step(e, e.tensor_copy(ident_b[:], ident_s[:]))
                e.nop().then_inc(sems["vv"], 1)
            return {"sync": f_sync, "vector": f_vector}

        def ph_ln1():
            vv0, ss0, tt0 = C["vv"], C["ss"], C["tt"]
            la0, lb0 = C["lda"], C["ldb"]
            C["lda"] += 16 * (NT // 2)
            C["ldb"] += 16 * (NT // 2)
            C["vv"] += 3 * NT
            C["ss"] += 2 * NT
            C["tt"] += 2 * NT

            def lsem(t):
                return (sems["lda"], la0 + 16 * (t // 2 + 1)) if t % 2 == 0 else (sems["ldb"], lb0 + 16 * (t // 2 + 1))

            def f_sync(e):
                for t in range(NT):
                    if t >= 2:
                        e.wait_ge(sems["vv"], vv0 + 3 * (t - 2) + 2)
                        e.wait_ge(sems["ss"], ss0 + 2 * (t - 2) + 1)
                    sm_, _ = lsem(t)
                    e.dma_start(out=ln_x[t % 2][:], in_=x0[t * 128:(t + 1) * 128, :]).then_inc(sm_, 16)

            def f_vector(e):
                for t in range(NT):
                    xt = ln_x[t % 2]
                    sm_, tv_ = lsem(t)
                    e.wait_ge(sm_, tv_)
                    ln_stats_vec(e, xt[:], vv0 + 3 * t + 1, ss0 + 2 * t + 1, ss0 + 2 * t + 2)
                    VF.step(e, e.tensor_scalar(ln_h[:], xt[:], stats[:, 1:2], stats[:, 7:8],
                                               AL.subtract, AL.mult))
                    e.nop().then_inc(sems["vv"], 1)
                    for half in range(2):
                        e.wait_ge(sems["tt"], tt0 + 2 * t + half + 1)
                        dst = bass.AP(hfm, (4 * half) * TOK + t * 128, [[KC * TOK, 128], [TOK, 4], [1, 128]])
                        VF.step(e, e.tensor_copy(dst, ps[half][:]))
                    e.nop().then_inc(sems["vv"], 1)

            def f_scalar(e):
                for t in range(NT):
                    sm_, tv_ = lsem(t)
                    e.wait_ge(sm_, tv_)
                    ln_stats_scal(e, ln_x[t % 2][:], 0, vv0 + 3 * t + 1)

            def f_tensor(e):
                for t in range(NT):
                    e.wait_ge(sems["vv"], vv0 + 3 * t + 2)
                    if t >= 1:
                        e.wait_ge(sems["vv"], vv0 + 3 * t)
                    for half in range(2):
                        for j in range(4):
                            c = half * 4 + j
                            mm = e.transpose(ps[half][:, j * 128:(j + 1) * 128],
                                             ln_h[:, c * 128:(c + 1) * 128], ident_s[:])
                        mm.then_inc(sems["tt"], 1)
            return {"sync": f_sync, "vector": f_vector, "scalar": f_scalar, "tensor": f_tensor}

        def ph_qkv():
            vv0, tt0 = C["vv"], C["tt"]
            la0, lb0, g0 = C["lda"], C["ldb"], C["ldg"]
            sa0, sb0 = C["spa"], C["spb"]
            NF = 16
            NIT = NF * TB + NT * 2
            C["lda"] += 16 * (NF // 2)
            C["ldb"] += 16 * (NF // 2)
            C["ldg"] += 16 * 2
            C["tt"] += NIT
            C["vv"] += NIT
            C["spa"] += 16 * ((NIT + 1) // 2)
            C["spb"] += 16 * (NIT // 2)

            def f_sync(e):
                for ft in range(NF):
                    if ft >= 2:
                        e.wait_ge(sems["tt"], tt0 + TB * (ft - 1))
                    w_, ldw = (wq, D) if ft < 8 else (wkv, 2 * D)
                    coloff = (ft % 8) * 128
                    dst = bass.AP(wst[ft % 2], 0, [[KC * 128, 128], [128, KC], [1, 128]])
                    srcv = bass.AP(w_.tensor, w_.offset + coloff, [[ldw, 128], [128 * ldw, KC], [1, 128]])
                    sm_ = sems["lda"] if ft % 2 == 0 else sems["ldb"]
                    e.dma_start(out=dst, in_=srcv).then_inc(sm_, 16)
                for vb in range(2):
                    dst = bass.AP(wmv[vb], 0, [[KC * 512, 128], [512, KC], [1, 512]])
                    srcv = bass.AP(wkv.tensor, wkv.offset + D + vb * 512, [[2 * D, 128], [128 * 2 * D, KC], [1, 512]])
                    e.dma_start(out=dst, in_=srcv).then_inc(sems["ldg"], 16)

            def f_tensor(e):
                for ft in range(NF):
                    sm_ = sems["lda"] if ft % 2 == 0 else sems["ldb"]
                    e.wait_ge(sm_, (la0 if ft % 2 == 0 else lb0) + 16 * (ft // 2 + 1))
                    for tb in range(TB):
                        it = ft * TB + tb
                        if it >= 2:
                            e.wait_ge(sems["vv"], vv0 + it - 1)
                        p = ps[it % 2]
                        for c in range(KC):
                            mm = e.matmul(p[:], wst[ft % 2][:, c * 128:(c + 1) * 128],
                                          hfm[:, c * TOK + tb * 512: c * TOK + (tb + 1) * 512],
                                          start=(c == 0), stop=(c == KC - 1))
                        mm.then_inc(sems["tt"], 1)
                e.wait_ge(sems["ldg"], g0 + 32)
                for t in range(NT):
                    for vb in range(2):
                        it = NF * TB + t * 2 + vb
                        if it >= 2:
                            e.wait_ge(sems["vv"], vv0 + it - 1)
                        p = ps[it % 2]
                        for c in range(KC):
                            mm = e.matmul(p[:], hfm[:, c * TOK + t * 128: c * TOK + t * 128 + 128],
                                          wmv[vb][:, c * 512:(c + 1) * 512],
                                          start=(c == 0), stop=(c == KC - 1))
                        mm.then_inc(sems["tt"], 1)

            def f_vector(e):
                NFTB = NF * TB
                for it in range(NF * TB + NT * 2):
                    e.wait_ge(sems["tt"], tt0 + it + 1)
                    if it >= 2:
                        pit = it - 2
                        sm_ = sems["spa"] if pit % 2 == 0 else sems["spb"]
                        e.wait_ge(sm_, (sa0 if pit % 2 == 0 else sb0) + 16 * (pit // 2 + 1))
                    p = ps[it % 2]
                    stg = evst[:, (it % 2) * 512:(it % 2) * 512 + 512]
                    if it < NFTB and (it // TB) < 8:
                        VF.step(e, e.tensor_scalar(stg, p[:], 0.125, None, AL.mult, AL.bypass))
                    else:
                        VF.step(e, e.tensor_copy(stg, p[:]))
                    e.nop().then_inc(sems["vv"], 1)

            def f_gpsimd(e):
                NFTB = NF * TB
                for it in range(NFTB + NT * 2):
                    e.wait_ge(sems["vv"], vv0 + it + 1)
                    stg = evst[:, (it % 2) * 512:(it % 2) * 512 + 512]
                    if it < NFTB:
                        ft, tb = it // TB, it % TB
                        if ft < 8:
                            dstv = qfm[ft * 128:(ft + 1) * 128, tb * 512:(tb + 1) * 512]
                        else:
                            dstv = kfm[(ft - 8) * 128:(ft - 7) * 128, tb * 512:(tb + 1) * 512]
                    else:
                        j = it - NFTB
                        t, vb = j // 2, j % 2
                        dstv = vscr[t, :, vb * 512:(vb + 1) * 512]
                    sm_ = sems["spa"] if it % 2 == 0 else sems["spb"]
                    e.dma_start(out=dstv, in_=stg).then_inc(sm_, 16)
                e.wait_ge(sems["spa"], sa0 + 16 * ((NIT + 1) // 2))
                e.wait_ge(sems["spb"], sb0 + 16 * (NIT // 2))
            return {"sync": f_sync, "tensor": f_tensor, "vector": f_vector, "gpsimd": f_gpsimd}

        def ph_wdq():
            vv0, tt0 = C["vv"], C["tt"]
            la0, lb0, sg0 = C["lda"], C["ldb"], C["spg"]
            tiles = ([("up", r) for r in range(IT)] + [("g", r) for r in range(IT)]
                     + [("d", r) for r in range(KC)])
            ntr = {"up": 2, "g": 2, "d": 6}
            tt_before = []
            vv_before = []
            acc = tt0
            vacc = vv0
            for k, _ in tiles:
                tt_before.append(acc)
                vv_before.append(vacc)
                acc += ntr[k]
                vacc += 1 + ntr[k]
            C["lda"] += 16 * ((len(tiles) + 1) // 2)
            C["ldb"] += 16 * (len(tiles) // 2)
            C["tt"] = acc
            C["vv"] = vacc
            C["spg"] += 32 * len(tiles)

            def tsrc(kind, r):
                return {"up": wup, "g": wgate, "d": wdown}[kind][r * 128:(r + 1) * 128, :]

            def f_sync(e):
                for i, (kind, r) in enumerate(tiles):
                    if i >= 2:
                        e.wait_ge(sems["vv"], vv_before[i - 2] + 1)
                    nb = 1024 if kind != "d" else I3
                    sm_ = sems["lda"] if i % 2 == 0 else sems["ldb"]
                    e.dma_start(out=dqin[i % 2][:, 0:nb], in_=tsrc(kind, r)).then_inc(sm_, 16)

            def f_vector(e):
                for i, (kind, r) in enumerate(tiles):
                    nb = 1024 if kind != "d" else I3
                    nblk = nb // 32
                    wsb = dqin[i % 2][:, 0:nb]
                    sm_ = sems["lda"] if i % 2 == 0 else sems["ldb"]
                    e.wait_ge(sm_, (la0 if i % 2 == 0 else lb0) + 16 * (i // 2 + 1))
                    if i >= 1:
                        e.wait_ge(sems["spg"], sg0 + 32 * i)
                    VF.step(e, e.tensor_scalar(dqscr[:, 0:nb], wsb, -1.0, None, AL.mult, AL.bypass))
                    VF.step(e, e.tensor_tensor(dqscr[:, 0:nb], dqscr[:, 0:nb], wsb, AL.max))
                    width = 16
                    while width >= 1:
                        a = bass.AP(dqscr, 0, [[I3, 128], [32, nblk], [1, width]])
                        b2 = bass.AP(dqscr, width, [[I3, 128], [32, nblk], [1, width]])
                        VF.step(e, e.tensor_tensor(a, a, b2, AL.max))
                        width //= 2
                    gat = bass.AP(dqscr, 0, [[I3, 128], [32, nblk]])
                    VF.step(e, e.tensor_copy(dqs[:, 0:nblk], gat))
                    VF.step(e, e.tensor_scalar(dqs[:, 0:nblk], dqs[:, 0:nblk], C7, QEPS, AL.mult, AL.max))
                    VF.step(e, e.reciprocal(dqr[:, 0:nblk], dqs[:, 0:nblk]))
                    rrep = bass.AP(dqr, 0, [[96, 128], [1, nblk], [0, 32]])
                    srep = bass.AP(dqs, 0, [[96, 128], [1, nblk], [0, 32]])
                    VF.step(e, e.tensor_tensor(dqscr[:, 0:nb], wsb, rrep, AL.mult))
                    VF.step(e, e.tensor_scalar(dqscr[:, 0:nb], dqscr[:, 0:nb], MAGIC, MAGIC, AL.add, AL.subtract))
                    VF.step(e, e.tensor_tensor(dqscr[:, 0:nb], dqscr[:, 0:nb], srep, AL.mult))
                    VF.step(e, e.tensor_copy(dqh[:, 0:nb], dqscr[:, 0:nb]))
                    VF.step(e, e.scalar_tensor_tensor(dql[:, 0:nb], dqscr[:, 0:nb], 1.0, dqh[:, 0:nb],
                                                      AL.mult, AL.subtract))
                    e.nop().then_inc(sems["vv"], 1)
                    for g in range(ntr[kind]):
                        e.wait_ge(sems["tt"], tt_before[i] + g + 1)
                        half = kind != "d" and g == 1 or kind == "d" and g >= 3
                        if kind != "d":
                            dst = (dqsth if g == 0 else dqstl)[:, 0:1024]
                        else:
                            dst = (dqsth if g < 3 else dqstl)[:, (g % 3) * 1024:(g % 3 + 1) * 1024]
                        VF.step(e, e.tensor_copy(dst, ps[g % 2][:].bitcast(mybir.dt.bfloat16)))
                        e.nop().then_inc(sems["vv"], 1)

            def f_tensor(e):
                for i, (kind, r) in enumerate(tiles):
                    nb = 1024 if kind != "d" else I3
                    nch = nb // 128
                    e.wait_ge(sems["vv"], vv_before[i] + 1)
                    for g in range(ntr[kind]):
                        if g >= 2:
                            e.wait_ge(sems["vv"], vv_before[i] + 1 + (g - 1))
                        elif i >= 1:
                            e.wait_ge(sems["vv"], vv_before[i - 1] + 1 + ntr[tiles[i - 1][0]] - (1 - g))
                        if kind != "d":
                            src = dqh if g == 0 else dql
                            c0 = 0
                        else:
                            src = dqh if g < 3 else dql
                            c0 = (g % 3) * 8
                        pb = ps[g % 2][:].bitcast(mybir.dt.bfloat16)
                        mm = None
                        for j in range(8):
                            c = c0 + j
                            mm = e.transpose(pb[:, j * 128:(j + 1) * 128],
                                             src[:, c * 128:(c + 1) * 128], ident_b[:])
                        mm.then_inc(sems["tt"], 1)

            def f_gpsimd(e):
                for i, (kind, r) in enumerate(tiles):
                    e.wait_ge(sems["vv"], vv_before[i] + 1 + ntr[kind])
                    nb = 1024 if kind != "d" else I3
                    nch = nb // 128
                    if kind == "up":
                        basep, off = wupT, r * KC * 128 * 128
                    elif kind == "g":
                        basep, off = wgT, r * KC * 128 * 128
                    else:
                        basep, off = wdT, r * IT * 128 * 128
                    for j, stg in ((0, dqsth), (1, dqstl)):
                        dst = bass.AP(basep[j].tensor, basep[j].offset + off,
                                      [[128, 128], [128 * 128, nch], [1, 128]])
                        e.dma_start(out=dst, in_=stg[:, 0:nb]).then_inc(sems["spg"], 16)
                e.wait_ge(sems["spg"], sg0 + 32 * len(tiles))
            return {"sync": f_sync, "vector": f_vector, "tensor": f_tensor, "gpsimd": f_gpsimd}

        def ph_attn():
            vv0, ss0, tt0, gg0 = C["vv"], C["ss"], C["tt"], C["gg"]
            la0, lb0, g0, sg0 = C["lda"], C["ldb"], C["ldg"], C["spg"]
            NIT = 2 * 4 * NH

            tt_scores = {}; tt_Mtp = {}; tt_negbc = {}; tt_av = {}
            tt_zmm = {}; tt_rzbc = {}
            ttc = tt0
            ttc += 1; tt_Mtp[0] = ttc
            ttc += 1; tt_negbc[0] = ttc
            for i in range(NIT + 2):
                if i < NIT:
                    for kc in range(4):
                        ttc += 1; tt_scores[(i, kc)] = ttc
                    if i >= 1:
                        ttc += 1; tt_negbc[i] = ttc
                    for kc in range(4, 16):
                        ttc += 1; tt_scores[(i, kc)] = ttc
                if 1 <= i <= NIT:
                    ttc += 1; tt_av[i - 1] = ttc
                if 2 <= i <= NIT + 1:
                    ttc += 1; tt_zmm[i - 2] = ttc
                    ttc += 1; tt_rzbc[i - 2] = ttc
                if i + 1 <= NIT - 1:
                    ttc += 1; tt_Mtp[i + 1] = ttc
            C["tt"] = ttc
            vv_evac = {}; vv_maxred = {}; vv_m1 = {}; vv_negm = {}; vv_zw = {}
            vv_rzrow = {}; vv_rzb = {}; vv_em = {}
            vvc = vv0
            for i in range(NIT + 2):
                if i < NIT:
                    vvc += 1; vv_m1[i] = vvc
                    for kc in range(4):
                        vvc += 1; vv_evac[i * 16 + kc] = vvc
                    vvc += 1; vv_negm[i] = vvc
                    for kc in range(4, 16):
                        vvc += 1; vv_evac[i * 16 + kc] = vvc
                    vvc += 1; vv_maxred[i] = vvc
                if 1 <= i <= NIT:
                    vvc += 1; vv_zw[i - 1] = vvc
                if 2 <= i <= NIT + 1:
                    p2 = i - 2
                    vvc += 1; vv_rzrow[p2] = vvc
                    vvc += 1; vv_rzb[p2] = vvc
                    vvc += 1; vv_em[p2] = vvc
            C["vv"] = vvc
            C["ss"] += 4 * NIT
            C["gg"] += NIT
            C["ldg"] += 16 * (2 * 16)
            C["lda"] += 32 * (NIT // 2)
            C["ldb"] += 32 * (NIT // 2)
            C["spg"] += 16 * (NIT // 2)

            def hdr(i):
                return i // 64, (i // 16) % 4, i % 16

            def f_sync(e):
                for v in range(16):
                    e.dma_start(out=VB[:, v * D:(v + 1) * D], in_=vscr[v]).then_inc(sems["ldg"], 16)
                for i in range(NIT):
                    b, qb, h = hdr(i)
                    if i >= 2:
                        e.wait_ge(sems["tt"], tt_scores[(i - 2, 15)])
                    sm_ = sems["lda"] if i % 2 == 0 else sems["ldb"]
                    e.dma_start(out=KH[i % 2][:],
                                in_=kfm[h * DH:(h + 1) * DH, b * S:(b + 1) * S]).then_inc(sm_, 16)
                    e.dma_start(out=QH[i % 2][:],
                                in_=qfm[h * DH:(h + 1) * DH, b * S + qb * 512: b * S + (qb + 1) * 512]).then_inc(sm_, 16)
                    if i == 65:
                        e.wait_ge(sems["tt"], tt_av[63])
                        for v in range(16):
                            e.dma_start(out=VB[:, v * D:(v + 1) * D], in_=vscr[16 + v]).then_inc(sems["ldg"], 16)

            def f_tensor(e):
                e.transpose(ps[7][0:1, 0:128], mtmp[:, 1:2], ident_s[:]).then_inc(sems["tt"], 1)
                e.wait_ge(sems["vv"], vv_m1[0])
                e.matmul(ps[7][:, 256:257], negr_s[:, 0:128], sc[0:1, 0:1],
                         start=True, stop=True).then_inc(sems["tt"], 1)
                for i in range(NIT + 2):
                    if i < NIT:
                        sm_ = sems["lda"] if i % 2 == 0 else sems["ldb"]
                        e.wait_ge(sm_, (la0 if i % 2 == 0 else lb0) + 32 * (i // 2 + 1))
                        for kc in range(16):
                            if kc % 4 == 0 and i * 16 + kc >= 4:
                                e.wait_ge(sems["vv"], vv_evac[i * 16 + kc - 1])
                            e.matmul(ps[kc % 4][:], KH[i % 2][:, kc * 128:(kc + 1) * 128],
                                     QH[i % 2][:], start=True, stop=True).then_inc(sems["tt"], 1)
                            if kc == 3 and i >= 1:
                                e.wait_ge(sems["vv"], vv_m1[i])
                                e.wait_ge(sems["vv"], vv_negm[i - 1])
                                e.matmul(ps[7][:, 256:257], negr_s[:, 0:128], sc[0:1, 0:1],
                                         start=True, stop=True).then_inc(sems["tt"], 1)
                    if 1 <= i <= NIT:
                        p = i - 1
                        b, qb, h = hdr(p)
                        e.wait_ge(sems["ldg"], g0 + 16 * 16 * (b + 1))
                        if p >= 2:
                            e.wait_ge(sems["vv"], vv_em[p - 2])
                        for kc in range(16):
                            if kc % 4 == 0:
                                e.wait_ge(sems["ss"], ss0 + 4 * p + kc // 4 + 1)
                            mm = e.matmul(ps[4 + p % 2][0:64, :], VB[:, kc * D + h * DH: kc * D + h * DH + DH],
                                          ESd[p % 2][:, kc * 512:(kc + 1) * 512],
                                          start=(kc == 0), stop=(kc == 15))
                        mm.then_inc(sems["tt"], 1)
                    if 2 <= i <= NIT + 1:
                        p2 = i - 2
                        e.wait_ge(sems["vv"], vv_zw[p2])
                        if p2 >= 1:
                            e.wait_ge(sems["vv"], vv_rzb[p2 - 1])
                        e.matmul(ps[6][0:1, 0:512], ones_c[:], zw[:, 0:512],
                                 start=True, stop=True).then_inc(sems["tt"], 1)
                        e.wait_ge(sems["vv"], vv_rzrow[p2])
                        e.matmul(ps[6][0:64, :], ones_s[:, 0:64], rzrow[:],
                                 start=True, stop=True).then_inc(sems["tt"], 1)
                    if i + 1 <= NIT - 1:
                        e.wait_ge(sems["vv"], vv_maxred[i])
                        e.wait_ge(sems["vv"], vv_m1[i])
                        e.transpose(ps[7][0:1, 0:128], mtmp[:, i % 2:i % 2 + 1], ident_s[:]).then_inc(sems["tt"], 1)

            def f_vector(e):
                for i in range(NIT + 2):
                    if i < NIT:
                        b, qb, h = hdr(i)
                        e.wait_ge(sems["tt"], tt_Mtp[i])
                        VF.step(e, e.tensor_copy(mrow[:], ps[7][0:1, 0:128]))
                        if i >= 1:
                            e.wait_ge(sems["tt"], tt_negbc[i - 1])
                        VF.step(e, e.reduce_max(sc[0:1, 0:1], mrow[:], axis=AX))
                        e.nop().then_inc(sems["vv"], 1)
                        if i >= 2:
                            e.wait_ge(sems["tt"], tt_av[i - 2])
                            e.wait_ge(sems["gg"], gg0 + i - 1)
                        for kc in range(16):
                            e.wait_ge(sems["tt"], tt_scores[(i, kc)])
                            VF.step(e, e.tensor_scalar(ESd[i % 2][:, kc * 512:(kc + 1) * 512], ps[kc % 4][:],
                                                       1.0, -3e38, AL.mult, AL.max,
                                                       accum_out=mt16[:, kc:kc + 1]))
                            e.nop().then_inc(sems["vv"], 1)
                            if kc == 3:
                                e.wait_ge(sems["tt"], tt_negbc[i])
                                e.wait_ge(sems["ss"], ss0 + 4 * i)
                                VF.step(e, e.tensor_copy(negm[:], ps[7][:, 256:257]))
                                e.nop().then_inc(sems["vv"], 1)
                        VF.step(e, e.reduce_max(mtmp[:, i % 2:i % 2 + 1], mt16[:], axis=AX))
                        e.nop().then_inc(sems["vv"], 1)
                    if 1 <= i <= NIT:
                        p = i - 1
                        e.wait_ge(sems["gg"], gg0 + p + 1)
                        for lv in range(4):
                            VF.step(e, e.tensor_tensor(zw[:, lv * 512:(lv + 1) * 512],
                                                       zw[:, (2 * lv) * 512:(2 * lv + 1) * 512],
                                                       zw[:, (2 * lv + 1) * 512:(2 * lv + 2) * 512], AL.add))
                        for lv in range(2):
                            VF.step(e, e.tensor_tensor(zw[:, lv * 512:(lv + 1) * 512],
                                                       zw[:, (2 * lv) * 512:(2 * lv + 1) * 512],
                                                       zw[:, (2 * lv + 1) * 512:(2 * lv + 2) * 512], AL.add))
                        VF.step(e, e.tensor_tensor(zw[:, 0:512], zw[:, 0:512], zw[:, 512:1024], AL.add))
                        e.nop().then_inc(sems["vv"], 1)
                    if 2 <= i <= NIT + 1:
                        p2 = i - 2
                        b, qb, h = hdr(p2)
                        e.wait_ge(sems["tt"], tt_zmm[p2])
                        VF.step(e, e.reciprocal(rzrow[:], ps[6][0:1, 0:512]))
                        e.nop().then_inc(sems["vv"], 1)
                        e.wait_ge(sems["tt"], tt_rzbc[p2])
                        VF.step(e, e.tensor_copy(rzb[:], ps[6][0:64, :]))
                        e.nop().then_inc(sems["vv"], 1)
                        e.wait_ge(sems["tt"], tt_av[p2])
                        c = h // 2
                        gc = (b * 4 + qb) * 8 + c
                        if gc >= 2:
                            e.wait_ge(sems["spg"], sg0 + 16 * (gc - 1))
                        VF.step(e, e.tensor_tensor(ast[c % 2][(h % 2) * 64:(h % 2) * 64 + 64, :],
                                                   ps[4 + p2 % 2][0:64, :], rzb[:], AL.mult))
                        e.nop().then_inc(sems["vv"], 1)

            def f_scalar(e):
                for p in range(NIT):
                    e.wait_ge(sems["vv"], vv_negm[p])
                    for q4 in range(4):
                        e.wait_ge(sems["vv"], vv_evac[p * 16 + q4 * 4 + 3])
                        SF.step(e, e.activation(ESd[p % 2][:, q4 * 2048:(q4 + 1) * 2048],
                                                ESd[p % 2][:, q4 * 2048:(q4 + 1) * 2048],
                                                ACT.Exp, bias=negm[:], scale=1.0))
                        e.nop().then_inc(sems["ss"], 1)

            def f_gpsimd(e):
                gfn = GFN

                def spill(sp):
                    b, qb, h = hdr(sp)
                    e.wait_ge(sems["vv"], vv_em[sp])
                    gc = (b * 4 + qb) * 8 + h // 2
                    if gc >= 1:
                        e.wait_ge(sems["spg"], sg0 + 16 * gc)
                    c = h // 2
                    e.dma_start(out=attn_fm[b, qb, c], in_=ast[c % 2][:]).then_inc(sems["spg"], 16)
                for p in range(NIT):
                    b, qb, h = hdr(p)
                    if p >= 1:
                        e.wait_ge(sems["tt"], tt_zmm[p - 1])
                        e.wait_ge(sems["vv"], vv_zw[p - 1])
                    for lv in range(8):
                        if lv % 2 == 0:
                            e.wait_ge(sems["ss"], ss0 + 4 * p + lv // 2 + 1)
                        gfn[0] += 1
                        e.tensor_tensor(zw[:, lv * 512:(lv + 1) * 512],
                                        ESd[p % 2][:, (2 * lv) * 512:(2 * lv + 1) * 512],
                                        ESd[p % 2][:, (2 * lv + 1) * 512:(2 * lv + 2) * 512],
                                        AL.add).then_inc(sems["gf"], 1)
                        e.wait_ge(sems["gf"], gfn[0])
                    e.nop().then_inc(sems["gg"], 1)
                    if p >= 1 and hdr(p - 1)[2] % 2 == 1:
                        spill(p - 1)
                if hdr(NIT - 1)[2] % 2 == 1:
                    spill(NIT - 1)
                e.wait_ge(sems["spg"], sg0 + 16 * (NIT // 2))
            return {"sync": f_sync, "tensor": f_tensor, "scalar": f_scalar,
                    "vector": f_vector, "gpsimd": f_gpsimd}

        def ph_wo():
            vv0, ss0, tt0 = C["vv"], C["ss"], C["tt"]
            la0, lb0, g0, sa0, sb0 = C["lda"], C["ldb"], C["ldg"], C["spa"], C["spb"]
            C["ldg"] += 16
            C["lda"] += 32 * (NT // 2)
            C["ldb"] += 32 * (NT // 2)
            C["tt"] += 1 + 2 * NT
            C["vv"] += 1 + 5 * NT
            C["ss"] += 4 * NT
            C["spa"] += 16 * NT
            C["spb"] += 16 * NT

            def f_sync(e):
                dst = bass.AP(woS, 0, [[KC * D, 128], [D, KC], [1, D]])
                srcv = bass.AP(wo.tensor, wo.offset, [[D, 128], [128 * D, KC], [1, D]])
                e.dma_start(out=dst, in_=srcv).then_inc(sems["ldg"], 16)
                for t in range(NT):
                    if t >= 2:
                        e.wait_ge(sems["tt"], tt0 + 1 + 2 * (t - 2) + 2)
                        e.wait_ge(sems["vv"], vv0 + 1 + 5 * (t - 2) + 3)
                    sm_ = sems["lda"] if t % 2 == 0 else sems["ldb"]
                    tb_, qb_, j_ = t // 16, (t % 16) // 4, t % 4
                    afsrc = bass.AP(attn_fm.tensor,
                                    attn_fm.offset + ((tb_ * 4 + qb_) * KC) * 128 * 512 + j_ * 128,
                                    [[512, 128], [128 * 512, KC], [1, 128]])
                    e.dma_start(out=afm[t % 2][:], in_=afsrc).then_inc(sm_, 16)
                    e.dma_start(out=x0r[t % 2][:], in_=x0[t * 128:(t + 1) * 128, :]).then_inc(sm_, 16)

            def f_tensor(e):
                e.wait_ge(sems["ldg"], g0 + 16)
                mm = e.matmul(ps[2][:, 0:512], ones_s[:, 0:128], bo_s[:, 0:512], start=True, stop=True)
                mm = e.matmul(ps[3][:, 0:512], ones_s[:, 0:128], bo_s[:, 512:1024], start=True, stop=True)
                mm.then_inc(sems["tt"], 1)
                for t in range(NT):
                    sm_ = sems["lda"] if t % 2 == 0 else sems["ldb"]
                    e.wait_ge(sm_, (la0 if t % 2 == 0 else lb0) + 32 * (t // 2 + 1))
                    if t >= 1:
                        e.wait_ge(sems["vv"], vv0 + 1 + 5 * (t - 1) + 1)
                    for db in range(2):
                        p = ps[4 + db]
                        for c in range(KC):
                            mm = e.matmul(p[:], afm[t % 2][:, c * 128:(c + 1) * 128],
                                          woS[:, c * D + db * 512: c * D + (db + 1) * 512],
                                          start=(c == 0), stop=(c == KC - 1))
                        mm.then_inc(sems["tt"], 1)

            def f_vector(e):
                e.wait_ge(sems["tt"], tt0 + 1)
                VF.step(e, e.tensor_copy(bobc[:, 0:512], ps[2][:, 0:512]))
                VF.step(e, e.tensor_copy(bobc[:, 512:1024], ps[3][:, 0:512]))
                e.nop().then_inc(sems["vv"], 1)
                for t in range(NT):
                    e.wait_ge(sems["tt"], tt0 + 1 + 2 * t + 1)
                    if t >= 1:
                        e.wait_ge(sems["spa"], sa0 + 16 * t)
                        e.wait_ge(sems["ss"], ss0 + 4 * (t - 1) + 3)
                    VF.step(e, e.scalar_tensor_tensor(x1t[:, 0:512], ps[4][:], 1.0, bobc[:, 0:512],
                                                      AL.mult, AL.add, accum_out=stats[:, 0:1]))
                    e.wait_ge(sems["tt"], tt0 + 1 + 2 * t + 2)
                    VF.step(e, e.scalar_tensor_tensor(x1t[:, 512:1024], ps[5][:], 1.0, bobc[:, 512:1024],
                                                      AL.mult, AL.add, accum_out=stats[:, 12:13]))
                    VF.step(e, e.tensor_tensor(stats[:, 0:1], stats[:, 0:1], stats[:, 12:13], AL.add))
                    e.nop().then_inc(sems["vv"], 1)
                    VF.step(e, e.tensor_scalar(stats[:, 1:2], stats[:, 0:1], 1.0 / D, None, AL.mult, AL.bypass))
                    e.wait_ge(sems["ss"], ss0 + 4 * t + 1)
                    VF.step(e, e.tensor_scalar(stats[:, 3:4], stats[:, 2:3], 1.0 / D, None, AL.mult, AL.bypass))
                    VF.step(e, e.tensor_tensor(stats[:, 4:5], stats[:, 1:2], stats[:, 1:2], AL.mult))
                    VF.step(e, e.tensor_tensor(stats[:, 5:6], stats[:, 3:4], stats[:, 4:5], AL.subtract))
                    e.nop().then_inc(sems["vv"], 1)
                    e.wait_ge(sems["ss"], ss0 + 4 * t + 2)
                    VF.step(e, e.reciprocal(stats[:, 7:8], stats[:, 6:7]))
                    VF.step(e, e.tensor_scalar(stats[:, 8:9], stats[:, 5:6], EPS, None, AL.add, AL.bypass))
                    for _ in range(2):
                        VF.step(e, e.tensor_tensor(stats[:, 9:10], stats[:, 7:8], stats[:, 7:8], AL.mult))
                        VF.step(e, e.tensor_tensor(stats[:, 9:10], stats[:, 9:10], stats[:, 8:9], AL.mult))
                        VF.step(e, e.tensor_scalar(stats[:, 9:10], stats[:, 9:10], -0.5, 1.5, AL.mult, AL.add))
                        VF.step(e, e.tensor_tensor(stats[:, 7:8], stats[:, 7:8], stats[:, 9:10], AL.mult))
                    VF.step(e, e.tensor_scalar(x1t[:], x1t[:], stats[:, 1:2], stats[:, 7:8], AL.subtract, AL.mult))
                    sm_ = sems["lda"] if t % 2 == 0 else sems["ldb"]
                    e.wait_ge(sm_, (la0 if t % 2 == 0 else lb0) + 32 * (t // 2 + 1))
                    VF.step(e, e.scalar_tensor_tensor(x1t[:], x1t[:], 1.0, x0r[t % 2][:],
                                                      AL.mult, AL.add, accum_out=stats[:, 10:11]))
                    e.nop().then_inc(sems["vv"], 1)
                    VF.step(e, e.tensor_scalar(stats[:, 11:12], stats[:, 10:11], 1.0 / D, None, AL.mult, AL.bypass))
                    e.wait_ge(sems["ss"], ss0 + 4 * t + 3)
                    VF.step(e, e.tensor_scalar(stats[:, 13:14], stats[:, 12:13], 1.0 / D, None, AL.mult, AL.bypass))
                    VF.step(e, e.tensor_tensor(stats[:, 14:15], stats[:, 11:12], stats[:, 11:12], AL.mult))
                    VF.step(e, e.tensor_tensor(stats[:, 5:6], stats[:, 13:14], stats[:, 14:15], AL.subtract))
                    e.nop().then_inc(sems["vv"], 1)
                    e.wait_ge(sems["ss"], ss0 + 4 * t + 4)
                    VF.step(e, e.reciprocal(stats[:, 7:8], stats[:, 6:7]))
                    VF.step(e, e.tensor_scalar(stats[:, 8:9], stats[:, 5:6], EPS, None, AL.add, AL.bypass))
                    for _ in range(2):
                        VF.step(e, e.tensor_tensor(stats[:, 9:10], stats[:, 7:8], stats[:, 7:8], AL.mult))
                        VF.step(e, e.tensor_tensor(stats[:, 9:10], stats[:, 9:10], stats[:, 8:9], AL.mult))
                        VF.step(e, e.tensor_scalar(stats[:, 9:10], stats[:, 9:10], -0.5, 1.5, AL.mult, AL.add))
                        VF.step(e, e.tensor_tensor(stats[:, 7:8], stats[:, 7:8], stats[:, 9:10], AL.mult))
                    if t >= 1:
                        e.wait_ge(sems["spb"], sb0 + 16 * t)
                    VF.step(e, e.tensor_scalar(h2t[:], x1t[:], stats[:, 11:12], stats[:, 7:8], AL.subtract, AL.mult))
                    VF.step(e, e.reduce_max(mtmp[:, 0:1], h2t[:], axis=AX, apply_absolute_value=True))
                    VF.step(e, e.tensor_tensor(macc[:, 0:1], macc[:, 0:1], mtmp[:, 0:1], AL.max))
                    e.nop().then_inc(sems["vv"], 1)

            def f_scalar(e):
                for t in range(NT):
                    e.wait_ge(sems["vv"], vv0 + 1 + 5 * t + 1)
                    SF.step(e, e.activation(sqscr[:], x1t[:], ACT.Square, accum_out=stats[:, 2:3]))
                    e.nop().then_inc(sems["ss"], 1)
                    e.wait_ge(sems["vv"], vv0 + 1 + 5 * t + 2)
                    SF.step(e, e.activation(stats[:, 6:7], stats[:, 5:6], ACT.Sqrt, bias=epsv[:]))
                    e.nop().then_inc(sems["ss"], 1)
                    e.wait_ge(sems["vv"], vv0 + 1 + 5 * t + 3)
                    SF.step(e, e.activation(sqscr[:], x1t[:], ACT.Square, accum_out=stats[:, 12:13]))
                    e.nop().then_inc(sems["ss"], 1)
                    e.wait_ge(sems["vv"], vv0 + 1 + 5 * t + 4)
                    SF.step(e, e.activation(stats[:, 6:7], stats[:, 5:6], ACT.Sqrt, bias=epsv[:]))
                    e.nop().then_inc(sems["ss"], 1)

            def f_gpsimd(e):
                for t in range(NT):
                    e.wait_ge(sems["vv"], vv0 + 1 + 5 * t + 3)
                    e.dma_start(out=x1scr[t * 128:(t + 1) * 128, :], in_=x1t[:]).then_inc(sems["spa"], 16)
                    e.wait_ge(sems["vv"], vv0 + 1 + 5 * t + 5)
                    e.dma_start(out=h2scr[t * 128:(t + 1) * 128, :], in_=h2t[:]).then_inc(sems["spb"], 16)
                e.wait_ge(sems["spa"], sa0 + 16 * NT)
                e.wait_ge(sems["spb"], sb0 + 16 * NT)
            return {"sync": f_sync, "tensor": f_tensor, "vector": f_vector,
                    "scalar": f_scalar, "gpsimd": f_gpsimd}



        def ph_ar(idx, cols, slots, chain, nchain, bcast_cols):
            vv0, tt0, gg0, cc0 = C["vv"], C["tt"], C["gg"], C["cc"]
            sg0 = C["spg"]
            C["tt"] += 2
            C["vv"] += 3
            C["gg"] += 1
            C["spg"] += 32
            C["cc"] += 1

            def f_tensor(e):
                for i, col in enumerate(cols):
                    mm = e.transpose(ps[2][0:1, i * 128:(i + 1) * 128], macc[:, col:col + 1], ident_s[:])
                mm.then_inc(sems["tt"], 1)
                e.wait_ge(sems["vv"], vv0 + 2)
                e.matmul(ps[3][:, 0:bcast_cols], ones_s[:, 0:128], sc[0:1, 0:bcast_cols],
                         start=True, stop=True).then_inc(sems["tt"], 1)

            def f_vector(e):
                e.wait_ge(sems["tt"], tt0 + 1)
                VF.step(e, e.memset(arv_s[:], 0.0))
                for i, col in enumerate(cols):
                    VF.step(e, e.tensor_copy(mrow[:], ps[2][0:1, i * 128:(i + 1) * 128]))
                    VF.step(e, e.reduce_max(sc[0:1, 8 + i:9 + i], mrow[:], axis=AX))
                    VF.step(e, e.tensor_scalar(arv_s[0:1, slots[i]:slots[i] + 8],
                                               slot_s[0:1, slots[i]:slots[i] + 8],
                                               sc[0:1, 8 + i:9 + i], None, AL.mult, AL.bypass))
                e.nop().then_inc(sems["vv"], 1)
                e.wait_ge(sems["gg"], gg0 + 1)
                chain(e)
                e.nop().then_inc(sems["vv"], 1)
                e.wait_ge(sems["tt"], tt0 + 2)
                VF.step(e, e.tensor_copy(bc[:, 0:bcast_cols], ps[3][:, 0:bcast_cols]))
                e.nop().then_inc(sems["vv"], 1)

            def f_gpsimd(e):
                e.wait_ge(sems["vv"], vv0 + 1)
                e.dma_start(out=ccin[idx].ap(), in_=arv_s[:]).then_inc(sems["spg"], 16)
                e.wait_ge(sems["spg"], sg0 + 16)
                e.collective_compute(
                    "AllReduce", AL.add,
                    replica_groups=[list(range(n_cores))],
                    ins=[ccin[idx].ap().opt()],
                    outs=[ccout[idx].ap().opt()],
                ).then_inc(sems["cc"], 1)
                e.wait_ge(sems["cc"], cc0 + 1)
                e.dma_start(out=arr_s[:], in_=ccout[idx].ap()).then_inc(sems["spg"], 16)
                e.wait_ge(sems["spg"], sg0 + 32)
                e.nop().then_inc(sems["gg"], 1)
            return {"tensor": f_tensor, "vector": f_vector, "gpsimd": f_gpsimd}

        def ph_x():
            vv0, tt0 = C["vv"], C["tt"]
            la0, lb0 = C["lda"], C["ldb"]
            C["lda"] += 16 * (NT // 2)
            C["ldb"] += 16 * (NT // 2)
            C["vv"] += 2 * NT
            C["tt"] += 2 * NT

            def f_sync(e):
                for t in range(NT):
                    if t >= 2:
                        e.wait_ge(sems["vv"], vv0 + 2 * (t - 2) + 2)
                    sm_ = sems["lda"] if t % 2 == 0 else sems["ldb"]
                    e.dma_start(out=h2in[t % 2][:], in_=h2scr[t * 128:(t + 1) * 128, :]).then_inc(sm_, 16)

            def f_vector(e):
                for t in range(NT):
                    sm_ = sems["lda"] if t % 2 == 0 else sems["ldb"]
                    e.wait_ge(sm_, (la0 if t % 2 == 0 else lb0) + 16 * (t // 2 + 1))
                    VF.step(e, e.tensor_scalar(h2in[t % 2][:], h2in[t % 2][:], bc[:, 0:1], MAGIC, AL.mult, AL.add))
                    VF.step(e, e.tensor_scalar(xh[:], h2in[t % 2][:], MAGIC, None, AL.subtract, AL.bypass))
                    VF.step(e, e.tensor_scalar(h2in[t % 2][:], h2in[t % 2][:], MAGIC, None, AL.subtract, AL.bypass))
                    VF.step(e, e.scalar_tensor_tensor(xl[:], h2in[t % 2][:], 1.0, xh[:],
                                                      AL.mult, AL.subtract))
                    e.nop().then_inc(sems["vv"], 1)
                    for half, (dstt, _) in enumerate(((xfmh, 0), (xfml, 0))):
                        e.wait_ge(sems["tt"], tt0 + 2 * t + half + 1)
                        dst = bass.AP(dstt, t * 128, [[KC * TOK, 128], [TOK, KC], [1, 128]])
                        VF.step(e, e.tensor_copy(dst, ps[half][:].bitcast(mybir.dt.bfloat16)))
                    e.nop().then_inc(sems["vv"], 1)

            def f_tensor(e):
                for t in range(NT):
                    e.wait_ge(sems["vv"], vv0 + 2 * t + 1)
                    if t >= 1:
                        e.wait_ge(sems["vv"], vv0 + 2 * t)
                    for half, src in enumerate((xh, xl)):
                        pb = ps[half][:].bitcast(mybir.dt.bfloat16)
                        for c in range(KC):
                            mm = e.transpose(pb[:, c * 128:(c + 1) * 128],
                                             src[:, c * 128:(c + 1) * 128], ident_b[:])
                        mm.then_inc(sems["tt"], 1)
            return {"sync": f_sync, "vector": f_vector, "tensor": f_tensor}

        def ph_upgate():
            vv0, ss0, tt0 = C["vv"], C["ss"], C["tt"]
            la0, lb0, sa0, sb0 = C["lda"], C["ldb"], C["spa"], C["spb"]
            mats = [("u", wupT, u0scr, 1), ("g", wgT, g0scr, 2)]
            NI = 2 * IT
            C["lda"] += 32 * (NI // 2)
            C["ldb"] += 32 * (NI // 2)
            C["tt"] += NI * TB
            C["ss"] += NI * TB
            C["vv"] += NI * (TB * 2 + 1)
            C["spa"] += 16 * (NI // 2)
            C["spb"] += 16 * (NI // 2)

            def f_sync(e):
                for mi, (kind, wsrc, dscr, mcol) in enumerate(mats):
                    for r in range(IT):
                        i = mi * IT + r
                        if i >= 2:
                            e.wait_ge(sems["tt"], tt0 + TB * (i - 1))
                        sm_ = sems["lda"] if i % 2 == 0 else sems["ldb"]
                        for j, dstw in ((0, wT7h), (1, wT7l)):
                            e.dma_start(out=dstw[i % 2][:],
                                        in_=bass.AP(wsrc[j].tensor, wsrc[j].offset + r * KC * 128 * 128,
                                                    [[128, 128], [128 * 128, KC], [1, 128]])).then_inc(sm_, 16)

            def f_tensor(e):
                for mi, (kind, wsrc, dscr, mcol) in enumerate(mats):
                    for r in range(IT):
                        i = mi * IT + r
                        sm_ = sems["lda"] if i % 2 == 0 else sems["ldb"]
                        e.wait_ge(sm_, (la0 if i % 2 == 0 else lb0) + 32 * (i // 2 + 1))
                        for tb in range(TB):
                            g = i * TB + tb
                            if g >= 2:
                                e.wait_ge(sems["ss"], ss0 + g - 1)
                            p = ps[g % 2]
                            for c in range(KC):
                                xs_h = xfmh[:, c * TOK + tb * 512: c * TOK + (tb + 1) * 512]
                                xs_l = xfml[:, c * TOK + tb * 512: c * TOK + (tb + 1) * 512]
                                e.matmul(p[:], wT7h[i % 2][:, c * 128:(c + 1) * 128], xs_h,
                                         start=(c == 0), stop=False)
                                e.matmul(p[:], wT7l[i % 2][:, c * 128:(c + 1) * 128], xs_h,
                                         start=False, stop=False)
                                mm = e.matmul(p[:], wT7h[i % 2][:, c * 128:(c + 1) * 128], xs_l,
                                              start=False, stop=(c == KC - 1))
                            mm.then_inc(sems["tt"], 1)

            def f_scalar(e):
                for i in range(NI):
                    if i >= 2:
                        pit = i - 2
                        sm_ = sems["spa"] if pit % 2 == 0 else sems["spb"]
                        e.wait_ge(sm_, (sa0 if pit % 2 == 0 else sb0) + 16 * (pit // 2 + 1))
                    for tb in range(TB):
                        g = i * TB + tb
                        e.wait_ge(sems["tt"], tt0 + g + 1)
                        SF.step(e, e.activation(u0j[i % 2][:, tb * 512:(tb + 1) * 512], ps[g % 2][:],
                                                ACT.Copy, bias=0.0, scale=1.0))
                        e.nop().then_inc(sems["ss"], 1)

            def f_vector(e):
                for mi, (kind, wsrc, dscr, mcol) in enumerate(mats):
                    for r in range(IT):
                        i = mi * IT + r
                        for tb in range(TB):
                            g = i * TB + tb
                            e.wait_ge(sems["ss"], ss0 + g + 1)
                            VF.step(e, e.reduce_max(mtmp[:, 1:2], u0j[i % 2][:, tb * 512:(tb + 1) * 512],
                                                    axis=AX, apply_absolute_value=True))
                            e.nop().then_inc(sems["vv"], 1)
                            VF.step(e, e.tensor_tensor(macc[:, mcol:mcol + 1], macc[:, mcol:mcol + 1],
                                                       mtmp[:, 1:2], AL.max))
                            e.nop().then_inc(sems["vv"], 1)
                        e.nop().then_inc(sems["vv"], 1)

            def f_gpsimd(e):
                for mi, (kind, wsrc, dscr, mcol) in enumerate(mats):
                    for r in range(IT):
                        i = mi * IT + r
                        e.wait_ge(sems["vv"], vv0 + (TB * 2 + 1) * (i + 1))
                        sm_ = sems["spa"] if i % 2 == 0 else sems["spb"]
                        e.dma_start(out=dscr[r], in_=u0j[i % 2][:]).then_inc(sm_, 16)
                e.wait_ge(sems["spa"], sa0 + 16 * (NI // 2))
                e.wait_ge(sems["spb"], sb0 + 16 * (NI // 2))
            return {"sync": f_sync, "tensor": f_tensor, "scalar": f_scalar,
                    "vector": f_vector, "gpsimd": f_gpsimd}

        def ph_ffnA():
            vv0, ss0 = C["vv"], C["ss"]
            ld0 = (C["lda"], C["ldb"], C["ldg"])
            ldn = ("lda", "ldb", "ldg")
            C["lda"] += 16 * 8
            C["ldb"] += 16 * 8
            C["ldg"] += 16 * 8
            C["ss"] += 2 * IT
            C["vv"] += 2 * IT
            ss_p1g = {r: ss0 + 2 * r + 1 for r in range(IT)}
            ss_sp1 = {r: ss0 + 2 * r + 2 for r in range(IT)}
            vv_gq = {}; vv_dn = {}
            vvc = vv0
            for r in range(IT):
                vvc += 1; vv_gq[r] = vvc
                if r >= 1:
                    vvc += 1; vv_dn[r - 1] = vvc
            vvc += 1; vv_dn[IT - 1] = vvc
            assert vvc == vv0 + 2 * IT

            def f_sync(e):
                for r in range(IT):
                    if r >= 3:
                        e.wait_ge(sems["vv"], vv_dn[r - 3])
                    e.dma_start(out=jb[r % 3][:], in_=g0scr[r]).then_inc(sems[ldn[r % 3]], 16)

            def f_scalar(e):
                for r in range(IT):
                    e.wait_ge(sems[ldn[r % 3]], ld0[r % 3] + 16 * (r // 3 + 1))
                    if r >= 3:
                        e.wait_ge(sems["vv"], vv_dn[r - 3])
                    SF.step(e, e.activation(jb[3 + r % 3][:], jb[r % 3][:], ACT.Copy,
                                            bias=MAGIC, scale=bc[:, 3:4]))
                    e.nop().then_inc(sems["ss"], 1)
                    e.wait_ge(sems["vv"], vv_gq[r])
                    if r >= 2:
                        e.wait_ge(sems["vv"], vv_dn[r - 2])
                    SF.step(e, e.activation(jb[6 + r % 2][:], jb[3 + r % 3][:], ACT.Sigmoid))
                    SF.step(e, e.activation(jb[6 + r % 2][:], jb[6 + r % 2][:], ACT.Copy,
                                            bias=MAGIC, scale=65536.0))
                    e.nop().then_inc(sems["ss"], 1)

            def f_vector(e):
                def tail(p):
                    e.wait_ge(sems["ss"], ss_sp1[p])
                    VF.step(e, e.tensor_scalar(jb[6 + p % 2][:], jb[6 + p % 2][:], MAGIC,
                                               65535.0, AL.subtract, AL.min))
                    VF.step(e, e.tensor_tensor(jb[6 + p % 2][:], jb[6 + p % 2][:],
                                               jb[3 + p % 3][:], AL.mult))
                    VF.step(e, e.reduce_max(mtmp[:, 1:2], jb[6 + p % 2][:], axis=AX,
                                            apply_absolute_value=True))
                    VF.step(e, e.tensor_tensor(macc[:, 3:4], macc[:, 3:4], mtmp[:, 1:2], AL.max))
                    e.nop().then_inc(sems["vv"], 1)
                for r in range(IT):
                    e.wait_ge(sems["ss"], ss_p1g[r])
                    VF.step(e, e.tensor_scalar(jb[3 + r % 3][:], jb[3 + r % 3][:], MAGIC,
                                               bc[:, 4:5], AL.subtract, AL.mult))
                    e.nop().then_inc(sems["vv"], 1)
                    if r >= 1:
                        tail(r - 1)
                tail(IT - 1)
            return {"sync": f_sync, "scalar": f_scalar, "vector": f_vector}

        def ph_ffnB():
            vv0, ss0 = C["vv"], C["ss"]
            sa0, sb0 = C["spa"], C["spb"]
            ld0 = (C["lda"], C["ldb"], C["ldg"])
            ldn = ("lda", "ldb", "ldg")
            C["lda"] += 32 * 8
            C["ldb"] += 32 * 8
            C["ldg"] += 32 * 8
            C["ss"] += 4 * IT
            C["vv"] += 3 * IT
            C["spa"] += 16 * (IT - IT // 2)
            C["spb"] += 16 * (IT // 2)
            ss_p1g = {}; ss_p1u = {}; ss_sp1 = {}; ss_p1t = {}
            ssc = ss0
            for r in range(IT):
                ssc += 1; ss_p1g[r] = ssc
                ssc += 1; ss_p1u[r] = ssc
                ssc += 1; ss_sp1[r] = ssc
                if r >= 1:
                    ssc += 1; ss_p1t[r - 1] = ssc
            ssc += 1; ss_p1t[IT - 1] = ssc
            assert ssc == ss0 + 4 * IT
            vv_gq = {}; vv_t0 = {}; vv_op = {}
            vvc = vv0
            for r in range(IT + 2):
                if r < IT:
                    vvc += 1; vv_gq[r] = vvc
                if 1 <= r < IT + 1:
                    vvc += 1; vv_t0[r - 1] = vvc
                if r >= 2:
                    vvc += 1; vv_op[r - 2] = vvc
            assert vvc == vv0 + 3 * IT
            G3 = [jb[0], jb[1], jbx[0]]
            U3 = [jb[2], jb[3], jbx[1]]

            def f_sync(e):
                for r in range(IT):
                    if r >= 3:
                        e.wait_ge(sems["vv"], vv_op[r - 3])
                        sp_ = sems["spa"] if (r - 3) % 2 == 0 else sems["spb"]
                        e.wait_ge(sp_, (sa0 if (r - 3) % 2 == 0 else sb0) + 16 * ((r - 3) // 2 + 1))
                    sm_ = sems[ldn[r % 3]]
                    e.dma_start(out=G3[r % 3][:], in_=g0scr[r]).then_inc(sm_, 16)
                    e.dma_start(out=U3[r % 3][:], in_=u0scr[r]).then_inc(sm_, 16)

            def f_scalar(e):
                for r in range(IT):
                    e.wait_ge(sems[ldn[r % 3]], ld0[r % 3] + 32 * (r // 3 + 1))
                    if r >= 2:
                        e.wait_ge(sems["vv"], vv_t0[r - 2])
                    SF.step(e, e.activation(jb[4 + r % 2][:], G3[r % 3][:], ACT.Copy,
                                            bias=MAGIC, scale=bc[:, 3:4]))
                    e.nop().then_inc(sems["ss"], 1)
                    SF.step(e, e.activation(U3[r % 3][:], U3[r % 3][:], ACT.Copy,
                                            bias=MAGIC, scale=bc[:, 1:2]))
                    e.nop().then_inc(sems["ss"], 1)
                    e.wait_ge(sems["vv"], vv_gq[r])
                    if r >= 2:
                        e.wait_ge(sems["vv"], vv_op[r - 2])
                    SF.step(e, e.activation(jb[6 + r % 2][:], jb[4 + r % 2][:], ACT.Sigmoid))
                    SF.step(e, e.activation(jb[6 + r % 2][:], jb[6 + r % 2][:], ACT.Copy,
                                            bias=MAGIC, scale=65536.0))
                    e.nop().then_inc(sems["ss"], 1)
                    if r >= 1:
                        e.wait_ge(sems["vv"], vv_t0[r - 1])
                        SF.step(e, e.activation(jb[6 + (r - 1) % 2][:], jb[6 + (r - 1) % 2][:],
                                                ACT.Copy, bias=MAGIC, scale=bc[:, 5:6]))
                        e.nop().then_inc(sems["ss"], 1)
                e.wait_ge(sems["vv"], vv_t0[IT - 1])
                SF.step(e, e.activation(jb[6 + (IT - 1) % 2][:], jb[6 + (IT - 1) % 2][:],
                                        ACT.Copy, bias=MAGIC, scale=bc[:, 5:6]))
                e.nop().then_inc(sems["ss"], 1)

            def f_vector(e):
                def tail1(p):
                    e.wait_ge(sems["ss"], ss_sp1[p])
                    VF.step(e, e.tensor_scalar(jb[6 + p % 2][:], jb[6 + p % 2][:], MAGIC,
                                               65535.0, AL.subtract, AL.min))
                    VF.step(e, e.tensor_tensor(jb[6 + p % 2][:], jb[6 + p % 2][:],
                                               jb[4 + p % 2][:], AL.mult))
                    e.nop().then_inc(sems["vv"], 1)
                    e.wait_ge(sems["ss"], ss_p1u[p])
                    VF.step(e, e.tensor_scalar(U3[p % 3][:], U3[p % 3][:], MAGIC,
                                               bc[:, 2:3], AL.subtract, AL.mult))

                def tail2(p):
                    e.wait_ge(sems["ss"], ss_p1t[p])
                    VF.step(e, e.scalar_tensor_tensor(G3[p % 3][:], jb[6 + p % 2][:], MAGIC,
                                                      U3[p % 3][:], AL.subtract, AL.mult))
                    VF.step(e, e.reduce_max(mtmp[:, 1:2], G3[p % 3][:], axis=AX,
                                            apply_absolute_value=True))
                    VF.step(e, e.tensor_tensor(macc[:, 3:4], macc[:, 3:4], mtmp[:, 1:2], AL.max))
                    e.nop().then_inc(sems["vv"], 1)
                for r in range(IT + 2):
                    if r < IT:
                        e.wait_ge(sems["ss"], ss_p1g[r])
                        VF.step(e, e.tensor_scalar(jb[4 + r % 2][:], jb[4 + r % 2][:], MAGIC,
                                                   bc[:, 4:5], AL.subtract, AL.mult))
                        e.nop().then_inc(sems["vv"], 1)
                    if 1 <= r < IT + 1:
                        tail1(r - 1)
                    if r >= 2:
                        tail2(r - 2)

            def f_gpsimd(e):
                for r in range(IT):
                    e.wait_ge(sems["vv"], vv_op[r])
                    sp_ = sems["spa"] if r % 2 == 0 else sems["spb"]
                    if r >= 2:
                        e.wait_ge(sp_, (sa0 if r % 2 == 0 else sb0) + 16 * (r // 2))
                    e.dma_start(out=opscr[r], in_=G3[r % 3][:]).then_inc(sp_, 16)
                e.wait_ge(sems["spa"], sa0 + 16 * (IT - IT // 2))
                e.wait_ge(sems["spb"], sb0 + 16 * (IT // 2))
            return {"sync": f_sync, "scalar": f_scalar, "vector": f_vector, "gpsimd": f_gpsimd}

        def ph_down():
            vv0, ss0, tt0 = C["vv"], C["ss"], C["tt"]
            la0, lb0, g0, sg0 = C["lda"], C["ldb"], C["ldg"], C["spg"]
            RNG = 8
            C["lda"] += 16 * 12 * TB
            C["ldb"] += 16 * 12 * TB
            C["ldg"] += 16 * (2 + 4 * TB)
            DCTX["wds_ldg"] = g0 + 32
            C["spg"] += 16 * 4 * TB
            tt_mm = {}; tt_tr = {}
            ttc = tt0
            for tb in range(TB):
                for r in range(IT):
                    ttc += 1; tt_mm[(tb, r)] = ttc
                for j in range(4):
                    ttc += 1; tt_tr[(tb, j)] = ttc
            C["tt"] = ttc
            ss_p1o = {}; ss_sq = {}; ss_sqrt = {}
            ssc = ss0
            for tb in range(TB + 1):
                if tb < TB:
                    for r in range(0, 8):
                        ssc += 1; ss_p1o[(tb, r)] = ssc
                if tb >= 1:
                    for j in range(4):
                        ssc += 1; ss_sq[(tb - 1, j)] = ssc
                        ssc += 1; ss_sqrt[(tb - 1, j)] = ssc
                if tb < TB:
                    for r in range(8, 24):
                        ssc += 1; ss_p1o[(tb, r)] = ssc
            C["ss"] = ssc
            vv_evac = {}; vv_slice = {}; vv_trev = {}; vv_var = {}; vv_out = {}
            vvc = vv0
            for tb in range(TB + 1):
                if tb >= 1:
                    for dt in range(KC):
                        vvc += 1; vv_evac[(tb - 1, dt)] = vvc
                if tb < TB:
                    for r in range(0, 8):
                        vvc += 1; vv_slice[(tb, r)] = vvc
                if tb >= 1:
                    for j in range(4):
                        vvc += 1; vv_trev[(tb - 1, j)] = vvc
                        vvc += 1; vv_var[(tb - 1, j)] = vvc
                        vvc += 1; vv_out[(tb - 1, j)] = vvc
                if tb < TB:
                    for r in range(8, 24):
                        vvc += 1; vv_slice[(tb, r)] = vvc
            C["vv"] = vvc

            def prev_slot_user(tb, r):
                if r >= RNG:
                    return (tb, r - RNG)
                if tb >= 1:
                    return (tb - 1, IT - RNG + r)
                return None

            def f_sync(e):
                for j, dstw in ((0, wdSh), (1, wdSl)):
                    dst = bass.AP(dstw, 0, [[KC * IT * 128, 128], [128, KC * IT], [1, 128]])
                    src = bass.AP(wdT[j].tensor, wdT[j].offset, [[128, 128], [128 * 128, KC * IT], [1, 128]])
                    e.dma_start(out=dst, in_=src).then_inc(sems["ldg"], 16)
                for tb in range(TB):
                    for r in range(IT):
                        gi = tb * IT + r
                        if gi >= 2:
                            e.wait_ge(sems["vv"], vv_slice[divmod(gi - 2, IT)])
                        sm_ = sems["lda"] if r % 2 == 0 else sems["ldb"]
                        e.dma_start(out=opin[r % 2][:],
                                    in_=bass.AP(opscr.tensor, opscr.offset + r * 128 * TOK + tb * 512,
                                                [[TOK, 128], [1, 512]])).then_inc(sm_, 16)
                    for j in range(4):
                        tj = tb * 4 + j
                        if tj >= 4:
                            e.wait_ge(sems["vv"], vv_out[divmod(tj - 4, 4)])
                        e.wait_ge(sems["ldg"], g0 + 16 * (2 + tj))
                        e.dma_start(out=x1r[j][:],
                                    in_=x1scr[tb * 512 + j * 128: tb * 512 + (j + 1) * 128, :]).then_inc(sems["ldg"], 16)

            def f_scalar(e):
                def p1o(tb, r):
                    sm_ = sems["lda"] if r % 2 == 0 else sems["ldb"]
                    e.wait_ge(sm_, (la0 if r % 2 == 0 else lb0) + 16 * (tb * 12 + r // 2 + 1))
                    SF.step(e, e.activation(opin[r % 2][:], opin[r % 2][:], ACT.Copy,
                                            bias=MAGIC, scale=bc[:, 7:8]))
                    e.nop().then_inc(sems["ss"], 1)

                def lnact(tbp, j):
                    e.wait_ge(sems["vv"], vv_trev[(tbp, j)])
                    SF.step(e, e.activation(sqd[:], ffnt[:], ACT.Square, accum_out=stats[:, 2:3]))
                    e.nop().then_inc(sems["ss"], 1)
                    e.wait_ge(sems["vv"], vv_var[(tbp, j)])
                    SF.step(e, e.activation(stats[:, 6:7], stats[:, 5:6], ACT.Sqrt, bias=epsv[:]))
                    e.nop().then_inc(sems["ss"], 1)
                for tb in range(TB + 1):
                    if tb < TB:
                        for r in range(0, 8):
                            p1o(tb, r)
                    if tb >= 1:
                        for j in range(4):
                            lnact(tb - 1, j)
                    if tb < TB:
                        for r in range(8, 24):
                            p1o(tb, r)

            def f_tensor(e):
                e.wait_ge(sems["ldg"], DCTX["wds_ldg"])
                for tb in range(TB):
                    for r in range(IT):
                        e.wait_ge(sems["vv"], vv_slice[(tb, r)])
                        sl = r % RNG
                        oh_ = oslh[:, sl * 512:(sl + 1) * 512]
                        ol_ = osll[:, sl * 512:(sl + 1) * 512]
                        for dt in range(KC):
                            if r == 0 and tb >= 1:
                                e.wait_ge(sems["vv"], vv_evac[(tb - 1, dt)])
                                if dt in (4, 5):
                                    e.wait_ge(sems["vv"], vv_trev[(tb - 1, 3)])
                            wh = wdSh[:, (dt * IT + r) * 128:(dt * IT + r + 1) * 128]
                            wl = wdSl[:, (dt * IT + r) * 128:(dt * IT + r + 1) * 128]
                            e.matmul(ps[dt][:], wh, oh_, start=(r == 0), stop=False)
                            e.matmul(ps[dt][:], wl, oh_, start=False, stop=False)
                            mm = e.matmul(ps[dt][:], wh, ol_, start=False, stop=(r == IT - 1))
                        mm.then_inc(sems["tt"], 1)
                    for j in range(4):
                        e.wait_ge(sems["vv"], vv_evac[(tb, KC - 1)])
                        if j >= 1:
                            e.wait_ge(sems["vv"], vv_trev[(tb, j - 1)])
                        for g2 in range(2):
                            for k2 in range(4):
                                dt = g2 * 4 + k2
                                mm = e.transpose(ps[4 + g2][:, k2 * 128:(k2 + 1) * 128],
                                                 ffm[:, dt * 512 + j * 128: dt * 512 + j * 128 + 128],
                                                 ident_s[:])
                        mm.then_inc(sems["tt"], 1)

            def f_vector(e):
                def c_slice(tb, r):
                    e.wait_ge(sems["ss"], ss_p1o[(tb, r)])
                    prev = prev_slot_user(tb, r)
                    if prev is not None:
                        e.wait_ge(sems["tt"], tt_mm[prev])
                    sl = r % RNG
                    q = r % 2
                    VF.step(e, e.tensor_scalar(oslh[:, sl * 512:(sl + 1) * 512], opin[q][:],
                                               MAGIC, None, AL.subtract, AL.bypass))
                    VF.step(e, e.tensor_scalar(opin[q][:], opin[q][:], MAGIC, None,
                                               AL.subtract, AL.bypass))
                    VF.step(e, e.scalar_tensor_tensor(osll[:, sl * 512:(sl + 1) * 512], opin[q][:],
                                                      1.0, oslh[:, sl * 512:(sl + 1) * 512],
                                                      AL.mult, AL.subtract))
                    e.nop().then_inc(sems["vv"], 1)

                def evacs(tbp):
                    e.wait_ge(sems["tt"], tt_mm[(tbp, IT - 1)])
                    if tbp >= 1:
                        e.wait_ge(sems["tt"], tt_tr[(tbp - 1, 3)])
                    for dt in range(KC):
                        VF.step(e, e.tensor_scalar(ffm[:, dt * 512:(dt + 1) * 512], ps[dt][:],
                                                   bc[:, 8:9], None, AL.mult, AL.bypass))
                        e.nop().then_inc(sems["vv"], 1)

                def ln_j(tbp, j):
                    tj = tbp * 4 + j
                    e.wait_ge(sems["tt"], tt_tr[(tbp, j)])
                    VF.step(e, e.tensor_copy(ffnt[:, 0:512], ps[4][:]))
                    VF.step(e, e.tensor_copy(ffnt[:, 512:1024], ps[5][:]))
                    e.nop().then_inc(sems["vv"], 1)
                    ln_stats_vec(e, ffnt[:], 0, ss_sq[(tbp, j)], ss_sqrt[(tbp, j)])
                    VF.step(e, e.tensor_scalar(ffnt[:], ffnt[:], stats[:, 1:2], stats[:, 7:8],
                                               AL.subtract, AL.mult))
                    e.wait_ge(sems["ldg"], g0 + 16 * (2 + tj + 1))
                    if tj >= 1:
                        e.wait_ge(sems["spg"], sg0 + 16 * tj)
                    VF.step(e, e.tensor_tensor(outt[:], ffnt[:], x1r[j][:], AL.add))
                    e.nop().then_inc(sems["vv"], 1)
                for tb in range(TB + 1):
                    if tb >= 1:
                        evacs(tb - 1)
                    if tb < TB:
                        for r in range(0, 8):
                            c_slice(tb, r)
                    if tb >= 1:
                        for j in range(4):
                            ln_j(tb - 1, j)
                    if tb < TB:
                        for r in range(8, 24):
                            c_slice(tb, r)

            def f_gpsimd(e):
                for tb in range(TB):
                    for j in range(4):
                        e.wait_ge(sems["vv"], vv_out[(tb, j)])
                        r0 = tb * 512 + j * 128
                        e.dma_start(out=out[r0:r0 + 128, :], in_=outt[:]).then_inc(sems["spg"], 16)
                e.wait_ge(sems["spg"], sg0 + 16 * TB * 4)
            return {"sync": f_sync, "tensor": f_tensor, "vector": f_vector,
                    "scalar": f_scalar, "gpsimd": f_gpsimd}

        add(ph_consts())
        add(ph_ln1())
        add(ph_qkv())
        add(ph_wdq())
        add(ph_attn())
        add(ph_wo())

        def chain1(e):
            VF.step(e, e.reduce_max(sc[0:1, 12:13], arr_s[0:1, 0:8], axis=AX))
            VF.step(e, e.tensor_scalar(sc[0:1, 13:14], sc[0:1, 12:13], CQ, QEPS, AL.mult, AL.max))
            VF.step(e, e.reciprocal(sc[0:1, 0:1], sc[0:1, 13:14]))
        add(ph_ar(0, [0], [0], chain1, 3, 1))

        add(ph_x())
        add(ph_upgate())

        def chain2(e):
            VF.step(e, e.reduce_max(sc[0:1, 12:13], arr_s[0:1, 0:8], axis=AX))
            VF.step(e, e.tensor_tensor(sc[0:1, 12:13], sc[0:1, 12:13], sc[0:1, 13:14], AL.mult))
            VF.step(e, e.tensor_scalar(sc[0:1, 2:3], sc[0:1, 12:13], CQ, QEPS, AL.mult, AL.max))
            VF.step(e, e.reciprocal(sc[0:1, 14:15], sc[0:1, 2:3]))
            VF.step(e, e.tensor_tensor(sc[0:1, 1:2], sc[0:1, 14:15], sc[0:1, 13:14], AL.mult))
            VF.step(e, e.reduce_max(sc[0:1, 12:13], arr_s[0:1, 8:16], axis=AX))
            VF.step(e, e.tensor_tensor(sc[0:1, 12:13], sc[0:1, 12:13], sc[0:1, 13:14], AL.mult))
            VF.step(e, e.tensor_scalar(sc[0:1, 4:5], sc[0:1, 12:13], CQ, QEPS, AL.mult, AL.max))
            VF.step(e, e.reciprocal(sc[0:1, 14:15], sc[0:1, 4:5]))
            VF.step(e, e.tensor_tensor(sc[0:1, 3:4], sc[0:1, 14:15], sc[0:1, 13:14], AL.mult))
        add(ph_ar(1, [1, 2], [0, 8], chain2, 10, 5))

        add(ph_ffnA())

        def chain3(e):
            VF.step(e, e.reduce_max(sc[0:1, 12:13], arr_s[0:1, 0:8], axis=AX))
            VF.step(e, e.tensor_scalar(sc[0:1, 12:13], sc[0:1, 12:13], 1.0 / 65536.0, None, AL.mult, AL.bypass))
            VF.step(e, e.tensor_scalar(sc[0:1, 6:7], sc[0:1, 12:13], CQ, QEPS, AL.mult, AL.max))
            VF.step(e, e.reciprocal(sc[0:1, 14:15], sc[0:1, 6:7]))
            VF.step(e, e.tensor_scalar(sc[0:1, 5:6], sc[0:1, 14:15], 1.0 / 65536.0, None, AL.mult, AL.bypass))
        def chain3z(e):
            chain3(e)
            VF.step(e, e.memset(macc[:, 3:4], 0.0))
        add(ph_ar(2, [3], [0], chain3z, 6, 7))

        add(ph_ffnB())

        def chain4(e):
            VF.step(e, e.reduce_max(sc[0:1, 12:13], arr_s[0:1, 0:8], axis=AX))
            VF.step(e, e.tensor_tensor(sc[0:1, 12:13], sc[0:1, 12:13], sc[0:1, 6:7], AL.mult))
            VF.step(e, e.tensor_scalar(sc[0:1, 8:9], sc[0:1, 12:13], CQ, QEPS, AL.mult, AL.max))
            VF.step(e, e.reciprocal(sc[0:1, 14:15], sc[0:1, 8:9]))
            VF.step(e, e.tensor_tensor(sc[0:1, 7:8], sc[0:1, 14:15], sc[0:1, 6:7], AL.mult))
        add(ph_ar(3, [3], [0], chain4, 5, 9))

        add(ph_down())

        for ph_ in phases:
            for en in ENGS:
                fn = ph_.get(en)
                if fn is not None:
                    getattr(block, en)(fn)
    return nc


def kernel(L0_attn_hidden_states, L0_attn_Wq, L0_attn_Wkv, L0_attn_Wo, L0_attn_bo,
           L0_ffn_w_gate, L0_ffn_w_up, L0_ffn_w_down):
    n = 8
    x = np.ascontiguousarray(np.asarray(L0_attn_hidden_states, dtype=np.float32))
    xs = x.reshape(8, TOK, D)
    nc = build_nc(n)
    base = {
        "wq": np.ascontiguousarray(np.asarray(L0_attn_Wq, dtype=np.float32)),
        "wkv": np.ascontiguousarray(np.asarray(L0_attn_Wkv, dtype=np.float32)),
        "wo": np.ascontiguousarray(np.asarray(L0_attn_Wo, dtype=np.float32)),
        "bo": np.ascontiguousarray(np.asarray(L0_attn_bo, dtype=np.float32)).reshape(1, D),
        "wup": np.ascontiguousarray(np.asarray(L0_ffn_w_up, dtype=np.float32)),
        "wgate": np.ascontiguousarray(np.asarray(L0_ffn_w_gate, dtype=np.float32)),
        "wdown": np.ascontiguousarray(np.asarray(L0_ffn_w_down, dtype=np.float32)),
        "identity": np.eye(128, dtype=np.float32),
        "onesrow": np.ones((1, 128), np.float32),
        "negrow": -np.ones((1, 128), np.float32),
    }
    in_maps = []
    for c in range(n):
        m = dict(base)
        m["x0"] = xs[c]
        sm = np.zeros((1, 16), np.float32)
        sm[0, c] = 1.0
        sm[0, 8 + c] = 1.0
        m["slotm"] = sm
        in_maps.append(m)
    res = run_bass_kernel_spmd(nc, in_maps, core_ids=list(range(n)))
    outp = np.concatenate([np.asarray(r["out"])[None] for r in res.results], axis=0)
    return outp.reshape(16, S, D)


# version 13
# speedup vs baseline: 1.1276x; 1.1276x over previous
import numpy as np
from contextlib import ExitStack
import concourse.bass as bass
import concourse.mybir as mybir
from concourse.bass_utils import run_bass_kernel_spmd

F32 = mybir.dt.float32
AL = mybir.AluOpType
ACT = mybir.ActivationFunctionType
AX = mybir.AxisListType.X

MAGIC = float(3 * 2.0 ** 22)
C7 = 1.0 / 7.0
CQ = float(1.0 / 32767.0)
EPS = 1e-5
QEPS = 1e-12

D = 1024
I3 = 3072
TOK = 4096
S = 2048
NH = 16
DH = 64
NT = TOK // 128
KC = D // 128
IT = I3 // 128
TB = TOK // 512
ENGS = ("sync", "tensor", "vector", "scalar", "gpsimd")


class Fine:

    def __init__(self, sem):
        self.sem = sem
        self.n = 0

    def step(self, eng, instr):
        self.n += 1
        instr.then_inc(self.sem, 1)
        eng.wait_ge(self.sem, self.n)
        return instr


def build_nc(n_cores=8):
    nc = bass.Bass("TRN2", target_bir_lowering=False)
    x0 = nc.dram_tensor("x0", [TOK, D], F32, kind="ExternalInput").ap()
    wq = nc.dram_tensor("wq", [D, D], F32, kind="ExternalInput").ap()
    wkv = nc.dram_tensor("wkv", [D, 2 * D], F32, kind="ExternalInput").ap()
    wo = nc.dram_tensor("wo", [D, D], F32, kind="ExternalInput").ap()
    bo = nc.dram_tensor("bo", [1, D], F32, kind="ExternalInput").ap()
    wup = nc.dram_tensor("wup", [I3, D], F32, kind="ExternalInput").ap()
    wgate = nc.dram_tensor("wgate", [I3, D], F32, kind="ExternalInput").ap()
    wdown = nc.dram_tensor("wdown", [D, I3], F32, kind="ExternalInput").ap()
    identity = nc.dram_tensor("identity", [128, 128], F32, kind="ExternalInput").ap()
    onesrow = nc.dram_tensor("onesrow", [1, 128], F32, kind="ExternalInput").ap()
    negrow = nc.dram_tensor("negrow", [1, 128], F32, kind="ExternalInput").ap()
    slotm = nc.dram_tensor("slotm", [1, 16], F32, kind="ExternalInput").ap()
    out = nc.dram_tensor("out", [TOK, D], F32, kind="ExternalOutput").ap()

    qfm = nc.dram_tensor("qfm", [D, TOK], F32).ap()
    kfm = nc.dram_tensor("kfm", [D, TOK], F32).ap()
    vscr = nc.dram_tensor("vscr", [NT, 128, D], F32).ap()
    attn_fm = nc.dram_tensor("attn_fm", [2, 4, KC, 128, 512], F32).ap()
    x1scr = nc.dram_tensor("x1scr", [TOK, D], F32).ap()
    h2scr = nc.dram_tensor("h2scr", [TOK, D], F32).ap()
    BF16 = mybir.dt.bfloat16
    wupT = [nc.dram_tensor(f"wupT{j}", [IT, KC, 128, 128], BF16).ap() for j in range(2)]
    wgT = [nc.dram_tensor(f"wgT{j}", [IT, KC, 128, 128], BF16).ap() for j in range(2)]
    wdT = [nc.dram_tensor(f"wdT{j}", [KC, IT, 128, 128], BF16).ap() for j in range(2)]
    u0scr = nc.dram_tensor("u0scr", [IT, 128, TOK], F32).ap()
    g0scr = nc.dram_tensor("g0scr", [IT, 128, TOK], F32).ap()
    opscr = nc.dram_tensor("opscr", [IT, 128, TOK], F32).ap()
    ccin = [nc.dram_tensor(f"ccin{i}", [1, 16], F32) for i in range(4)]
    ccout = [nc.dram_tensor(f"ccout{i}", [1, 16], F32) for i in range(4)]

    es = ExitStack()
    with es:
        RBASE = 16512
        es.enter_context(nc.sbuf_tensor("reserve_arena", [128, 196000], mybir.dt.uint8))

        def sb(name, shape, start, dt_=F32):
            per = int(np.prod(shape[1:])) * (2 if dt_ == mybir.dt.bfloat16 else 4)
            start = start + RBASE
            assert start + per <= 212512, (name, start, per)
            return nc.alloc_sbuf_tensor_at(name, shape, dt_, offset=start)
        KB = 1024
        A0, B0, S0 = 0, 128 * KB, 182 * KB
        hfm = sb("hfm", [128, KC * TOK], A0)
        VB = sb("VB", [128, 16 * D], A0)
        ESd = [sb(f"ESd{i}", [128, 8192], A0 + (64 + 32 * i) * KB) for i in range(2)]
        woS = sb("woS", [128, KC * D], A0)
        bobc = sb("bobc", [128, D], A0 + 32 * KB)
        afm = [sb(f"afm{i}", [128, D], A0 + 36 * KB + i * 4 * KB) for i in range(2)]
        BF = mybir.dt.bfloat16
        xfmh = sb("xfmh", [128, KC * TOK], A0, BF)
        xfml = sb("xfml", [128, KC * TOK], A0 + 64 * KB, BF)
        jb = [sb(f"jb{i}", [128, TOK], A0 + i * 16 * KB) for i in range(8)]
        wdSh = sb("wdSh", [128, KC * IT * 128], A0, BF)
        wdSl = sb("wdSl", [128, KC * IT * 128], A0 + 48 * KB, BF)
        ffm = sb("ffm", [128, KC * 512], A0 + 96 * KB)
        ffnt = sb("ffnt", [128, D], A0 + 112 * KB)
        x1r = [sb(f"x1r{i}", [128, D], (A0 + 116 * KB + i * 4 * KB) if i < 2 else
                  (B0 + 24 * KB + (i - 2) * 4 * KB)) for i in range(4)]
        outt = sb("outt", [128, D], A0 + 124 * KB)
        ln_x = [sb(f"ln_x{i}", [128, D], B0 + i * 4 * KB) for i in range(2)]
        ln_h = sb("ln_h", [128, D], B0 + 8 * KB)
        sqscr = sb("sqscr", [128, D], B0 + 12 * KB)
        wst = [sb(f"wst{i}", [128, KC * 128], B0 + 4 * KB + i * 4 * KB) for i in range(2)]
        wmv = [sb(f"wmv{i}", [128, KC * 512], B0 + 12 * KB + i * 16 * KB) for i in range(2)]
        evst = sb("evst", [128, 1024], B0)
        dqin = [sb(f"dqin{i}", [128, I3], B0 + i * 12 * KB) for i in range(2)]
        dqscr = sb("dqscr", [128, I3], B0 + 24 * KB)
        dqh = sb("dqh", [128, I3], B0 + 36 * KB, BF)
        dql = sb("dql", [128, I3], B0 + 42 * KB, BF)
        dqsth = sb("dqsth", [128, I3], B0 + 24 * KB, BF)
        dqstl = sb("dqstl", [128, I3], B0 + 30 * KB, BF)
        KH = [sb(f"KH{i}", [64, S], B0 + i * 8 * KB) for i in range(2)]
        QH = [sb(f"QH{i}", [64, 512], B0 + 16 * KB + i * 2 * KB) for i in range(2)]
        zw = sb("zw", [128, 4096], B0 + 20 * KB)
        ast = [sb(f"ast{i}", [128, 512], B0 + 36 * KB + i * 2 * KB) for i in range(2)]
        rzb = sb("rzb", [64, 512], B0 + 40 * KB)
        mt16 = sb("mt16", [128, 16], B0 + 42 * KB)
        rzrow = sb("rzrow", [1, 512], B0 + 42 * KB + 512)
        zq4 = sb("zq4", [128, 4], B0 + 42 * KB + 2560)
        att_in = [sb(f"att_in{i}", [128, D], B0 + 16 * KB + i * 4 * KB) for i in range(2)]
        x0r = [sb(f"x0r{i}", [128, D], B0 + 24 * KB + i * 4 * KB) for i in range(2)]
        x1t = sb("x1t", [128, D], B0 + 32 * KB)
        h2t = sb("h2t", [128, D], B0 + 36 * KB)
        h2in = [sb(f"h2in{i}", [128, D], B0 + i * 4 * KB) for i in range(2)]
        xh = sb("xh", [128, D], B0 + 8 * KB, BF)
        xl = sb("xl", [128, D], B0 + 10 * KB, BF)
        u0j = [sb(f"u0j{i}", [128, TOK], B0 + i * 16 * KB) for i in range(2)]
        jbx = [sb(f"jbx{i}", [128, TOK], B0 + i * 16 * KB) for i in range(2)]
        wT7h = [sb(f"wT7h{i}", [128, KC * 128], B0 + 32 * KB + i * 2 * KB, BF) for i in range(2)]
        wT7l = [sb(f"wT7l{i}", [128, KC * 128], B0 + 36 * KB + i * 2 * KB, BF) for i in range(2)]
        ev7 = [sb(f"ev7_{i}", [128, 512], B0 + 40 * KB + i * 2 * KB) for i in range(2)]
        oslh = sb("oslh", [128, 8 * 512], B0, BF)
        osll = sb("osll", [128, 8 * 512], B0 + 8 * KB, BF)
        opin = [sb(f"opin{i}", [128, 512], B0 + 16 * KB + i * 2 * KB) for i in range(2)]
        sqd = sb("sqd", [128, D], B0 + 20 * KB)
        ident_s = sb("ident_s", [128, 128], S0)
        ident_b = sb("ident_b", [128, 128], S0 + 8320, BF)
        ones_s = sb("ones_s", [1, 128], S0 + 512)
        negr_s = sb("negr_s", [1, 128], S0 + 1024)
        slot_s = sb("slot_s", [1, 16], S0 + 1536)
        arv_s = sb("arv_s", [1, 16], S0 + 1600)
        arr_s = sb("arr_s", [1, 16], S0 + 1664)
        epsv = sb("epsv", [128, 1], S0 + 1728)
        stats = sb("stats", [128, 16], S0 + 1792)
        bc = sb("bc", [128, 16], S0 + 1856)
        sc = sb("sc", [1, 16], S0 + 1920)
        macc = sb("macc", [128, 4], S0 + 1984)
        mtmp = sb("mtmp", [128, 2], S0 + 2048)
        mrow = sb("mrow", [1, 128], S0 + 2112)
        negm = sb("negm", [128, 1], S0 + 2624)
        zred = sb("zred", [128, 8], S0 + 2688)
        bo_s = sb("bo_s", [1, D], S0 + 2752)
        ones_c = sb("ones_c", [128, 1], S0 + 6912)
        dqs = sb("dqs", [128, 96], S0 + 7 * KB)
        dqr = sb("dqr", [128, 96], S0 + 7 * KB + 512)

        ps = [es.enter_context(nc.psum_tensor(f"ps{i}", [128, 512], F32)) for i in range(8)]

        sems = {}
        for s_ in ("lda", "ldb", "ldg", "spa", "spb", "spg", "sha", "shb", "vv", "ss", "tt", "gg", "bar", "cc", "vf", "sf", "gf"):
            sems[s_] = es.enter_context(nc.semaphore(s_))
        block = es.enter_context(nc.Block())

        C = {k: 0 for k in sems}
        VF = Fine(sems["vf"])
        SF = Fine(sems["sf"])
        GFN = [0]
        DCTX = {}
        phases = []

        def barrier():
            C["bar"] += 5
            tgt = C["bar"]

            def mk():
                def f(e):
                    e.nop().then_inc(sems["bar"], 1)
                    e.wait_ge(sems["bar"], tgt)
                return f
            phases.append({n_: mk() for n_ in ENGS})

        def add(ph):
            phases.append(ph)
            barrier()

        def ln_stats_vec(e, xt, vv_var_tgt, ss_sq_tgt, ss_sqrt_tgt, mcol=1, sumcol=0):
            VF.step(e, e.reduce_sum(stats[:, sumcol:sumcol + 1], xt, axis=AX))
            VF.step(e, e.tensor_scalar(stats[:, mcol:mcol + 1], stats[:, sumcol:sumcol + 1],
                                       1.0 / D, None, AL.mult, AL.bypass))
            e.wait_ge(sems["ss"], ss_sq_tgt)
            VF.step(e, e.tensor_scalar(stats[:, 3:4], stats[:, 2:3], 1.0 / D, None, AL.mult, AL.bypass))
            VF.step(e, e.tensor_tensor(stats[:, 4:5], stats[:, mcol:mcol + 1], stats[:, mcol:mcol + 1], AL.mult))
            VF.step(e, e.tensor_tensor(stats[:, 5:6], stats[:, 3:4], stats[:, 4:5], AL.subtract))
            e.nop().then_inc(sems["vv"], 1)
            e.wait_ge(sems["ss"], ss_sqrt_tgt)
            VF.step(e, e.reciprocal(stats[:, 7:8], stats[:, 6:7]))
            VF.step(e, e.tensor_scalar(stats[:, 8:9], stats[:, 5:6], EPS, None, AL.add, AL.bypass))
            for _ in range(2):
                VF.step(e, e.tensor_tensor(stats[:, 9:10], stats[:, 7:8], stats[:, 7:8], AL.mult))
                VF.step(e, e.tensor_tensor(stats[:, 9:10], stats[:, 9:10], stats[:, 8:9], AL.mult))
                VF.step(e, e.tensor_scalar(stats[:, 9:10], stats[:, 9:10], -0.5, 1.5, AL.mult, AL.add))
                VF.step(e, e.tensor_tensor(stats[:, 7:8], stats[:, 7:8], stats[:, 9:10], AL.mult))

        def ln_stats_scal(e, xt, vv_xt_tgt, vv_var_tgt):
            e.wait_ge(sems["vv"], vv_xt_tgt)
            SF.step(e, e.activation(sqscr[:], xt, ACT.Square, accum_out=stats[:, 2:3]))
            e.nop().then_inc(sems["ss"], 1)
            e.wait_ge(sems["vv"], vv_var_tgt)
            SF.step(e, e.activation(stats[:, 6:7], stats[:, 5:6], ACT.Sqrt, bias=epsv[:]))
            e.nop().then_inc(sems["ss"], 1)

        def ph_consts():
            g0 = C["ldg"]
            C["ldg"] += 16 * 5
            C["vv"] += 1

            def f_sync(e):
                e.dma_start(out=ident_s[:], in_=identity).then_inc(sems["ldg"], 16)
                e.dma_start(out=ones_s[:], in_=onesrow).then_inc(sems["ldg"], 16)
                e.dma_start(out=negr_s[:], in_=negrow).then_inc(sems["ldg"], 16)
                e.dma_start(out=slot_s[:], in_=slotm).then_inc(sems["ldg"], 16)
                e.dma_start(out=bo_s[:], in_=bo).then_inc(sems["ldg"], 16)

            def f_vector(e):
                VF.step(e, e.memset(epsv[:], EPS))
                VF.step(e, e.memset(macc[:], 0.0))
                VF.step(e, e.memset(mtmp[:], 0.0))
                VF.step(e, e.memset(ones_c[:], 1.0))
                e.wait_ge(sems["ldg"], g0 + 80)
                VF.step(e, e.tensor_copy(ident_b[:], ident_s[:]))
                e.nop().then_inc(sems["vv"], 1)
            return {"sync": f_sync, "vector": f_vector}

        def ph_ln1():
            vv0, ss0, tt0 = C["vv"], C["ss"], C["tt"]
            la0, lb0 = C["lda"], C["ldb"]
            C["lda"] += 16 * (NT // 2)
            C["ldb"] += 16 * (NT // 2)
            C["vv"] += 3 * NT
            C["ss"] += 2 * NT
            C["tt"] += 2 * NT

            def lsem(t):
                return (sems["lda"], la0 + 16 * (t // 2 + 1)) if t % 2 == 0 else (sems["ldb"], lb0 + 16 * (t // 2 + 1))

            def f_sync(e):
                for t in range(NT):
                    if t >= 2:
                        e.wait_ge(sems["vv"], vv0 + 3 * (t - 2) + 2)
                        e.wait_ge(sems["ss"], ss0 + 2 * (t - 2) + 1)
                    sm_, _ = lsem(t)
                    e.dma_start(out=ln_x[t % 2][:], in_=x0[t * 128:(t + 1) * 128, :]).then_inc(sm_, 16)

            def f_vector(e):
                for t in range(NT):
                    xt = ln_x[t % 2]
                    sm_, tv_ = lsem(t)
                    e.wait_ge(sm_, tv_)
                    ln_stats_vec(e, xt[:], vv0 + 3 * t + 1, ss0 + 2 * t + 1, ss0 + 2 * t + 2)
                    VF.step(e, e.tensor_scalar(ln_h[:], xt[:], stats[:, 1:2], stats[:, 7:8],
                                               AL.subtract, AL.mult))
                    e.nop().then_inc(sems["vv"], 1)
                    for half in range(2):
                        e.wait_ge(sems["tt"], tt0 + 2 * t + half + 1)
                        dst = bass.AP(hfm, (4 * half) * TOK + t * 128, [[KC * TOK, 128], [TOK, 4], [1, 128]])
                        VF.step(e, e.tensor_copy(dst, ps[half][:]))
                    e.nop().then_inc(sems["vv"], 1)

            def f_scalar(e):
                for t in range(NT):
                    sm_, tv_ = lsem(t)
                    e.wait_ge(sm_, tv_)
                    ln_stats_scal(e, ln_x[t % 2][:], 0, vv0 + 3 * t + 1)

            def f_tensor(e):
                for t in range(NT):
                    e.wait_ge(sems["vv"], vv0 + 3 * t + 2)
                    if t >= 1:
                        e.wait_ge(sems["vv"], vv0 + 3 * t)
                    for half in range(2):
                        for j in range(4):
                            c = half * 4 + j
                            mm = e.transpose(ps[half][:, j * 128:(j + 1) * 128],
                                             ln_h[:, c * 128:(c + 1) * 128], ident_s[:])
                        mm.then_inc(sems["tt"], 1)
            return {"sync": f_sync, "vector": f_vector, "scalar": f_scalar, "tensor": f_tensor}

        def ph_qkv():
            vv0, tt0 = C["vv"], C["tt"]
            la0, lb0, g0 = C["lda"], C["ldb"], C["ldg"]
            sa0, sb0 = C["spa"], C["spb"]
            NF = 16
            NIT = NF * TB + NT * 2
            C["lda"] += 16 * (NF // 2)
            C["ldb"] += 16 * (NF // 2)
            C["ldg"] += 16 * 2
            C["tt"] += NIT
            C["vv"] += NIT
            C["spa"] += 16 * ((NIT + 1) // 2)
            C["spb"] += 16 * (NIT // 2)

            def f_sync(e):
                for ft in range(NF):
                    if ft >= 2:
                        e.wait_ge(sems["tt"], tt0 + TB * (ft - 1))
                    w_, ldw = (wq, D) if ft < 8 else (wkv, 2 * D)
                    coloff = (ft % 8) * 128
                    dst = bass.AP(wst[ft % 2], 0, [[KC * 128, 128], [128, KC], [1, 128]])
                    srcv = bass.AP(w_.tensor, w_.offset + coloff, [[ldw, 128], [128 * ldw, KC], [1, 128]])
                    sm_ = sems["lda"] if ft % 2 == 0 else sems["ldb"]
                    e.dma_start(out=dst, in_=srcv).then_inc(sm_, 16)
                for vb in range(2):
                    dst = bass.AP(wmv[vb], 0, [[KC * 512, 128], [512, KC], [1, 512]])
                    srcv = bass.AP(wkv.tensor, wkv.offset + D + vb * 512, [[2 * D, 128], [128 * 2 * D, KC], [1, 512]])
                    e.dma_start(out=dst, in_=srcv).then_inc(sems["ldg"], 16)

            def f_tensor(e):
                for ft in range(NF):
                    sm_ = sems["lda"] if ft % 2 == 0 else sems["ldb"]
                    e.wait_ge(sm_, (la0 if ft % 2 == 0 else lb0) + 16 * (ft // 2 + 1))
                    for tb in range(TB):
                        it = ft * TB + tb
                        if it >= 2:
                            e.wait_ge(sems["vv"], vv0 + it - 1)
                        p = ps[it % 2]
                        for c in range(KC):
                            mm = e.matmul(p[:], wst[ft % 2][:, c * 128:(c + 1) * 128],
                                          hfm[:, c * TOK + tb * 512: c * TOK + (tb + 1) * 512],
                                          start=(c == 0), stop=(c == KC - 1))
                        mm.then_inc(sems["tt"], 1)
                e.wait_ge(sems["ldg"], g0 + 32)
                for t in range(NT):
                    for vb in range(2):
                        it = NF * TB + t * 2 + vb
                        if it >= 2:
                            e.wait_ge(sems["vv"], vv0 + it - 1)
                        p = ps[it % 2]
                        for c in range(KC):
                            mm = e.matmul(p[:], hfm[:, c * TOK + t * 128: c * TOK + t * 128 + 128],
                                          wmv[vb][:, c * 512:(c + 1) * 512],
                                          start=(c == 0), stop=(c == KC - 1))
                        mm.then_inc(sems["tt"], 1)

            def f_vector(e):
                NFTB = NF * TB
                for it in range(NF * TB + NT * 2):
                    e.wait_ge(sems["tt"], tt0 + it + 1)
                    if it >= 2:
                        pit = it - 2
                        sm_ = sems["spa"] if pit % 2 == 0 else sems["spb"]
                        e.wait_ge(sm_, (sa0 if pit % 2 == 0 else sb0) + 16 * (pit // 2 + 1))
                    p = ps[it % 2]
                    stg = evst[:, (it % 2) * 512:(it % 2) * 512 + 512]
                    if it < NFTB and (it // TB) < 8:
                        VF.step(e, e.tensor_scalar(stg, p[:], 0.125, None, AL.mult, AL.bypass))
                    else:
                        VF.step(e, e.tensor_copy(stg, p[:]))
                    e.nop().then_inc(sems["vv"], 1)

            def f_gpsimd(e):
                NFTB = NF * TB
                for it in range(NFTB + NT * 2):
                    e.wait_ge(sems["vv"], vv0 + it + 1)
                    stg = evst[:, (it % 2) * 512:(it % 2) * 512 + 512]
                    if it < NFTB:
                        ft, tb = it // TB, it % TB
                        if ft < 8:
                            dstv = qfm[ft * 128:(ft + 1) * 128, tb * 512:(tb + 1) * 512]
                        else:
                            dstv = kfm[(ft - 8) * 128:(ft - 7) * 128, tb * 512:(tb + 1) * 512]
                    else:
                        j = it - NFTB
                        t, vb = j // 2, j % 2
                        dstv = vscr[t, :, vb * 512:(vb + 1) * 512]
                    sm_ = sems["spa"] if it % 2 == 0 else sems["spb"]
                    e.dma_start(out=dstv, in_=stg).then_inc(sm_, 16)
                e.wait_ge(sems["spa"], sa0 + 16 * ((NIT + 1) // 2))
                e.wait_ge(sems["spb"], sb0 + 16 * (NIT // 2))
            return {"sync": f_sync, "tensor": f_tensor, "vector": f_vector, "gpsimd": f_gpsimd}

        def ph_wdq():
            vv0, tt0 = C["vv"], C["tt"]
            la0, lb0, sg0 = C["lda"], C["ldb"], C["spg"]
            tiles = ([("up", r) for r in range(IT)] + [("g", r) for r in range(IT)]
                     + [("d", r) for r in range(KC)])
            ntr = {"up": 2, "g": 2, "d": 6}
            tt_before = []
            vv_before = []
            acc = tt0
            vacc = vv0
            for k, _ in tiles:
                tt_before.append(acc)
                vv_before.append(vacc)
                acc += ntr[k]
                vacc += 1 + ntr[k]
            C["lda"] += 16 * ((len(tiles) + 1) // 2)
            C["ldb"] += 16 * (len(tiles) // 2)
            C["tt"] = acc
            C["vv"] = vacc
            C["spg"] += 32 * len(tiles)

            def tsrc(kind, r):
                return {"up": wup, "g": wgate, "d": wdown}[kind][r * 128:(r + 1) * 128, :]

            def f_sync(e):
                for i, (kind, r) in enumerate(tiles):
                    if i >= 2:
                        e.wait_ge(sems["vv"], vv_before[i - 2] + 1)
                    nb = 1024 if kind != "d" else I3
                    sm_ = sems["lda"] if i % 2 == 0 else sems["ldb"]
                    e.dma_start(out=dqin[i % 2][:, 0:nb], in_=tsrc(kind, r)).then_inc(sm_, 16)

            def f_vector(e):
                for i, (kind, r) in enumerate(tiles):
                    nb = 1024 if kind != "d" else I3
                    nblk = nb // 32
                    wsb = dqin[i % 2][:, 0:nb]
                    sm_ = sems["lda"] if i % 2 == 0 else sems["ldb"]
                    e.wait_ge(sm_, (la0 if i % 2 == 0 else lb0) + 16 * (i // 2 + 1))
                    if i >= 1:
                        e.wait_ge(sems["spg"], sg0 + 32 * i)
                    VF.step(e, e.tensor_scalar(dqscr[:, 0:nb], wsb, -1.0, None, AL.mult, AL.bypass))
                    VF.step(e, e.tensor_tensor(dqscr[:, 0:nb], dqscr[:, 0:nb], wsb, AL.max))
                    width = 16
                    while width >= 1:
                        a = bass.AP(dqscr, 0, [[I3, 128], [32, nblk], [1, width]])
                        b2 = bass.AP(dqscr, width, [[I3, 128], [32, nblk], [1, width]])
                        VF.step(e, e.tensor_tensor(a, a, b2, AL.max))
                        width //= 2
                    gat = bass.AP(dqscr, 0, [[I3, 128], [32, nblk]])
                    VF.step(e, e.tensor_copy(dqs[:, 0:nblk], gat))
                    VF.step(e, e.tensor_scalar(dqs[:, 0:nblk], dqs[:, 0:nblk], C7, QEPS, AL.mult, AL.max))
                    VF.step(e, e.reciprocal(dqr[:, 0:nblk], dqs[:, 0:nblk]))
                    rrep = bass.AP(dqr, 0, [[96, 128], [1, nblk], [0, 32]])
                    srep = bass.AP(dqs, 0, [[96, 128], [1, nblk], [0, 32]])
                    VF.step(e, e.tensor_tensor(dqscr[:, 0:nb], wsb, rrep, AL.mult))
                    VF.step(e, e.tensor_scalar(dqscr[:, 0:nb], dqscr[:, 0:nb], MAGIC, MAGIC, AL.add, AL.subtract))
                    VF.step(e, e.tensor_tensor(dqscr[:, 0:nb], dqscr[:, 0:nb], srep, AL.mult))
                    VF.step(e, e.tensor_copy(dqh[:, 0:nb], dqscr[:, 0:nb]))
                    VF.step(e, e.scalar_tensor_tensor(dql[:, 0:nb], dqscr[:, 0:nb], 1.0, dqh[:, 0:nb],
                                                      AL.mult, AL.subtract))
                    e.nop().then_inc(sems["vv"], 1)
                    for g in range(ntr[kind]):
                        e.wait_ge(sems["tt"], tt_before[i] + g + 1)
                        half = kind != "d" and g == 1 or kind == "d" and g >= 3
                        if kind != "d":
                            dst = (dqsth if g == 0 else dqstl)[:, 0:1024]
                        else:
                            dst = (dqsth if g < 3 else dqstl)[:, (g % 3) * 1024:(g % 3 + 1) * 1024]
                        VF.step(e, e.tensor_copy(dst, ps[g % 2][:].bitcast(mybir.dt.bfloat16)))
                        e.nop().then_inc(sems["vv"], 1)

            def f_tensor(e):
                for i, (kind, r) in enumerate(tiles):
                    nb = 1024 if kind != "d" else I3
                    nch = nb // 128
                    e.wait_ge(sems["vv"], vv_before[i] + 1)
                    for g in range(ntr[kind]):
                        if g >= 2:
                            e.wait_ge(sems["vv"], vv_before[i] + 1 + (g - 1))
                        elif i >= 1:
                            e.wait_ge(sems["vv"], vv_before[i - 1] + 1 + ntr[tiles[i - 1][0]] - (1 - g))
                        if kind != "d":
                            src = dqh if g == 0 else dql
                            c0 = 0
                        else:
                            src = dqh if g < 3 else dql
                            c0 = (g % 3) * 8
                        pb = ps[g % 2][:].bitcast(mybir.dt.bfloat16)
                        mm = None
                        for j in range(8):
                            c = c0 + j
                            mm = e.transpose(pb[:, j * 128:(j + 1) * 128],
                                             src[:, c * 128:(c + 1) * 128], ident_b[:])
                        mm.then_inc(sems["tt"], 1)

            def f_gpsimd(e):
                for i, (kind, r) in enumerate(tiles):
                    e.wait_ge(sems["vv"], vv_before[i] + 1 + ntr[kind])
                    nb = 1024 if kind != "d" else I3
                    nch = nb // 128
                    if kind == "up":
                        basep, off = wupT, r * KC * 128 * 128
                    elif kind == "g":
                        basep, off = wgT, r * KC * 128 * 128
                    else:
                        basep, off = wdT, r * IT * 128 * 128
                    for j, stg in ((0, dqsth), (1, dqstl)):
                        dst = bass.AP(basep[j].tensor, basep[j].offset + off,
                                      [[128, 128], [128 * 128, nch], [1, 128]])
                        e.dma_start(out=dst, in_=stg[:, 0:nb]).then_inc(sems["spg"], 16)
                e.wait_ge(sems["spg"], sg0 + 32 * len(tiles))
            return {"sync": f_sync, "vector": f_vector, "tensor": f_tensor, "gpsimd": f_gpsimd}

        def ph_attn():
            vv0, ss0, tt0, gg0 = C["vv"], C["ss"], C["tt"], C["gg"]
            la0, lb0, g0, sg0 = C["lda"], C["ldb"], C["ldg"], C["spg"]
            NIT = 2 * 4 * NH

            tt_scores = {}; tt_Mtp = {}; tt_negbc = {}; tt_av = {}
            tt_zmm = {}; tt_rzbc = {}
            ttc = tt0
            ttc += 1; tt_Mtp[0] = ttc
            ttc += 1; tt_negbc[0] = ttc
            for i in range(NIT + 2):
                if i < NIT:
                    for kc in range(4):
                        ttc += 1; tt_scores[(i, kc)] = ttc
                    if i >= 1:
                        ttc += 1; tt_negbc[i] = ttc
                    for kc in range(4, 16):
                        ttc += 1; tt_scores[(i, kc)] = ttc
                if 2 <= i <= NIT + 1:
                    ttc += 1; tt_zmm[i - 2] = ttc
                if 1 <= i <= NIT:
                    ttc += 1; tt_av[i - 1] = ttc
                if 2 <= i <= NIT + 1:
                    ttc += 1; tt_rzbc[i - 2] = ttc
                if i + 1 <= NIT - 1:
                    ttc += 1; tt_Mtp[i + 1] = ttc
            C["tt"] = ttc
            vv_evac = {}; vv_maxred = {}; vv_m1 = {}; vv_negm = {}; vv_zw = {}
            vv_rzrow = {}; vv_rzb = {}; vv_em = {}
            vvc = vv0
            for i in range(NIT + 2):
                if i < NIT:
                    vvc += 1; vv_m1[i] = vvc
                    for kc in range(4):
                        vvc += 1; vv_evac[i * 16 + kc] = vvc
                    vvc += 1; vv_negm[i] = vvc
                    for kc in range(4, 16):
                        vvc += 1; vv_evac[i * 16 + kc] = vvc
                    vvc += 1; vv_maxred[i] = vvc
                if 2 <= i <= NIT + 1:
                    p2 = i - 2
                    vvc += 1; vv_rzrow[p2] = vvc
                    vvc += 1; vv_rzb[p2] = vvc
                    vvc += 1; vv_em[p2] = vvc
                if 1 <= i <= NIT:
                    vvc += 1; vv_zw[i - 1] = vvc
            C["vv"] = vvc
            C["ss"] += 4 * NIT
            C["gg"] += NIT
            C["ldg"] += 16 * (2 * 16)
            C["lda"] += 32 * (NIT // 2)
            C["ldb"] += 32 * (NIT // 2)
            C["spg"] += 16 * (NIT // 2)

            def hdr(i):
                return i // 64, (i // 16) % 4, i % 16

            def f_sync(e):
                for v in range(16):
                    e.dma_start(out=VB[:, v * D:(v + 1) * D], in_=vscr[v]).then_inc(sems["ldg"], 16)
                for i in range(NIT):
                    b, qb, h = hdr(i)
                    if i >= 2:
                        e.wait_ge(sems["tt"], tt_scores[(i - 2, 15)])
                    sm_ = sems["lda"] if i % 2 == 0 else sems["ldb"]
                    e.dma_start(out=KH[i % 2][:],
                                in_=kfm[h * DH:(h + 1) * DH, b * S:(b + 1) * S]).then_inc(sm_, 16)
                    e.dma_start(out=QH[i % 2][:],
                                in_=qfm[h * DH:(h + 1) * DH, b * S + qb * 512: b * S + (qb + 1) * 512]).then_inc(sm_, 16)
                    if i == 65:
                        e.wait_ge(sems["tt"], tt_av[63])
                        for v in range(16):
                            e.dma_start(out=VB[:, v * D:(v + 1) * D], in_=vscr[16 + v]).then_inc(sems["ldg"], 16)

            def f_tensor(e):
                e.transpose(ps[7][0:1, 0:128], mtmp[:, 1:2], ident_s[:]).then_inc(sems["tt"], 1)
                e.wait_ge(sems["vv"], vv_m1[0])
                e.matmul(ps[7][:, 256:257], negr_s[:, 0:128], sc[0:1, 0:1],
                         start=True, stop=True).then_inc(sems["tt"], 1)
                for i in range(NIT + 2):
                    if i < NIT:
                        sm_ = sems["lda"] if i % 2 == 0 else sems["ldb"]
                        e.wait_ge(sm_, (la0 if i % 2 == 0 else lb0) + 32 * (i // 2 + 1))
                        for kc in range(16):
                            if kc % 4 == 0 and i * 16 + kc >= 4:
                                e.wait_ge(sems["vv"], vv_evac[i * 16 + kc - 1])
                            e.matmul(ps[kc % 4][:], KH[i % 2][:, kc * 128:(kc + 1) * 128],
                                     QH[i % 2][:], start=True, stop=True).then_inc(sems["tt"], 1)
                            if kc == 3 and i >= 1:
                                e.wait_ge(sems["vv"], vv_m1[i])
                                e.wait_ge(sems["vv"], vv_negm[i - 1])
                                e.matmul(ps[7][:, 256:257], negr_s[:, 0:128], sc[0:1, 0:1],
                                         start=True, stop=True).then_inc(sems["tt"], 1)
                    if 2 <= i <= NIT + 1:
                        p2 = i - 2
                        e.wait_ge(sems["vv"], vv_zw[p2])
                        if p2 >= 1:
                            e.wait_ge(sems["vv"], vv_rzb[p2 - 1])
                        e.matmul(ps[6][0:1, 0:512], ones_c[:], zw[:, 0:512],
                                 start=True, stop=True).then_inc(sems["tt"], 1)
                    if 1 <= i <= NIT:
                        p = i - 1
                        b, qb, h = hdr(p)
                        e.wait_ge(sems["ldg"], g0 + 16 * 16 * (b + 1))
                        if p >= 2:
                            e.wait_ge(sems["vv"], vv_em[p - 2])
                        for kc in range(16):
                            if kc % 4 == 0:
                                e.wait_ge(sems["ss"], ss0 + 4 * p + kc // 4 + 1)
                            mm = e.matmul(ps[4 + p % 2][0:64, :], VB[:, kc * D + h * DH: kc * D + h * DH + DH],
                                          ESd[p % 2][:, kc * 512:(kc + 1) * 512],
                                          start=(kc == 0), stop=(kc == 15))
                        mm.then_inc(sems["tt"], 1)
                    if 2 <= i <= NIT + 1:
                        p2 = i - 2
                        e.wait_ge(sems["vv"], vv_rzrow[p2])
                        e.matmul(ps[6][0:64, :], ones_s[:, 0:64], rzrow[:],
                                 start=True, stop=True).then_inc(sems["tt"], 1)
                    if i + 1 <= NIT - 1:
                        e.wait_ge(sems["vv"], vv_maxred[i])
                        e.wait_ge(sems["vv"], vv_m1[i])
                        e.transpose(ps[7][0:1, 0:128], mtmp[:, i % 2:i % 2 + 1], ident_s[:]).then_inc(sems["tt"], 1)

            def f_vector(e):
                for i in range(NIT + 2):
                    if i < NIT:
                        b, qb, h = hdr(i)
                        e.wait_ge(sems["tt"], tt_Mtp[i])
                        VF.step(e, e.tensor_copy(mrow[:], ps[7][0:1, 0:128]))
                        if i >= 1:
                            e.wait_ge(sems["tt"], tt_negbc[i - 1])
                        VF.step(e, e.reduce_max(sc[0:1, 0:1], mrow[:], axis=AX))
                        e.nop().then_inc(sems["vv"], 1)
                        if i >= 2:
                            e.wait_ge(sems["tt"], tt_av[i - 2])
                            e.wait_ge(sems["gg"], gg0 + i - 1)
                        for kc in range(16):
                            e.wait_ge(sems["tt"], tt_scores[(i, kc)])
                            VF.step(e, e.tensor_scalar(ESd[i % 2][:, kc * 512:(kc + 1) * 512], ps[kc % 4][:],
                                                       1.0, -3e38, AL.mult, AL.max,
                                                       accum_out=mt16[:, kc:kc + 1]))
                            e.nop().then_inc(sems["vv"], 1)
                            if kc == 3:
                                e.wait_ge(sems["tt"], tt_negbc[i])
                                e.wait_ge(sems["ss"], ss0 + 4 * i)
                                VF.step(e, e.tensor_copy(negm[:], ps[7][:, 256:257]))
                                e.nop().then_inc(sems["vv"], 1)
                        VF.step(e, e.reduce_max(mtmp[:, i % 2:i % 2 + 1], mt16[:], axis=AX))
                        e.nop().then_inc(sems["vv"], 1)
                    if 2 <= i <= NIT + 1:
                        p2 = i - 2
                        b, qb, h = hdr(p2)
                        e.wait_ge(sems["tt"], tt_zmm[p2])
                        VF.step(e, e.reciprocal(rzrow[:], ps[6][0:1, 0:512]))
                        e.nop().then_inc(sems["vv"], 1)
                        e.wait_ge(sems["tt"], tt_rzbc[p2])
                        VF.step(e, e.tensor_copy(rzb[:], ps[6][0:64, :]))
                        e.nop().then_inc(sems["vv"], 1)
                        e.wait_ge(sems["tt"], tt_av[p2])
                        c = h // 2
                        gc = (b * 4 + qb) * 8 + c
                        if gc >= 2:
                            e.wait_ge(sems["spg"], sg0 + 16 * (gc - 1))
                        VF.step(e, e.tensor_tensor(ast[c % 2][(h % 2) * 64:(h % 2) * 64 + 64, :],
                                                   ps[4 + p2 % 2][0:64, :], rzb[:], AL.mult))
                        e.nop().then_inc(sems["vv"], 1)
                    if 1 <= i <= NIT:
                        p = i - 1
                        e.wait_ge(sems["gg"], gg0 + p + 1)
                        for lv in range(4):
                            VF.step(e, e.tensor_tensor(zw[:, lv * 512:(lv + 1) * 512],
                                                       zw[:, (2 * lv) * 512:(2 * lv + 1) * 512],
                                                       zw[:, (2 * lv + 1) * 512:(2 * lv + 2) * 512], AL.add))
                        for lv in range(2):
                            VF.step(e, e.tensor_tensor(zw[:, lv * 512:(lv + 1) * 512],
                                                       zw[:, (2 * lv) * 512:(2 * lv + 1) * 512],
                                                       zw[:, (2 * lv + 1) * 512:(2 * lv + 2) * 512], AL.add))
                        VF.step(e, e.tensor_tensor(zw[:, 0:512], zw[:, 0:512], zw[:, 512:1024], AL.add))
                        e.nop().then_inc(sems["vv"], 1)

            def f_scalar(e):
                for p in range(NIT):
                    e.wait_ge(sems["vv"], vv_negm[p])
                    for q4 in range(4):
                        e.wait_ge(sems["vv"], vv_evac[p * 16 + q4 * 4 + 3])
                        SF.step(e, e.activation(ESd[p % 2][:, q4 * 2048:(q4 + 1) * 2048],
                                                ESd[p % 2][:, q4 * 2048:(q4 + 1) * 2048],
                                                ACT.Exp, bias=negm[:], scale=1.0))
                        e.nop().then_inc(sems["ss"], 1)

            def f_gpsimd(e):
                gfn = GFN

                def spill(sp):
                    b, qb, h = hdr(sp)
                    e.wait_ge(sems["vv"], vv_em[sp])
                    gc = (b * 4 + qb) * 8 + h // 2
                    if gc >= 1:
                        e.wait_ge(sems["spg"], sg0 + 16 * gc)
                    c = h // 2
                    e.dma_start(out=attn_fm[b, qb, c], in_=ast[c % 2][:]).then_inc(sems["spg"], 16)
                for p in range(NIT):
                    b, qb, h = hdr(p)
                    if p >= 1:
                        e.wait_ge(sems["tt"], tt_zmm[p - 1])
                        e.wait_ge(sems["vv"], vv_zw[p - 1])
                    for lv in range(8):
                        if lv % 2 == 0:
                            e.wait_ge(sems["ss"], ss0 + 4 * p + lv // 2 + 1)
                        gfn[0] += 1
                        e.tensor_tensor(zw[:, lv * 512:(lv + 1) * 512],
                                        ESd[p % 2][:, (2 * lv) * 512:(2 * lv + 1) * 512],
                                        ESd[p % 2][:, (2 * lv + 1) * 512:(2 * lv + 2) * 512],
                                        AL.add).then_inc(sems["gf"], 1)
                        e.wait_ge(sems["gf"], gfn[0])
                    e.nop().then_inc(sems["gg"], 1)
                    if p >= 1 and hdr(p - 1)[2] % 2 == 1:
                        spill(p - 1)
                if hdr(NIT - 1)[2] % 2 == 1:
                    spill(NIT - 1)
                e.wait_ge(sems["spg"], sg0 + 16 * (NIT // 2))
            return {"sync": f_sync, "tensor": f_tensor, "scalar": f_scalar,
                    "vector": f_vector, "gpsimd": f_gpsimd}

        def ph_wo():
            vv0, ss0, tt0 = C["vv"], C["ss"], C["tt"]
            la0, lb0, g0, sa0, sb0 = C["lda"], C["ldb"], C["ldg"], C["spa"], C["spb"]
            C["ldg"] += 16
            C["lda"] += 32 * (NT // 2)
            C["ldb"] += 32 * (NT // 2)
            C["tt"] += 1 + 2 * NT
            C["vv"] += 1 + 5 * NT
            C["ss"] += 4 * NT
            C["spa"] += 16 * NT
            C["spb"] += 16 * NT

            def f_sync(e):
                dst = bass.AP(woS, 0, [[KC * D, 128], [D, KC], [1, D]])
                srcv = bass.AP(wo.tensor, wo.offset, [[D, 128], [128 * D, KC], [1, D]])
                e.dma_start(out=dst, in_=srcv).then_inc(sems["ldg"], 16)
                for t in range(NT):
                    if t >= 2:
                        e.wait_ge(sems["tt"], tt0 + 1 + 2 * (t - 2) + 2)
                        e.wait_ge(sems["vv"], vv0 + 1 + 5 * (t - 2) + 3)
                    sm_ = sems["lda"] if t % 2 == 0 else sems["ldb"]
                    tb_, qb_, j_ = t // 16, (t % 16) // 4, t % 4
                    afsrc = bass.AP(attn_fm.tensor,
                                    attn_fm.offset + ((tb_ * 4 + qb_) * KC) * 128 * 512 + j_ * 128,
                                    [[512, 128], [128 * 512, KC], [1, 128]])
                    e.dma_start(out=afm[t % 2][:], in_=afsrc).then_inc(sm_, 16)
                    e.dma_start(out=x0r[t % 2][:], in_=x0[t * 128:(t + 1) * 128, :]).then_inc(sm_, 16)

            def f_tensor(e):
                e.wait_ge(sems["ldg"], g0 + 16)
                mm = e.matmul(ps[2][:, 0:512], ones_s[:, 0:128], bo_s[:, 0:512], start=True, stop=True)
                mm = e.matmul(ps[3][:, 0:512], ones_s[:, 0:128], bo_s[:, 512:1024], start=True, stop=True)
                mm.then_inc(sems["tt"], 1)
                for t in range(NT):
                    sm_ = sems["lda"] if t % 2 == 0 else sems["ldb"]
                    e.wait_ge(sm_, (la0 if t % 2 == 0 else lb0) + 32 * (t // 2 + 1))
                    if t >= 1:
                        e.wait_ge(sems["vv"], vv0 + 1 + 5 * (t - 1) + 1)
                    for db in range(2):
                        p = ps[4 + db]
                        for c in range(KC):
                            mm = e.matmul(p[:], afm[t % 2][:, c * 128:(c + 1) * 128],
                                          woS[:, c * D + db * 512: c * D + (db + 1) * 512],
                                          start=(c == 0), stop=(c == KC - 1))
                        mm.then_inc(sems["tt"], 1)

            def f_vector(e):
                e.wait_ge(sems["tt"], tt0 + 1)
                VF.step(e, e.tensor_copy(bobc[:, 0:512], ps[2][:, 0:512]))
                VF.step(e, e.tensor_copy(bobc[:, 512:1024], ps[3][:, 0:512]))
                e.nop().then_inc(sems["vv"], 1)
                for t in range(NT):
                    e.wait_ge(sems["tt"], tt0 + 1 + 2 * t + 1)
                    if t >= 1:
                        e.wait_ge(sems["spa"], sa0 + 16 * t)
                        e.wait_ge(sems["ss"], ss0 + 4 * (t - 1) + 3)
                    VF.step(e, e.scalar_tensor_tensor(x1t[:, 0:512], ps[4][:], 1.0, bobc[:, 0:512],
                                                      AL.mult, AL.add, accum_out=stats[:, 0:1]))
                    e.wait_ge(sems["tt"], tt0 + 1 + 2 * t + 2)
                    VF.step(e, e.scalar_tensor_tensor(x1t[:, 512:1024], ps[5][:], 1.0, bobc[:, 512:1024],
                                                      AL.mult, AL.add, accum_out=stats[:, 12:13]))
                    VF.step(e, e.tensor_tensor(stats[:, 0:1], stats[:, 0:1], stats[:, 12:13], AL.add))
                    e.nop().then_inc(sems["vv"], 1)
                    VF.step(e, e.tensor_scalar(stats[:, 1:2], stats[:, 0:1], 1.0 / D, None, AL.mult, AL.bypass))
                    e.wait_ge(sems["ss"], ss0 + 4 * t + 1)
                    VF.step(e, e.tensor_scalar(stats[:, 3:4], stats[:, 2:3], 1.0 / D, None, AL.mult, AL.bypass))
                    VF.step(e, e.tensor_tensor(stats[:, 4:5], stats[:, 1:2], stats[:, 1:2], AL.mult))
                    VF.step(e, e.tensor_tensor(stats[:, 5:6], stats[:, 3:4], stats[:, 4:5], AL.subtract))
                    e.nop().then_inc(sems["vv"], 1)
                    e.wait_ge(sems["ss"], ss0 + 4 * t + 2)
                    VF.step(e, e.reciprocal(stats[:, 7:8], stats[:, 6:7]))
                    VF.step(e, e.tensor_scalar(stats[:, 8:9], stats[:, 5:6], EPS, None, AL.add, AL.bypass))
                    for _ in range(2):
                        VF.step(e, e.tensor_tensor(stats[:, 9:10], stats[:, 7:8], stats[:, 7:8], AL.mult))
                        VF.step(e, e.tensor_tensor(stats[:, 9:10], stats[:, 9:10], stats[:, 8:9], AL.mult))
                        VF.step(e, e.tensor_scalar(stats[:, 9:10], stats[:, 9:10], -0.5, 1.5, AL.mult, AL.add))
                        VF.step(e, e.tensor_tensor(stats[:, 7:8], stats[:, 7:8], stats[:, 9:10], AL.mult))
                    VF.step(e, e.tensor_scalar(x1t[:], x1t[:], stats[:, 1:2], stats[:, 7:8], AL.subtract, AL.mult))
                    sm_ = sems["lda"] if t % 2 == 0 else sems["ldb"]
                    e.wait_ge(sm_, (la0 if t % 2 == 0 else lb0) + 32 * (t // 2 + 1))
                    VF.step(e, e.scalar_tensor_tensor(x1t[:], x1t[:], 1.0, x0r[t % 2][:],
                                                      AL.mult, AL.add, accum_out=stats[:, 10:11]))
                    e.nop().then_inc(sems["vv"], 1)
                    VF.step(e, e.tensor_scalar(stats[:, 11:12], stats[:, 10:11], 1.0 / D, None, AL.mult, AL.bypass))
                    e.wait_ge(sems["ss"], ss0 + 4 * t + 3)
                    VF.step(e, e.tensor_scalar(stats[:, 13:14], stats[:, 12:13], 1.0 / D, None, AL.mult, AL.bypass))
                    VF.step(e, e.tensor_tensor(stats[:, 14:15], stats[:, 11:12], stats[:, 11:12], AL.mult))
                    VF.step(e, e.tensor_tensor(stats[:, 5:6], stats[:, 13:14], stats[:, 14:15], AL.subtract))
                    e.nop().then_inc(sems["vv"], 1)
                    e.wait_ge(sems["ss"], ss0 + 4 * t + 4)
                    VF.step(e, e.reciprocal(stats[:, 7:8], stats[:, 6:7]))
                    VF.step(e, e.tensor_scalar(stats[:, 8:9], stats[:, 5:6], EPS, None, AL.add, AL.bypass))
                    for _ in range(2):
                        VF.step(e, e.tensor_tensor(stats[:, 9:10], stats[:, 7:8], stats[:, 7:8], AL.mult))
                        VF.step(e, e.tensor_tensor(stats[:, 9:10], stats[:, 9:10], stats[:, 8:9], AL.mult))
                        VF.step(e, e.tensor_scalar(stats[:, 9:10], stats[:, 9:10], -0.5, 1.5, AL.mult, AL.add))
                        VF.step(e, e.tensor_tensor(stats[:, 7:8], stats[:, 7:8], stats[:, 9:10], AL.mult))
                    if t >= 1:
                        e.wait_ge(sems["spb"], sb0 + 16 * t)
                    VF.step(e, e.tensor_scalar(h2t[:], x1t[:], stats[:, 11:12], stats[:, 7:8], AL.subtract, AL.mult))
                    VF.step(e, e.reduce_max(mtmp[:, 0:1], h2t[:], axis=AX, apply_absolute_value=True))
                    VF.step(e, e.tensor_tensor(macc[:, 0:1], macc[:, 0:1], mtmp[:, 0:1], AL.max))
                    e.nop().then_inc(sems["vv"], 1)

            def f_scalar(e):
                for t in range(NT):
                    e.wait_ge(sems["vv"], vv0 + 1 + 5 * t + 1)
                    SF.step(e, e.activation(sqscr[:], x1t[:], ACT.Square, accum_out=stats[:, 2:3]))
                    e.nop().then_inc(sems["ss"], 1)
                    e.wait_ge(sems["vv"], vv0 + 1 + 5 * t + 2)
                    SF.step(e, e.activation(stats[:, 6:7], stats[:, 5:6], ACT.Sqrt, bias=epsv[:]))
                    e.nop().then_inc(sems["ss"], 1)
                    e.wait_ge(sems["vv"], vv0 + 1 + 5 * t + 3)
                    SF.step(e, e.activation(sqscr[:], x1t[:], ACT.Square, accum_out=stats[:, 12:13]))
                    e.nop().then_inc(sems["ss"], 1)
                    e.wait_ge(sems["vv"], vv0 + 1 + 5 * t + 4)
                    SF.step(e, e.activation(stats[:, 6:7], stats[:, 5:6], ACT.Sqrt, bias=epsv[:]))
                    e.nop().then_inc(sems["ss"], 1)

            def f_gpsimd(e):
                for t in range(NT):
                    e.wait_ge(sems["vv"], vv0 + 1 + 5 * t + 3)
                    e.dma_start(out=x1scr[t * 128:(t + 1) * 128, :], in_=x1t[:]).then_inc(sems["spa"], 16)
                    e.wait_ge(sems["vv"], vv0 + 1 + 5 * t + 5)
                    e.dma_start(out=h2scr[t * 128:(t + 1) * 128, :], in_=h2t[:]).then_inc(sems["spb"], 16)
                e.wait_ge(sems["spa"], sa0 + 16 * NT)
                e.wait_ge(sems["spb"], sb0 + 16 * NT)
            return {"sync": f_sync, "tensor": f_tensor, "vector": f_vector,
                    "scalar": f_scalar, "gpsimd": f_gpsimd}



        def ph_ar(idx, cols, slots, chain, nchain, bcast_cols):
            vv0, tt0, gg0, cc0 = C["vv"], C["tt"], C["gg"], C["cc"]
            sg0 = C["spg"]
            C["tt"] += 2
            C["vv"] += 3
            C["gg"] += 1
            C["spg"] += 32
            C["cc"] += 1

            def f_tensor(e):
                for i, col in enumerate(cols):
                    mm = e.transpose(ps[2][0:1, i * 128:(i + 1) * 128], macc[:, col:col + 1], ident_s[:])
                mm.then_inc(sems["tt"], 1)
                e.wait_ge(sems["vv"], vv0 + 2)
                e.matmul(ps[3][:, 0:bcast_cols], ones_s[:, 0:128], sc[0:1, 0:bcast_cols],
                         start=True, stop=True).then_inc(sems["tt"], 1)

            def f_vector(e):
                e.wait_ge(sems["tt"], tt0 + 1)
                VF.step(e, e.memset(arv_s[:], 0.0))
                for i, col in enumerate(cols):
                    VF.step(e, e.tensor_copy(mrow[:], ps[2][0:1, i * 128:(i + 1) * 128]))
                    VF.step(e, e.reduce_max(sc[0:1, 8 + i:9 + i], mrow[:], axis=AX))
                    VF.step(e, e.tensor_scalar(arv_s[0:1, slots[i]:slots[i] + 8],
                                               slot_s[0:1, slots[i]:slots[i] + 8],
                                               sc[0:1, 8 + i:9 + i], None, AL.mult, AL.bypass))
                e.nop().then_inc(sems["vv"], 1)
                e.wait_ge(sems["gg"], gg0 + 1)
                chain(e)
                e.nop().then_inc(sems["vv"], 1)
                e.wait_ge(sems["tt"], tt0 + 2)
                VF.step(e, e.tensor_copy(bc[:, 0:bcast_cols], ps[3][:, 0:bcast_cols]))
                e.nop().then_inc(sems["vv"], 1)

            def f_gpsimd(e):
                e.wait_ge(sems["vv"], vv0 + 1)
                e.dma_start(out=ccin[idx].ap(), in_=arv_s[:]).then_inc(sems["spg"], 16)
                e.wait_ge(sems["spg"], sg0 + 16)
                e.collective_compute(
                    "AllReduce", AL.add,
                    replica_groups=[list(range(n_cores))],
                    ins=[ccin[idx].ap().opt()],
                    outs=[ccout[idx].ap().opt()],
                ).then_inc(sems["cc"], 1)
                e.wait_ge(sems["cc"], cc0 + 1)
                e.dma_start(out=arr_s[:], in_=ccout[idx].ap()).then_inc(sems["spg"], 16)
                e.wait_ge(sems["spg"], sg0 + 32)
                e.nop().then_inc(sems["gg"], 1)
            return {"tensor": f_tensor, "vector": f_vector, "gpsimd": f_gpsimd}

        def ph_x():
            vv0, tt0 = C["vv"], C["tt"]
            la0, lb0 = C["lda"], C["ldb"]
            C["lda"] += 16 * (NT // 2)
            C["ldb"] += 16 * (NT // 2)
            C["vv"] += 2 * NT
            C["tt"] += 2 * NT

            def f_sync(e):
                for t in range(NT):
                    if t >= 2:
                        e.wait_ge(sems["vv"], vv0 + 2 * (t - 2) + 2)
                    sm_ = sems["lda"] if t % 2 == 0 else sems["ldb"]
                    e.dma_start(out=h2in[t % 2][:], in_=h2scr[t * 128:(t + 1) * 128, :]).then_inc(sm_, 16)

            def f_vector(e):
                for t in range(NT):
                    sm_ = sems["lda"] if t % 2 == 0 else sems["ldb"]
                    e.wait_ge(sm_, (la0 if t % 2 == 0 else lb0) + 16 * (t // 2 + 1))
                    VF.step(e, e.tensor_scalar(h2in[t % 2][:], h2in[t % 2][:], bc[:, 0:1], MAGIC, AL.mult, AL.add))
                    VF.step(e, e.tensor_scalar(xh[:], h2in[t % 2][:], MAGIC, None, AL.subtract, AL.bypass))
                    VF.step(e, e.tensor_scalar(h2in[t % 2][:], h2in[t % 2][:], MAGIC, None, AL.subtract, AL.bypass))
                    VF.step(e, e.scalar_tensor_tensor(xl[:], h2in[t % 2][:], 1.0, xh[:],
                                                      AL.mult, AL.subtract))
                    e.nop().then_inc(sems["vv"], 1)
                    for half, (dstt, _) in enumerate(((xfmh, 0), (xfml, 0))):
                        e.wait_ge(sems["tt"], tt0 + 2 * t + half + 1)
                        dst = bass.AP(dstt, t * 128, [[KC * TOK, 128], [TOK, KC], [1, 128]])
                        VF.step(e, e.tensor_copy(dst, ps[half][:].bitcast(mybir.dt.bfloat16)))
                    e.nop().then_inc(sems["vv"], 1)

            def f_tensor(e):
                for t in range(NT):
                    e.wait_ge(sems["vv"], vv0 + 2 * t + 1)
                    if t >= 1:
                        e.wait_ge(sems["vv"], vv0 + 2 * t)
                    for half, src in enumerate((xh, xl)):
                        pb = ps[half][:].bitcast(mybir.dt.bfloat16)
                        for c in range(KC):
                            mm = e.transpose(pb[:, c * 128:(c + 1) * 128],
                                             src[:, c * 128:(c + 1) * 128], ident_b[:])
                        mm.then_inc(sems["tt"], 1)
            return {"sync": f_sync, "vector": f_vector, "tensor": f_tensor}

        def ph_upgate():
            vv0, ss0, tt0 = C["vv"], C["ss"], C["tt"]
            la0, lb0, sa0, sb0 = C["lda"], C["ldb"], C["spa"], C["spb"]
            mats = [("u", wupT, u0scr, 1), ("g", wgT, g0scr, 2)]
            NI = 2 * IT
            C["lda"] += 32 * (NI // 2)
            C["ldb"] += 32 * (NI // 2)
            C["tt"] += NI * TB
            C["ss"] += NI * TB
            C["vv"] += NI * (TB * 2 + 1)
            C["spa"] += 16 * (NI // 2)
            C["spb"] += 16 * (NI // 2)

            def f_sync(e):
                for mi, (kind, wsrc, dscr, mcol) in enumerate(mats):
                    for r in range(IT):
                        i = mi * IT + r
                        if i >= 2:
                            e.wait_ge(sems["tt"], tt0 + TB * (i - 1))
                        sm_ = sems["lda"] if i % 2 == 0 else sems["ldb"]
                        for j, dstw in ((0, wT7h), (1, wT7l)):
                            e.dma_start(out=dstw[i % 2][:],
                                        in_=bass.AP(wsrc[j].tensor, wsrc[j].offset + r * KC * 128 * 128,
                                                    [[128, 128], [128 * 128, KC], [1, 128]])).then_inc(sm_, 16)

            def f_tensor(e):
                for mi, (kind, wsrc, dscr, mcol) in enumerate(mats):
                    for r in range(IT):
                        i = mi * IT + r
                        sm_ = sems["lda"] if i % 2 == 0 else sems["ldb"]
                        e.wait_ge(sm_, (la0 if i % 2 == 0 else lb0) + 32 * (i // 2 + 1))
                        for tb in range(TB):
                            g = i * TB + tb
                            if g >= 2:
                                e.wait_ge(sems["ss"], ss0 + g - 1)
                            p = ps[g % 2]
                            for c in range(KC):
                                xs_h = xfmh[:, c * TOK + tb * 512: c * TOK + (tb + 1) * 512]
                                xs_l = xfml[:, c * TOK + tb * 512: c * TOK + (tb + 1) * 512]
                                e.matmul(p[:], wT7h[i % 2][:, c * 128:(c + 1) * 128], xs_h,
                                         start=(c == 0), stop=False)
                                e.matmul(p[:], wT7l[i % 2][:, c * 128:(c + 1) * 128], xs_h,
                                         start=False, stop=False)
                                mm = e.matmul(p[:], wT7h[i % 2][:, c * 128:(c + 1) * 128], xs_l,
                                              start=False, stop=(c == KC - 1))
                            mm.then_inc(sems["tt"], 1)

            def f_scalar(e):
                for i in range(NI):
                    if i >= 2:
                        pit = i - 2
                        sm_ = sems["spa"] if pit % 2 == 0 else sems["spb"]
                        e.wait_ge(sm_, (sa0 if pit % 2 == 0 else sb0) + 16 * (pit // 2 + 1))
                    for tb in range(TB):
                        g = i * TB + tb
                        e.wait_ge(sems["tt"], tt0 + g + 1)
                        SF.step(e, e.activation(u0j[i % 2][:, tb * 512:(tb + 1) * 512], ps[g % 2][:],
                                                ACT.Copy, bias=0.0, scale=1.0))
                        e.nop().then_inc(sems["ss"], 1)

            def f_vector(e):
                for mi, (kind, wsrc, dscr, mcol) in enumerate(mats):
                    for r in range(IT):
                        i = mi * IT + r
                        for tb in range(TB):
                            g = i * TB + tb
                            e.wait_ge(sems["ss"], ss0 + g + 1)
                            VF.step(e, e.reduce_max(mtmp[:, 1:2], u0j[i % 2][:, tb * 512:(tb + 1) * 512],
                                                    axis=AX, apply_absolute_value=True))
                            e.nop().then_inc(sems["vv"], 1)
                            VF.step(e, e.tensor_tensor(macc[:, mcol:mcol + 1], macc[:, mcol:mcol + 1],
                                                       mtmp[:, 1:2], AL.max))
                            e.nop().then_inc(sems["vv"], 1)
                        e.nop().then_inc(sems["vv"], 1)

            def f_gpsimd(e):
                for mi, (kind, wsrc, dscr, mcol) in enumerate(mats):
                    for r in range(IT):
                        i = mi * IT + r
                        e.wait_ge(sems["vv"], vv0 + (TB * 2 + 1) * (i + 1))
                        sm_ = sems["spa"] if i % 2 == 0 else sems["spb"]
                        e.dma_start(out=dscr[r], in_=u0j[i % 2][:]).then_inc(sm_, 16)
                e.wait_ge(sems["spa"], sa0 + 16 * (NI // 2))
                e.wait_ge(sems["spb"], sb0 + 16 * (NI // 2))
            return {"sync": f_sync, "tensor": f_tensor, "scalar": f_scalar,
                    "vector": f_vector, "gpsimd": f_gpsimd}

        def ph_ffnA():
            vv0, ss0 = C["vv"], C["ss"]
            ld0 = (C["lda"], C["ldb"], C["ldg"])
            ldn = ("lda", "ldb", "ldg")
            C["lda"] += 16 * 8
            C["ldb"] += 16 * 8
            C["ldg"] += 16 * 8
            C["ss"] += 2 * IT
            C["vv"] += 2 * IT
            ss_p1g = {r: ss0 + 2 * r + 1 for r in range(IT)}
            ss_sp1 = {r: ss0 + 2 * r + 2 for r in range(IT)}
            vv_gq = {}; vv_dn = {}
            vvc = vv0
            for r in range(IT):
                vvc += 1; vv_gq[r] = vvc
                if r >= 1:
                    vvc += 1; vv_dn[r - 1] = vvc
            vvc += 1; vv_dn[IT - 1] = vvc
            assert vvc == vv0 + 2 * IT

            def f_sync(e):
                for r in range(IT):
                    if r >= 3:
                        e.wait_ge(sems["vv"], vv_dn[r - 3])
                    e.dma_start(out=jb[r % 3][:], in_=g0scr[r]).then_inc(sems[ldn[r % 3]], 16)

            def f_scalar(e):
                for r in range(IT):
                    e.wait_ge(sems[ldn[r % 3]], ld0[r % 3] + 16 * (r // 3 + 1))
                    if r >= 3:
                        e.wait_ge(sems["vv"], vv_dn[r - 3])
                    SF.step(e, e.activation(jb[3 + r % 3][:], jb[r % 3][:], ACT.Copy,
                                            bias=MAGIC, scale=bc[:, 3:4]))
                    e.nop().then_inc(sems["ss"], 1)
                    e.wait_ge(sems["vv"], vv_gq[r])
                    if r >= 2:
                        e.wait_ge(sems["vv"], vv_dn[r - 2])
                    SF.step(e, e.activation(jb[6 + r % 2][:], jb[3 + r % 3][:], ACT.Sigmoid))
                    SF.step(e, e.activation(jb[6 + r % 2][:], jb[6 + r % 2][:], ACT.Copy,
                                            bias=MAGIC, scale=65536.0))
                    e.nop().then_inc(sems["ss"], 1)

            def f_vector(e):
                def tail(p):
                    e.wait_ge(sems["ss"], ss_sp1[p])
                    VF.step(e, e.tensor_scalar(jb[6 + p % 2][:], jb[6 + p % 2][:], MAGIC,
                                               65535.0, AL.subtract, AL.min))
                    VF.step(e, e.tensor_tensor(jb[6 + p % 2][:], jb[6 + p % 2][:],
                                               jb[3 + p % 3][:], AL.mult))
                    VF.step(e, e.reduce_max(mtmp[:, 1:2], jb[6 + p % 2][:], axis=AX,
                                            apply_absolute_value=True))
                    VF.step(e, e.tensor_tensor(macc[:, 3:4], macc[:, 3:4], mtmp[:, 1:2], AL.max))
                    e.nop().then_inc(sems["vv"], 1)
                for r in range(IT):
                    e.wait_ge(sems["ss"], ss_p1g[r])
                    VF.step(e, e.tensor_scalar(jb[3 + r % 3][:], jb[3 + r % 3][:], MAGIC,
                                               bc[:, 4:5], AL.subtract, AL.mult))
                    e.nop().then_inc(sems["vv"], 1)
                    if r >= 1:
                        tail(r - 1)
                tail(IT - 1)
            return {"sync": f_sync, "scalar": f_scalar, "vector": f_vector}

        def ph_ffnB():
            vv0, ss0 = C["vv"], C["ss"]
            sa0, sb0 = C["spa"], C["spb"]
            ld0 = (C["lda"], C["ldb"], C["ldg"])
            ldn = ("lda", "ldb", "ldg")
            C["lda"] += 32 * 8
            C["ldb"] += 32 * 8
            C["ldg"] += 32 * 8
            C["ss"] += 4 * IT
            C["vv"] += 3 * IT
            C["spa"] += 16 * (IT - IT // 2)
            C["spb"] += 16 * (IT // 2)
            ss_p1g = {}; ss_p1u = {}; ss_sp1 = {}; ss_p1t = {}
            ssc = ss0
            for r in range(IT):
                ssc += 1; ss_p1g[r] = ssc
                ssc += 1; ss_p1u[r] = ssc
                ssc += 1; ss_sp1[r] = ssc
                if r >= 1:
                    ssc += 1; ss_p1t[r - 1] = ssc
            ssc += 1; ss_p1t[IT - 1] = ssc
            assert ssc == ss0 + 4 * IT
            vv_gq = {}; vv_t0 = {}; vv_op = {}
            vvc = vv0
            for r in range(IT + 2):
                if r < IT:
                    vvc += 1; vv_gq[r] = vvc
                if 1 <= r < IT + 1:
                    vvc += 1; vv_t0[r - 1] = vvc
                if r >= 2:
                    vvc += 1; vv_op[r - 2] = vvc
            assert vvc == vv0 + 3 * IT
            G3 = [jb[0], jb[1], jbx[0]]
            U3 = [jb[2], jb[3], jbx[1]]

            def f_sync(e):
                for r in range(IT):
                    if r >= 3:
                        e.wait_ge(sems["vv"], vv_op[r - 3])
                        sp_ = sems["spa"] if (r - 3) % 2 == 0 else sems["spb"]
                        e.wait_ge(sp_, (sa0 if (r - 3) % 2 == 0 else sb0) + 16 * ((r - 3) // 2 + 1))
                    sm_ = sems[ldn[r % 3]]
                    e.dma_start(out=G3[r % 3][:], in_=g0scr[r]).then_inc(sm_, 16)
                    e.dma_start(out=U3[r % 3][:], in_=u0scr[r]).then_inc(sm_, 16)

            def f_scalar(e):
                for r in range(IT):
                    e.wait_ge(sems[ldn[r % 3]], ld0[r % 3] + 32 * (r // 3 + 1))
                    if r >= 2:
                        e.wait_ge(sems["vv"], vv_t0[r - 2])
                    SF.step(e, e.activation(jb[4 + r % 2][:], G3[r % 3][:], ACT.Copy,
                                            bias=MAGIC, scale=bc[:, 3:4]))
                    e.nop().then_inc(sems["ss"], 1)
                    SF.step(e, e.activation(U3[r % 3][:], U3[r % 3][:], ACT.Copy,
                                            bias=MAGIC, scale=bc[:, 1:2]))
                    e.nop().then_inc(sems["ss"], 1)
                    e.wait_ge(sems["vv"], vv_gq[r])
                    if r >= 2:
                        e.wait_ge(sems["vv"], vv_op[r - 2])
                    SF.step(e, e.activation(jb[6 + r % 2][:], jb[4 + r % 2][:], ACT.Sigmoid))
                    SF.step(e, e.activation(jb[6 + r % 2][:], jb[6 + r % 2][:], ACT.Copy,
                                            bias=MAGIC, scale=65536.0))
                    e.nop().then_inc(sems["ss"], 1)
                    if r >= 1:
                        e.wait_ge(sems["vv"], vv_t0[r - 1])
                        SF.step(e, e.activation(jb[6 + (r - 1) % 2][:], jb[6 + (r - 1) % 2][:],
                                                ACT.Copy, bias=MAGIC, scale=bc[:, 5:6]))
                        e.nop().then_inc(sems["ss"], 1)
                e.wait_ge(sems["vv"], vv_t0[IT - 1])
                SF.step(e, e.activation(jb[6 + (IT - 1) % 2][:], jb[6 + (IT - 1) % 2][:],
                                        ACT.Copy, bias=MAGIC, scale=bc[:, 5:6]))
                e.nop().then_inc(sems["ss"], 1)

            def f_vector(e):
                def tail1(p):
                    e.wait_ge(sems["ss"], ss_sp1[p])
                    VF.step(e, e.tensor_scalar(jb[6 + p % 2][:], jb[6 + p % 2][:], MAGIC,
                                               65535.0, AL.subtract, AL.min))
                    VF.step(e, e.tensor_tensor(jb[6 + p % 2][:], jb[6 + p % 2][:],
                                               jb[4 + p % 2][:], AL.mult))
                    e.nop().then_inc(sems["vv"], 1)
                    e.wait_ge(sems["ss"], ss_p1u[p])
                    VF.step(e, e.tensor_scalar(U3[p % 3][:], U3[p % 3][:], MAGIC,
                                               bc[:, 2:3], AL.subtract, AL.mult))

                def tail2(p):
                    e.wait_ge(sems["ss"], ss_p1t[p])
                    VF.step(e, e.scalar_tensor_tensor(G3[p % 3][:], jb[6 + p % 2][:], MAGIC,
                                                      U3[p % 3][:], AL.subtract, AL.mult))
                    VF.step(e, e.reduce_max(mtmp[:, 1:2], G3[p % 3][:], axis=AX,
                                            apply_absolute_value=True))
                    VF.step(e, e.tensor_tensor(macc[:, 3:4], macc[:, 3:4], mtmp[:, 1:2], AL.max))
                    e.nop().then_inc(sems["vv"], 1)
                for r in range(IT + 2):
                    if r < IT:
                        e.wait_ge(sems["ss"], ss_p1g[r])
                        VF.step(e, e.tensor_scalar(jb[4 + r % 2][:], jb[4 + r % 2][:], MAGIC,
                                                   bc[:, 4:5], AL.subtract, AL.mult))
                        e.nop().then_inc(sems["vv"], 1)
                    if 1 <= r < IT + 1:
                        tail1(r - 1)
                    if r >= 2:
                        tail2(r - 2)

            def f_gpsimd(e):
                for r in range(IT):
                    e.wait_ge(sems["vv"], vv_op[r])
                    sp_ = sems["spa"] if r % 2 == 0 else sems["spb"]
                    if r >= 2:
                        e.wait_ge(sp_, (sa0 if r % 2 == 0 else sb0) + 16 * (r // 2))
                    e.dma_start(out=opscr[r], in_=G3[r % 3][:]).then_inc(sp_, 16)
                e.wait_ge(sems["spa"], sa0 + 16 * (IT - IT // 2))
                e.wait_ge(sems["spb"], sb0 + 16 * (IT // 2))
            return {"sync": f_sync, "scalar": f_scalar, "vector": f_vector, "gpsimd": f_gpsimd}

        def ph_down():
            vv0, ss0, tt0 = C["vv"], C["ss"], C["tt"]
            la0, lb0, g0, sg0 = C["lda"], C["ldb"], C["ldg"], C["spg"]
            RNG = 8
            C["lda"] += 16 * 12 * TB
            C["ldb"] += 16 * 12 * TB
            C["ldg"] += 16 * (2 + 4 * TB)
            DCTX["wds_ldg"] = g0 + 32
            C["spg"] += 16 * 4 * TB
            tt_mm = {}; tt_tr = {}
            ttc = tt0
            for tb in range(TB):
                for r in range(IT):
                    ttc += 1; tt_mm[(tb, r)] = ttc
                for j in range(4):
                    ttc += 1; tt_tr[(tb, j)] = ttc
            C["tt"] = ttc
            ss_p1o = {}; ss_sq = {}; ss_sqrt = {}
            ssc = ss0
            for tb in range(TB + 1):
                if tb < TB:
                    for r in range(0, 8):
                        ssc += 1; ss_p1o[(tb, r)] = ssc
                if tb >= 1:
                    for j in range(4):
                        ssc += 1; ss_sq[(tb - 1, j)] = ssc
                        ssc += 1; ss_sqrt[(tb - 1, j)] = ssc
                if tb < TB:
                    for r in range(8, 24):
                        ssc += 1; ss_p1o[(tb, r)] = ssc
            C["ss"] = ssc
            vv_evac = {}; vv_slice = {}; vv_trev = {}; vv_var = {}; vv_out = {}
            vvc = vv0
            for tb in range(TB + 1):
                if tb >= 1:
                    for dt in range(KC):
                        vvc += 1; vv_evac[(tb - 1, dt)] = vvc
                if tb < TB:
                    for r in range(0, 8):
                        vvc += 1; vv_slice[(tb, r)] = vvc
                if tb >= 1:
                    for j in range(4):
                        vvc += 1; vv_trev[(tb - 1, j)] = vvc
                        vvc += 1; vv_var[(tb - 1, j)] = vvc
                        vvc += 1; vv_out[(tb - 1, j)] = vvc
                if tb < TB:
                    for r in range(8, 24):
                        vvc += 1; vv_slice[(tb, r)] = vvc
            C["vv"] = vvc

            def prev_slot_user(tb, r):
                if r >= RNG:
                    return (tb, r - RNG)
                if tb >= 1:
                    return (tb - 1, IT - RNG + r)
                return None

            def f_sync(e):
                for j, dstw in ((0, wdSh), (1, wdSl)):
                    dst = bass.AP(dstw, 0, [[KC * IT * 128, 128], [128, KC * IT], [1, 128]])
                    src = bass.AP(wdT[j].tensor, wdT[j].offset, [[128, 128], [128 * 128, KC * IT], [1, 128]])
                    e.dma_start(out=dst, in_=src).then_inc(sems["ldg"], 16)
                for tb in range(TB):
                    for r in range(IT):
                        gi = tb * IT + r
                        if gi >= 2:
                            e.wait_ge(sems["vv"], vv_slice[divmod(gi - 2, IT)])
                        sm_ = sems["lda"] if r % 2 == 0 else sems["ldb"]
                        e.dma_start(out=opin[r % 2][:],
                                    in_=bass.AP(opscr.tensor, opscr.offset + r * 128 * TOK + tb * 512,
                                                [[TOK, 128], [1, 512]])).then_inc(sm_, 16)
                    for j in range(4):
                        tj = tb * 4 + j
                        if tj >= 4:
                            e.wait_ge(sems["vv"], vv_out[divmod(tj - 4, 4)])
                        e.wait_ge(sems["ldg"], g0 + 16 * (2 + tj))
                        e.dma_start(out=x1r[j][:],
                                    in_=x1scr[tb * 512 + j * 128: tb * 512 + (j + 1) * 128, :]).then_inc(sems["ldg"], 16)

            def f_scalar(e):
                def p1o(tb, r):
                    sm_ = sems["lda"] if r % 2 == 0 else sems["ldb"]
                    e.wait_ge(sm_, (la0 if r % 2 == 0 else lb0) + 16 * (tb * 12 + r // 2 + 1))
                    SF.step(e, e.activation(opin[r % 2][:], opin[r % 2][:], ACT.Copy,
                                            bias=MAGIC, scale=bc[:, 7:8]))
                    e.nop().then_inc(sems["ss"], 1)

                def lnact(tbp, j):
                    e.wait_ge(sems["vv"], vv_trev[(tbp, j)])
                    SF.step(e, e.activation(sqd[:], ffnt[:], ACT.Square, accum_out=stats[:, 2:3]))
                    e.nop().then_inc(sems["ss"], 1)
                    e.wait_ge(sems["vv"], vv_var[(tbp, j)])
                    SF.step(e, e.activation(stats[:, 6:7], stats[:, 5:6], ACT.Sqrt, bias=epsv[:]))
                    e.nop().then_inc(sems["ss"], 1)
                for tb in range(TB + 1):
                    if tb < TB:
                        for r in range(0, 8):
                            p1o(tb, r)
                    if tb >= 1:
                        for j in range(4):
                            lnact(tb - 1, j)
                    if tb < TB:
                        for r in range(8, 24):
                            p1o(tb, r)

            def f_tensor(e):
                e.wait_ge(sems["ldg"], DCTX["wds_ldg"])
                for tb in range(TB):
                    for r in range(IT):
                        e.wait_ge(sems["vv"], vv_slice[(tb, r)])
                        sl = r % RNG
                        oh_ = oslh[:, sl * 512:(sl + 1) * 512]
                        ol_ = osll[:, sl * 512:(sl + 1) * 512]
                        for dt in range(KC):
                            if r == 0 and tb >= 1:
                                e.wait_ge(sems["vv"], vv_evac[(tb - 1, dt)])
                                if dt in (4, 5):
                                    e.wait_ge(sems["vv"], vv_trev[(tb - 1, 3)])
                            wh = wdSh[:, (dt * IT + r) * 128:(dt * IT + r + 1) * 128]
                            wl = wdSl[:, (dt * IT + r) * 128:(dt * IT + r + 1) * 128]
                            e.matmul(ps[dt][:], wh, oh_, start=(r == 0), stop=False)
                            e.matmul(ps[dt][:], wl, oh_, start=False, stop=False)
                            mm = e.matmul(ps[dt][:], wh, ol_, start=False, stop=(r == IT - 1))
                        mm.then_inc(sems["tt"], 1)
                    for j in range(4):
                        e.wait_ge(sems["vv"], vv_evac[(tb, KC - 1)])
                        if j >= 1:
                            e.wait_ge(sems["vv"], vv_trev[(tb, j - 1)])
                        for g2 in range(2):
                            for k2 in range(4):
                                dt = g2 * 4 + k2
                                mm = e.transpose(ps[4 + g2][:, k2 * 128:(k2 + 1) * 128],
                                                 ffm[:, dt * 512 + j * 128: dt * 512 + j * 128 + 128],
                                                 ident_s[:])
                        mm.then_inc(sems["tt"], 1)

            def f_vector(e):
                def c_slice(tb, r):
                    e.wait_ge(sems["ss"], ss_p1o[(tb, r)])
                    prev = prev_slot_user(tb, r)
                    if prev is not None:
                        e.wait_ge(sems["tt"], tt_mm[prev])
                    sl = r % RNG
                    q = r % 2
                    VF.step(e, e.tensor_scalar(oslh[:, sl * 512:(sl + 1) * 512], opin[q][:],
                                               MAGIC, None, AL.subtract, AL.bypass))
                    VF.step(e, e.tensor_scalar(opin[q][:], opin[q][:], MAGIC, None,
                                               AL.subtract, AL.bypass))
                    VF.step(e, e.scalar_tensor_tensor(osll[:, sl * 512:(sl + 1) * 512], opin[q][:],
                                                      1.0, oslh[:, sl * 512:(sl + 1) * 512],
                                                      AL.mult, AL.subtract))
                    e.nop().then_inc(sems["vv"], 1)

                def evacs(tbp):
                    e.wait_ge(sems["tt"], tt_mm[(tbp, IT - 1)])
                    if tbp >= 1:
                        e.wait_ge(sems["tt"], tt_tr[(tbp - 1, 3)])
                    for dt in range(KC):
                        VF.step(e, e.tensor_scalar(ffm[:, dt * 512:(dt + 1) * 512], ps[dt][:],
                                                   bc[:, 8:9], None, AL.mult, AL.bypass))
                        e.nop().then_inc(sems["vv"], 1)

                def ln_j(tbp, j):
                    tj = tbp * 4 + j
                    e.wait_ge(sems["tt"], tt_tr[(tbp, j)])
                    VF.step(e, e.tensor_copy(ffnt[:, 0:512], ps[4][:]))
                    VF.step(e, e.tensor_copy(ffnt[:, 512:1024], ps[5][:]))
                    e.nop().then_inc(sems["vv"], 1)
                    ln_stats_vec(e, ffnt[:], 0, ss_sq[(tbp, j)], ss_sqrt[(tbp, j)])
                    VF.step(e, e.tensor_scalar(ffnt[:], ffnt[:], stats[:, 1:2], stats[:, 7:8],
                                               AL.subtract, AL.mult))
                    e.wait_ge(sems["ldg"], g0 + 16 * (2 + tj + 1))
                    if tj >= 1:
                        e.wait_ge(sems["spg"], sg0 + 16 * tj)
                    VF.step(e, e.tensor_tensor(outt[:], ffnt[:], x1r[j][:], AL.add))
                    e.nop().then_inc(sems["vv"], 1)
                for tb in range(TB + 1):
                    if tb >= 1:
                        evacs(tb - 1)
                    if tb < TB:
                        for r in range(0, 8):
                            c_slice(tb, r)
                    if tb >= 1:
                        for j in range(4):
                            ln_j(tb - 1, j)
                    if tb < TB:
                        for r in range(8, 24):
                            c_slice(tb, r)

            def f_gpsimd(e):
                for tb in range(TB):
                    for j in range(4):
                        e.wait_ge(sems["vv"], vv_out[(tb, j)])
                        r0 = tb * 512 + j * 128
                        e.dma_start(out=out[r0:r0 + 128, :], in_=outt[:]).then_inc(sems["spg"], 16)
                e.wait_ge(sems["spg"], sg0 + 16 * TB * 4)
            return {"sync": f_sync, "tensor": f_tensor, "vector": f_vector,
                    "scalar": f_scalar, "gpsimd": f_gpsimd}

        add(ph_consts())
        add(ph_ln1())
        add(ph_qkv())
        add(ph_wdq())
        add(ph_attn())
        add(ph_wo())

        def chain1(e):
            VF.step(e, e.reduce_max(sc[0:1, 12:13], arr_s[0:1, 0:8], axis=AX))
            VF.step(e, e.tensor_scalar(sc[0:1, 13:14], sc[0:1, 12:13], CQ, QEPS, AL.mult, AL.max))
            VF.step(e, e.reciprocal(sc[0:1, 0:1], sc[0:1, 13:14]))
        add(ph_ar(0, [0], [0], chain1, 3, 1))

        add(ph_x())
        add(ph_upgate())

        def chain2(e):
            VF.step(e, e.reduce_max(sc[0:1, 12:13], arr_s[0:1, 0:8], axis=AX))
            VF.step(e, e.tensor_tensor(sc[0:1, 12:13], sc[0:1, 12:13], sc[0:1, 13:14], AL.mult))
            VF.step(e, e.tensor_scalar(sc[0:1, 2:3], sc[0:1, 12:13], CQ, QEPS, AL.mult, AL.max))
            VF.step(e, e.reciprocal(sc[0:1, 14:15], sc[0:1, 2:3]))
            VF.step(e, e.tensor_tensor(sc[0:1, 1:2], sc[0:1, 14:15], sc[0:1, 13:14], AL.mult))
            VF.step(e, e.reduce_max(sc[0:1, 12:13], arr_s[0:1, 8:16], axis=AX))
            VF.step(e, e.tensor_tensor(sc[0:1, 12:13], sc[0:1, 12:13], sc[0:1, 13:14], AL.mult))
            VF.step(e, e.tensor_scalar(sc[0:1, 4:5], sc[0:1, 12:13], CQ, QEPS, AL.mult, AL.max))
            VF.step(e, e.reciprocal(sc[0:1, 14:15], sc[0:1, 4:5]))
            VF.step(e, e.tensor_tensor(sc[0:1, 3:4], sc[0:1, 14:15], sc[0:1, 13:14], AL.mult))
        add(ph_ar(1, [1, 2], [0, 8], chain2, 10, 5))

        add(ph_ffnA())

        def chain3(e):
            VF.step(e, e.reduce_max(sc[0:1, 12:13], arr_s[0:1, 0:8], axis=AX))
            VF.step(e, e.tensor_scalar(sc[0:1, 12:13], sc[0:1, 12:13], 1.0 / 65536.0, None, AL.mult, AL.bypass))
            VF.step(e, e.tensor_scalar(sc[0:1, 6:7], sc[0:1, 12:13], CQ, QEPS, AL.mult, AL.max))
            VF.step(e, e.reciprocal(sc[0:1, 14:15], sc[0:1, 6:7]))
            VF.step(e, e.tensor_scalar(sc[0:1, 5:6], sc[0:1, 14:15], 1.0 / 65536.0, None, AL.mult, AL.bypass))
        def chain3z(e):
            chain3(e)
            VF.step(e, e.memset(macc[:, 3:4], 0.0))
        add(ph_ar(2, [3], [0], chain3z, 6, 7))

        add(ph_ffnB())

        def chain4(e):
            VF.step(e, e.reduce_max(sc[0:1, 12:13], arr_s[0:1, 0:8], axis=AX))
            VF.step(e, e.tensor_tensor(sc[0:1, 12:13], sc[0:1, 12:13], sc[0:1, 6:7], AL.mult))
            VF.step(e, e.tensor_scalar(sc[0:1, 8:9], sc[0:1, 12:13], CQ, QEPS, AL.mult, AL.max))
            VF.step(e, e.reciprocal(sc[0:1, 14:15], sc[0:1, 8:9]))
            VF.step(e, e.tensor_tensor(sc[0:1, 7:8], sc[0:1, 14:15], sc[0:1, 6:7], AL.mult))
        add(ph_ar(3, [3], [0], chain4, 5, 9))

        add(ph_down())

        for ph_ in phases:
            for en in ENGS:
                fn = ph_.get(en)
                if fn is not None:
                    getattr(block, en)(fn)
    return nc


def kernel(L0_attn_hidden_states, L0_attn_Wq, L0_attn_Wkv, L0_attn_Wo, L0_attn_bo,
           L0_ffn_w_gate, L0_ffn_w_up, L0_ffn_w_down):
    n = 8
    x = np.ascontiguousarray(np.asarray(L0_attn_hidden_states, dtype=np.float32))
    xs = x.reshape(8, TOK, D)
    nc = build_nc(n)
    base = {
        "wq": np.ascontiguousarray(np.asarray(L0_attn_Wq, dtype=np.float32)),
        "wkv": np.ascontiguousarray(np.asarray(L0_attn_Wkv, dtype=np.float32)),
        "wo": np.ascontiguousarray(np.asarray(L0_attn_Wo, dtype=np.float32)),
        "bo": np.ascontiguousarray(np.asarray(L0_attn_bo, dtype=np.float32)).reshape(1, D),
        "wup": np.ascontiguousarray(np.asarray(L0_ffn_w_up, dtype=np.float32)),
        "wgate": np.ascontiguousarray(np.asarray(L0_ffn_w_gate, dtype=np.float32)),
        "wdown": np.ascontiguousarray(np.asarray(L0_ffn_w_down, dtype=np.float32)),
        "identity": np.eye(128, dtype=np.float32),
        "onesrow": np.ones((1, 128), np.float32),
        "negrow": -np.ones((1, 128), np.float32),
    }
    in_maps = []
    for c in range(n):
        m = dict(base)
        m["x0"] = xs[c]
        sm = np.zeros((1, 16), np.float32)
        sm[0, c] = 1.0
        sm[0, 8 + c] = 1.0
        m["slotm"] = sm
        in_maps.append(m)
    res = run_bass_kernel_spmd(nc, in_maps, core_ids=list(range(n)))
    outp = np.concatenate([np.asarray(r["out"])[None] for r in res.results], axis=0)
    return outp.reshape(16, S, D)


# version 14 (final)
# speedup vs baseline: 1.1578x; 1.0268x over previous
import numpy as np
from contextlib import ExitStack
import concourse.bass as bass
import concourse.mybir as mybir
from concourse.bass_utils import run_bass_kernel_spmd

F32 = mybir.dt.float32
AL = mybir.AluOpType
ACT = mybir.ActivationFunctionType
AX = mybir.AxisListType.X

MAGIC = float(3 * 2.0 ** 22)
C7 = 1.0 / 7.0
CQ = float(1.0 / 32767.0)
EPS = 1e-5
QEPS = 1e-12

D = 1024
I3 = 3072
TOK = 4096
S = 2048
NH = 16
DH = 64
NT = TOK // 128
KC = D // 128
IT = I3 // 128
TB = TOK // 512
ENGS = ("sync", "tensor", "vector", "scalar", "gpsimd")


class Fine:

    def __init__(self, sem):
        self.sem = sem
        self.n = 0

    def step(self, eng, instr):
        self.n += 1
        instr.then_inc(self.sem, 1)
        eng.wait_ge(self.sem, self.n)
        return instr


def build_nc(n_cores=8):
    nc = bass.Bass("TRN2", target_bir_lowering=False)
    x0 = nc.dram_tensor("x0", [TOK, D], F32, kind="ExternalInput").ap()
    wq = nc.dram_tensor("wq", [D, D], F32, kind="ExternalInput").ap()
    wkv = nc.dram_tensor("wkv", [D, 2 * D], F32, kind="ExternalInput").ap()
    wo = nc.dram_tensor("wo", [D, D], F32, kind="ExternalInput").ap()
    bo = nc.dram_tensor("bo", [1, D], F32, kind="ExternalInput").ap()
    wup = nc.dram_tensor("wup", [I3, D], F32, kind="ExternalInput").ap()
    wgate = nc.dram_tensor("wgate", [I3, D], F32, kind="ExternalInput").ap()
    wdown = nc.dram_tensor("wdown", [D, I3], F32, kind="ExternalInput").ap()
    identity = nc.dram_tensor("identity", [128, 128], F32, kind="ExternalInput").ap()
    onesrow = nc.dram_tensor("onesrow", [1, 128], F32, kind="ExternalInput").ap()
    negrow = nc.dram_tensor("negrow", [1, 128], F32, kind="ExternalInput").ap()
    slotm = nc.dram_tensor("slotm", [1, 16], F32, kind="ExternalInput").ap()
    out = nc.dram_tensor("out", [TOK, D], F32, kind="ExternalOutput").ap()

    qfm = nc.dram_tensor("qfm", [D, TOK], F32).ap()
    kfm = nc.dram_tensor("kfm", [D, TOK], F32).ap()
    vscr = nc.dram_tensor("vscr", [NT, 128, D], F32).ap()
    attn_fm = nc.dram_tensor("attn_fm", [2, 4, KC, 128, 512], F32).ap()
    x1scr = nc.dram_tensor("x1scr", [TOK, D], F32).ap()
    h2scr = nc.dram_tensor("h2scr", [TOK, D], F32).ap()
    BF16 = mybir.dt.bfloat16
    wupT = [nc.dram_tensor(f"wupT{j}", [IT, KC, 128, 128], BF16).ap() for j in range(2)]
    wgT = [nc.dram_tensor(f"wgT{j}", [IT, KC, 128, 128], BF16).ap() for j in range(2)]
    wdT = [nc.dram_tensor(f"wdT{j}", [KC, IT, 128, 128], BF16).ap() for j in range(2)]
    u0scr = nc.dram_tensor("u0scr", [IT, 128, TOK], F32).ap()
    g0scr = nc.dram_tensor("g0scr", [IT, 128, TOK], F32).ap()
    opscr = nc.dram_tensor("opscr", [IT, 128, TOK], F32).ap()
    ccin = [nc.dram_tensor(f"ccin{i}", [1, 16], F32) for i in range(4)]
    ccout = [nc.dram_tensor(f"ccout{i}", [1, 16], F32) for i in range(4)]

    es = ExitStack()
    with es:
        RBASE = 16512
        es.enter_context(nc.sbuf_tensor("reserve_arena", [128, 196000], mybir.dt.uint8))

        def sb(name, shape, start, dt_=F32):
            per = int(np.prod(shape[1:])) * (2 if dt_ == mybir.dt.bfloat16 else 4)
            start = start + RBASE
            assert start + per <= 212512, (name, start, per)
            return nc.alloc_sbuf_tensor_at(name, shape, dt_, offset=start)
        KB = 1024
        A0, B0, S0 = 0, 128 * KB, 182 * KB
        hfm = sb("hfm", [128, KC * TOK], A0)
        VB = sb("VB", [128, 16 * D], A0)
        ESd = [sb(f"ESd{i}", [128, 8192], A0 + (64 + 32 * i) * KB) for i in range(2)]
        woS = sb("woS", [128, KC * D], A0)
        bobc = sb("bobc", [128, D], A0 + 32 * KB)
        afm = [sb(f"afm{i}", [128, D], A0 + 36 * KB + i * 4 * KB) for i in range(2)]
        BF = mybir.dt.bfloat16
        xfmh = sb("xfmh", [128, KC * TOK], A0, BF)
        xfml = sb("xfml", [128, KC * TOK], A0 + 64 * KB, BF)
        jb = [sb(f"jb{i}", [128, TOK], A0 + i * 16 * KB) for i in range(8)]
        wdSh = sb("wdSh", [128, KC * IT * 128], A0, BF)
        wdSl = sb("wdSl", [128, KC * IT * 128], A0 + 48 * KB, BF)
        ffm = sb("ffm", [128, KC * 512], A0 + 96 * KB)
        ffnt = sb("ffnt", [128, D], A0 + 112 * KB)
        x1r = [sb(f"x1r{i}", [128, D], (A0 + 116 * KB + i * 4 * KB) if i < 2 else
                  (B0 + 24 * KB + (i - 2) * 4 * KB)) for i in range(4)]
        outt = sb("outt", [128, D], A0 + 124 * KB)
        ln_x = [sb(f"ln_x{i}", [128, D], B0 + i * 4 * KB) for i in range(2)]
        ln_h = sb("ln_h", [128, D], B0 + 8 * KB)
        sqscr = sb("sqscr", [128, D], B0 + 12 * KB)
        wst = [sb(f"wst{i}", [128, KC * 128], B0 + 4 * KB + i * 4 * KB) for i in range(2)]
        wmv = [sb(f"wmv{i}", [128, KC * 512], B0 + 12 * KB + i * 16 * KB) for i in range(2)]
        evst = sb("evst", [128, 1024], B0)
        dqin = [sb(f"dqin{i}", [128, I3], B0 + i * 12 * KB) for i in range(2)]
        dqscr = sb("dqscr", [128, I3], B0 + 24 * KB)
        dqh = sb("dqh", [128, I3], B0 + 36 * KB, BF)
        dql = sb("dql", [128, I3], B0 + 42 * KB, BF)
        dqsth = sb("dqsth", [128, I3], B0 + 24 * KB, BF)
        dqstl = sb("dqstl", [128, I3], B0 + 30 * KB, BF)
        KH = [sb(f"KH{i}", [64, S], B0 + i * 8 * KB) for i in range(2)]
        QH = [sb(f"QH{i}", [64, 512], B0 + 16 * KB + i * 2 * KB) for i in range(2)]
        zw = sb("zw", [128, 4096], B0 + 20 * KB)
        ast = [sb(f"ast{i}", [128, 512], B0 + 36 * KB + i * 2 * KB) for i in range(2)]
        rzb = sb("rzb", [64, 512], B0 + 40 * KB)
        mt16 = sb("mt16", [128, 16], B0 + 42 * KB)
        rzrow = sb("rzrow", [1, 512], B0 + 42 * KB + 512)
        zq4 = sb("zq4", [128, 4], B0 + 42 * KB + 2560)
        att_in = [sb(f"att_in{i}", [128, D], B0 + 16 * KB + i * 4 * KB) for i in range(2)]
        x0r = [sb(f"x0r{i}", [128, D], B0 + 24 * KB + i * 4 * KB) for i in range(2)]
        x1t = sb("x1t", [128, D], B0 + 32 * KB)
        h2t = sb("h2t", [128, D], B0 + 36 * KB)
        h2in = [sb(f"h2in{i}", [128, D], B0 + i * 4 * KB) for i in range(2)]
        xh = sb("xh", [128, D], B0 + 8 * KB, BF)
        xl = sb("xl", [128, D], B0 + 10 * KB, BF)
        u0j = [sb(f"u0j{i}", [128, TOK], B0 + i * 16 * KB) for i in range(2)]
        jbx = [sb(f"jbx{i}", [128, TOK], B0 + i * 16 * KB) for i in range(2)]
        wT7h = [sb(f"wT7h{i}", [128, KC * 128], B0 + 32 * KB + i * 2 * KB, BF) for i in range(2)]
        wT7l = [sb(f"wT7l{i}", [128, KC * 128], B0 + 36 * KB + i * 2 * KB, BF) for i in range(2)]
        ev7 = [sb(f"ev7_{i}", [128, 512], B0 + 40 * KB + i * 2 * KB) for i in range(2)]
        oslh = sb("oslh", [128, 8 * 512], B0, BF)
        osll = sb("osll", [128, 8 * 512], B0 + 8 * KB, BF)
        opin = [sb(f"opin{i}", [128, 512], B0 + 16 * KB + i * 2 * KB) for i in range(2)]
        sqd = sb("sqd", [128, D], B0 + 20 * KB)
        ident_s = sb("ident_s", [128, 128], S0)
        ident_b = sb("ident_b", [128, 128], S0 + 8320, BF)
        ones_s = sb("ones_s", [1, 128], S0 + 512)
        negr_s = sb("negr_s", [1, 128], S0 + 1024)
        slot_s = sb("slot_s", [1, 16], S0 + 1536)
        arv_s = sb("arv_s", [1, 16], S0 + 1600)
        arr_s = sb("arr_s", [1, 16], S0 + 1664)
        epsv = sb("epsv", [128, 1], S0 + 1728)
        stats = sb("stats", [128, 16], S0 + 1792)
        bc = sb("bc", [128, 16], S0 + 1856)
        sc = sb("sc", [1, 16], S0 + 1920)
        macc = sb("macc", [128, 4], S0 + 1984)
        mtmp = sb("mtmp", [128, 2], S0 + 2048)
        mrow = sb("mrow", [1, 128], S0 + 2112)
        negm = sb("negm", [128, 1], S0 + 2624)
        zred = sb("zred", [128, 8], S0 + 2688)
        bo_s = sb("bo_s", [1, D], S0 + 2752)
        ones_c = sb("ones_c", [128, 1], S0 + 6912)
        dqs = sb("dqs", [128, 96], S0 + 7 * KB)
        dqr = sb("dqr", [128, 96], S0 + 7 * KB + 512)

        ps = [es.enter_context(nc.psum_tensor(f"ps{i}", [128, 512], F32)) for i in range(8)]

        sems = {}
        for s_ in ("lda", "ldb", "ldg", "spa", "spb", "spg", "sha", "shb", "vv", "ss", "tt", "gg", "bar", "cc", "vf", "sf", "gf"):
            sems[s_] = es.enter_context(nc.semaphore(s_))
        block = es.enter_context(nc.Block())

        C = {k: 0 for k in sems}
        VF = Fine(sems["vf"])
        SF = Fine(sems["sf"])
        GFN = [0]
        DCTX = {}
        phases = []

        def barrier():
            C["bar"] += 5
            tgt = C["bar"]

            def mk():
                def f(e):
                    e.nop().then_inc(sems["bar"], 1)
                    e.wait_ge(sems["bar"], tgt)
                return f
            phases.append({n_: mk() for n_ in ENGS})

        def add(ph):
            phases.append(ph)
            barrier()

        def ln_stats_vec(e, xt, vv_var_tgt, ss_sq_tgt, ss_sqrt_tgt, mcol=1, sumcol=0):
            VF.step(e, e.reduce_sum(stats[:, sumcol:sumcol + 1], xt, axis=AX))
            VF.step(e, e.tensor_scalar(stats[:, mcol:mcol + 1], stats[:, sumcol:sumcol + 1],
                                       1.0 / D, None, AL.mult, AL.bypass))
            e.wait_ge(sems["ss"], ss_sq_tgt)
            VF.step(e, e.tensor_scalar(stats[:, 3:4], stats[:, 2:3], 1.0 / D, None, AL.mult, AL.bypass))
            VF.step(e, e.tensor_tensor(stats[:, 4:5], stats[:, mcol:mcol + 1], stats[:, mcol:mcol + 1], AL.mult))
            VF.step(e, e.tensor_tensor(stats[:, 5:6], stats[:, 3:4], stats[:, 4:5], AL.subtract))
            e.nop().then_inc(sems["vv"], 1)
            e.wait_ge(sems["ss"], ss_sqrt_tgt)
            VF.step(e, e.reciprocal(stats[:, 7:8], stats[:, 6:7]))
            VF.step(e, e.tensor_scalar(stats[:, 8:9], stats[:, 5:6], EPS, None, AL.add, AL.bypass))
            for _ in range(2):
                VF.step(e, e.tensor_tensor(stats[:, 9:10], stats[:, 7:8], stats[:, 7:8], AL.mult))
                VF.step(e, e.tensor_tensor(stats[:, 9:10], stats[:, 9:10], stats[:, 8:9], AL.mult))
                VF.step(e, e.tensor_scalar(stats[:, 9:10], stats[:, 9:10], -0.5, 1.5, AL.mult, AL.add))
                VF.step(e, e.tensor_tensor(stats[:, 7:8], stats[:, 7:8], stats[:, 9:10], AL.mult))

        def ln_stats_scal(e, xt, vv_xt_tgt, vv_var_tgt):
            e.wait_ge(sems["vv"], vv_xt_tgt)
            SF.step(e, e.activation(sqscr[:], xt, ACT.Square, accum_out=stats[:, 2:3]))
            e.nop().then_inc(sems["ss"], 1)
            e.wait_ge(sems["vv"], vv_var_tgt)
            SF.step(e, e.activation(stats[:, 6:7], stats[:, 5:6], ACT.Sqrt, bias=epsv[:]))
            e.nop().then_inc(sems["ss"], 1)

        def ph_consts():
            g0 = C["ldg"]
            C["ldg"] += 16 * 5
            C["vv"] += 1

            def f_sync(e):
                e.dma_start(out=ident_s[:], in_=identity).then_inc(sems["ldg"], 16)
                e.dma_start(out=ones_s[:], in_=onesrow).then_inc(sems["ldg"], 16)
                e.dma_start(out=negr_s[:], in_=negrow).then_inc(sems["ldg"], 16)
                e.dma_start(out=slot_s[:], in_=slotm).then_inc(sems["ldg"], 16)
                e.dma_start(out=bo_s[:], in_=bo).then_inc(sems["ldg"], 16)

            def f_vector(e):
                VF.step(e, e.memset(epsv[:], EPS))
                VF.step(e, e.memset(macc[:], 0.0))
                VF.step(e, e.memset(mtmp[:], 0.0))
                VF.step(e, e.memset(ones_c[:], 1.0))
                e.wait_ge(sems["ldg"], g0 + 80)
                VF.step(e, e.tensor_copy(ident_b[:], ident_s[:]))
                e.nop().then_inc(sems["vv"], 1)
            return {"sync": f_sync, "vector": f_vector}

        def ph_ln1():
            vv0, ss0, tt0 = C["vv"], C["ss"], C["tt"]
            la0, lb0 = C["lda"], C["ldb"]
            C["lda"] += 16 * (NT // 2)
            C["ldb"] += 16 * (NT // 2)
            C["vv"] += 3 * NT
            C["ss"] += 2 * NT
            C["tt"] += 2 * NT

            def lsem(t):
                return (sems["lda"], la0 + 16 * (t // 2 + 1)) if t % 2 == 0 else (sems["ldb"], lb0 + 16 * (t // 2 + 1))

            def f_sync(e):
                for t in range(NT):
                    if t >= 2:
                        e.wait_ge(sems["vv"], vv0 + 3 * (t - 2) + 2)
                        e.wait_ge(sems["ss"], ss0 + 2 * (t - 2) + 1)
                    sm_, _ = lsem(t)
                    e.dma_start(out=ln_x[t % 2][:], in_=x0[t * 128:(t + 1) * 128, :]).then_inc(sm_, 16)

            def f_vector(e):
                for t in range(NT):
                    xt = ln_x[t % 2]
                    sm_, tv_ = lsem(t)
                    e.wait_ge(sm_, tv_)
                    ln_stats_vec(e, xt[:], vv0 + 3 * t + 1, ss0 + 2 * t + 1, ss0 + 2 * t + 2)
                    VF.step(e, e.tensor_scalar(ln_h[:], xt[:], stats[:, 1:2], stats[:, 7:8],
                                               AL.subtract, AL.mult))
                    e.nop().then_inc(sems["vv"], 1)
                    for half in range(2):
                        e.wait_ge(sems["tt"], tt0 + 2 * t + half + 1)
                        dst = bass.AP(hfm, (4 * half) * TOK + t * 128, [[KC * TOK, 128], [TOK, 4], [1, 128]])
                        VF.step(e, e.tensor_copy(dst, ps[half][:]))
                    e.nop().then_inc(sems["vv"], 1)

            def f_scalar(e):
                for t in range(NT):
                    sm_, tv_ = lsem(t)
                    e.wait_ge(sm_, tv_)
                    ln_stats_scal(e, ln_x[t % 2][:], 0, vv0 + 3 * t + 1)

            def f_tensor(e):
                for t in range(NT):
                    e.wait_ge(sems["vv"], vv0 + 3 * t + 2)
                    if t >= 1:
                        e.wait_ge(sems["vv"], vv0 + 3 * t)
                    for half in range(2):
                        for j in range(4):
                            c = half * 4 + j
                            mm = e.transpose(ps[half][:, j * 128:(j + 1) * 128],
                                             ln_h[:, c * 128:(c + 1) * 128], ident_s[:])
                        mm.then_inc(sems["tt"], 1)
            return {"sync": f_sync, "vector": f_vector, "scalar": f_scalar, "tensor": f_tensor}

        def ph_qkv():
            vv0, tt0 = C["vv"], C["tt"]
            la0, lb0, g0 = C["lda"], C["ldb"], C["ldg"]
            sa0, sb0 = C["spa"], C["spb"]
            NF = 16
            NIT = NF * TB + NT * 2
            C["lda"] += 16 * (NF // 2)
            C["ldb"] += 16 * (NF // 2)
            C["ldg"] += 16 * 2
            C["tt"] += NIT
            C["vv"] += NIT
            C["spa"] += 16 * ((NIT + 1) // 2)
            C["spb"] += 16 * (NIT // 2)

            def f_sync(e):
                for ft in range(NF):
                    if ft >= 2:
                        e.wait_ge(sems["tt"], tt0 + TB * (ft - 1))
                    w_, ldw = (wq, D) if ft < 8 else (wkv, 2 * D)
                    coloff = (ft % 8) * 128
                    dst = bass.AP(wst[ft % 2], 0, [[KC * 128, 128], [128, KC], [1, 128]])
                    srcv = bass.AP(w_.tensor, w_.offset + coloff, [[ldw, 128], [128 * ldw, KC], [1, 128]])
                    sm_ = sems["lda"] if ft % 2 == 0 else sems["ldb"]
                    e.dma_start(out=dst, in_=srcv).then_inc(sm_, 16)
                for vb in range(2):
                    dst = bass.AP(wmv[vb], 0, [[KC * 512, 128], [512, KC], [1, 512]])
                    srcv = bass.AP(wkv.tensor, wkv.offset + D + vb * 512, [[2 * D, 128], [128 * 2 * D, KC], [1, 512]])
                    e.dma_start(out=dst, in_=srcv).then_inc(sems["ldg"], 16)

            def f_tensor(e):
                for ft in range(NF):
                    sm_ = sems["lda"] if ft % 2 == 0 else sems["ldb"]
                    e.wait_ge(sm_, (la0 if ft % 2 == 0 else lb0) + 16 * (ft // 2 + 1))
                    for tb in range(TB):
                        it = ft * TB + tb
                        if it >= 2:
                            e.wait_ge(sems["vv"], vv0 + it - 1)
                        p = ps[it % 2]
                        for c in range(KC):
                            mm = e.matmul(p[:], wst[ft % 2][:, c * 128:(c + 1) * 128],
                                          hfm[:, c * TOK + tb * 512: c * TOK + (tb + 1) * 512],
                                          start=(c == 0), stop=(c == KC - 1))
                        mm.then_inc(sems["tt"], 1)
                e.wait_ge(sems["ldg"], g0 + 32)
                for t in range(NT):
                    for vb in range(2):
                        it = NF * TB + t * 2 + vb
                        if it >= 2:
                            e.wait_ge(sems["vv"], vv0 + it - 1)
                        p = ps[it % 2]
                        for c in range(KC):
                            mm = e.matmul(p[:], hfm[:, c * TOK + t * 128: c * TOK + t * 128 + 128],
                                          wmv[vb][:, c * 512:(c + 1) * 512],
                                          start=(c == 0), stop=(c == KC - 1))
                        mm.then_inc(sems["tt"], 1)

            def f_vector(e):
                NFTB = NF * TB
                for it in range(NF * TB + NT * 2):
                    e.wait_ge(sems["tt"], tt0 + it + 1)
                    if it >= 2:
                        pit = it - 2
                        sm_ = sems["spa"] if pit % 2 == 0 else sems["spb"]
                        e.wait_ge(sm_, (sa0 if pit % 2 == 0 else sb0) + 16 * (pit // 2 + 1))
                    p = ps[it % 2]
                    stg = evst[:, (it % 2) * 512:(it % 2) * 512 + 512]
                    if it < NFTB and (it // TB) < 8:
                        VF.step(e, e.tensor_scalar(stg, p[:], 0.125, None, AL.mult, AL.bypass))
                    else:
                        VF.step(e, e.tensor_copy(stg, p[:]))
                    e.nop().then_inc(sems["vv"], 1)

            def f_gpsimd(e):
                NFTB = NF * TB
                for it in range(NFTB + NT * 2):
                    e.wait_ge(sems["vv"], vv0 + it + 1)
                    stg = evst[:, (it % 2) * 512:(it % 2) * 512 + 512]
                    if it < NFTB:
                        ft, tb = it // TB, it % TB
                        if ft < 8:
                            dstv = qfm[ft * 128:(ft + 1) * 128, tb * 512:(tb + 1) * 512]
                        else:
                            dstv = kfm[(ft - 8) * 128:(ft - 7) * 128, tb * 512:(tb + 1) * 512]
                    else:
                        j = it - NFTB
                        t, vb = j // 2, j % 2
                        dstv = vscr[t, :, vb * 512:(vb + 1) * 512]
                    sm_ = sems["spa"] if it % 2 == 0 else sems["spb"]
                    e.dma_start(out=dstv, in_=stg).then_inc(sm_, 16)
                e.wait_ge(sems["spa"], sa0 + 16 * ((NIT + 1) // 2))
                e.wait_ge(sems["spb"], sb0 + 16 * (NIT // 2))
            return {"sync": f_sync, "tensor": f_tensor, "vector": f_vector, "gpsimd": f_gpsimd}

        def ph_wdq():
            vv0, tt0 = C["vv"], C["tt"]
            la0, lb0, sg0 = C["lda"], C["ldb"], C["spg"]
            tiles = ([("up", r) for r in range(IT)] + [("g", r) for r in range(IT)]
                     + [("d", r) for r in range(KC)])
            ntr = {"up": 2, "g": 2, "d": 6}
            tt_before = []
            vv_before = []
            acc = tt0
            vacc = vv0
            for k, _ in tiles:
                tt_before.append(acc)
                vv_before.append(vacc)
                acc += ntr[k]
                vacc += 1 + ntr[k]
            C["lda"] += 16 * ((len(tiles) + 1) // 2)
            C["ldb"] += 16 * (len(tiles) // 2)
            C["tt"] = acc
            C["vv"] = vacc
            C["spg"] += 32 * len(tiles)

            def tsrc(kind, r):
                return {"up": wup, "g": wgate, "d": wdown}[kind][r * 128:(r + 1) * 128, :]

            def f_sync(e):
                for i, (kind, r) in enumerate(tiles):
                    if i >= 2:
                        e.wait_ge(sems["vv"], vv_before[i - 2] + 1)
                    nb = 1024 if kind != "d" else I3
                    sm_ = sems["lda"] if i % 2 == 0 else sems["ldb"]
                    e.dma_start(out=dqin[i % 2][:, 0:nb], in_=tsrc(kind, r)).then_inc(sm_, 16)

            def f_vector(e):
                for i, (kind, r) in enumerate(tiles):
                    nb = 1024 if kind != "d" else I3
                    nblk = nb // 32
                    wsb = dqin[i % 2][:, 0:nb]
                    sm_ = sems["lda"] if i % 2 == 0 else sems["ldb"]
                    e.wait_ge(sm_, (la0 if i % 2 == 0 else lb0) + 16 * (i // 2 + 1))
                    if i >= 1:
                        e.wait_ge(sems["spg"], sg0 + 32 * i)
                    VF.step(e, e.tensor_scalar(dqscr[:, 0:nb], wsb, -1.0, None, AL.mult, AL.bypass))
                    VF.step(e, e.tensor_tensor(dqscr[:, 0:nb], dqscr[:, 0:nb], wsb, AL.max))
                    width = 16
                    while width >= 1:
                        a = bass.AP(dqscr, 0, [[I3, 128], [32, nblk], [1, width]])
                        b2 = bass.AP(dqscr, width, [[I3, 128], [32, nblk], [1, width]])
                        VF.step(e, e.tensor_tensor(a, a, b2, AL.max))
                        width //= 2
                    gat = bass.AP(dqscr, 0, [[I3, 128], [32, nblk]])
                    VF.step(e, e.tensor_copy(dqs[:, 0:nblk], gat))
                    VF.step(e, e.tensor_scalar(dqs[:, 0:nblk], dqs[:, 0:nblk], C7, QEPS, AL.mult, AL.max))
                    VF.step(e, e.reciprocal(dqr[:, 0:nblk], dqs[:, 0:nblk]))
                    rrep = bass.AP(dqr, 0, [[96, 128], [1, nblk], [0, 32]])
                    srep = bass.AP(dqs, 0, [[96, 128], [1, nblk], [0, 32]])
                    VF.step(e, e.tensor_tensor(dqscr[:, 0:nb], wsb, rrep, AL.mult))
                    VF.step(e, e.tensor_scalar(dqscr[:, 0:nb], dqscr[:, 0:nb], MAGIC, MAGIC, AL.add, AL.subtract))
                    VF.step(e, e.tensor_tensor(dqscr[:, 0:nb], dqscr[:, 0:nb], srep, AL.mult))
                    VF.step(e, e.tensor_copy(dqh[:, 0:nb], dqscr[:, 0:nb]))
                    VF.step(e, e.scalar_tensor_tensor(dql[:, 0:nb], dqscr[:, 0:nb], 1.0, dqh[:, 0:nb],
                                                      AL.mult, AL.subtract))
                    e.nop().then_inc(sems["vv"], 1)
                    for g in range(ntr[kind]):
                        e.wait_ge(sems["tt"], tt_before[i] + g + 1)
                        half = kind != "d" and g == 1 or kind == "d" and g >= 3
                        if kind != "d":
                            dst = (dqsth if g == 0 else dqstl)[:, 0:1024]
                        else:
                            dst = (dqsth if g < 3 else dqstl)[:, (g % 3) * 1024:(g % 3 + 1) * 1024]
                        VF.step(e, e.tensor_copy(dst, ps[g % 2][:].bitcast(mybir.dt.bfloat16)))
                        e.nop().then_inc(sems["vv"], 1)

            def f_tensor(e):
                for i, (kind, r) in enumerate(tiles):
                    nb = 1024 if kind != "d" else I3
                    nch = nb // 128
                    e.wait_ge(sems["vv"], vv_before[i] + 1)
                    for g in range(ntr[kind]):
                        if g >= 2:
                            e.wait_ge(sems["vv"], vv_before[i] + 1 + (g - 1))
                        elif i >= 1:
                            e.wait_ge(sems["vv"], vv_before[i - 1] + 1 + ntr[tiles[i - 1][0]] - (1 - g))
                        if kind != "d":
                            src = dqh if g == 0 else dql
                            c0 = 0
                        else:
                            src = dqh if g < 3 else dql
                            c0 = (g % 3) * 8
                        pb = ps[g % 2][:].bitcast(mybir.dt.bfloat16)
                        mm = None
                        for j in range(8):
                            c = c0 + j
                            mm = e.transpose(pb[:, j * 128:(j + 1) * 128],
                                             src[:, c * 128:(c + 1) * 128], ident_b[:])
                        mm.then_inc(sems["tt"], 1)

            def f_gpsimd(e):
                for i, (kind, r) in enumerate(tiles):
                    e.wait_ge(sems["vv"], vv_before[i] + 1 + ntr[kind])
                    nb = 1024 if kind != "d" else I3
                    nch = nb // 128
                    if kind == "up":
                        basep, off = wupT, r * KC * 128 * 128
                    elif kind == "g":
                        basep, off = wgT, r * KC * 128 * 128
                    else:
                        basep, off = wdT, r * IT * 128 * 128
                    for j, stg in ((0, dqsth), (1, dqstl)):
                        dst = bass.AP(basep[j].tensor, basep[j].offset + off,
                                      [[128, 128], [128 * 128, nch], [1, 128]])
                        e.dma_start(out=dst, in_=stg[:, 0:nb]).then_inc(sems["spg"], 16)
                e.wait_ge(sems["spg"], sg0 + 32 * len(tiles))
            return {"sync": f_sync, "vector": f_vector, "tensor": f_tensor, "gpsimd": f_gpsimd}

        def ph_attn():
            vv0, ss0, tt0, gg0 = C["vv"], C["ss"], C["tt"], C["gg"]
            la0, lb0, g0, sg0 = C["lda"], C["ldb"], C["ldg"], C["spg"]
            NIT = 2 * 4 * NH

            tt_scores = {}; tt_Mtp = {}; tt_negbc = {}; tt_av = {}
            tt_zmm = {}; tt_rzbc = {}
            ttc = tt0
            ttc += 1; tt_Mtp[0] = ttc
            ttc += 1; tt_negbc[0] = ttc
            for i in range(NIT + 2):
                if i < NIT:
                    for kc in range(4):
                        ttc += 1; tt_scores[(i, kc)] = ttc
                    if i >= 1:
                        ttc += 1; tt_negbc[i] = ttc
                    for kc in range(4, 16):
                        ttc += 1; tt_scores[(i, kc)] = ttc
                if 2 <= i <= NIT + 1:
                    ttc += 1; tt_zmm[i - 2] = ttc
                if 1 <= i <= NIT:
                    ttc += 1; tt_av[i - 1] = ttc
                if 2 <= i <= NIT + 1:
                    ttc += 1; tt_rzbc[i - 2] = ttc
                if i + 1 <= NIT - 1:
                    ttc += 1; tt_Mtp[i + 1] = ttc
            C["tt"] = ttc
            vv_evac = {}; vv_maxred = {}; vv_m1 = {}; vv_negm = {}; vv_zw = {}
            vv_rzrow = {}; vv_rzb = {}; vv_em = {}
            vvc = vv0
            for i in range(NIT + 3):
                if i < NIT:
                    vvc += 1; vv_m1[i] = vvc
                    for kc in range(4):
                        vvc += 1; vv_evac[i * 16 + kc] = vvc
                    vvc += 1; vv_negm[i] = vvc
                if 3 <= i <= NIT + 2:
                    p3 = i - 3
                    vvc += 1; vv_rzb[p3] = vvc
                    vvc += 1; vv_em[p3] = vvc
                if 2 <= i <= NIT + 1:
                    vvc += 1; vv_zw[i - 2] = vvc
                if i < NIT:
                    for kc in range(4, 16):
                        vvc += 1; vv_evac[i * 16 + kc] = vvc
                    vvc += 1; vv_maxred[i] = vvc
                if 2 <= i <= NIT + 1:
                    vvc += 1; vv_rzrow[i - 2] = vvc
            C["vv"] = vvc
            C["ss"] += 4 * NIT
            C["gg"] += NIT
            C["ldg"] += 16 * (2 * 16)
            C["lda"] += 32 * (NIT // 2)
            C["ldb"] += 32 * (NIT // 2)
            C["spg"] += 16 * (NIT // 2)

            def hdr(i):
                return i // 64, (i // 16) % 4, i % 16

            def f_sync(e):
                for v in range(16):
                    e.dma_start(out=VB[:, v * D:(v + 1) * D], in_=vscr[v]).then_inc(sems["ldg"], 16)
                for i in range(NIT):
                    b, qb, h = hdr(i)
                    if i >= 2:
                        e.wait_ge(sems["tt"], tt_scores[(i - 2, 15)])
                    sm_ = sems["lda"] if i % 2 == 0 else sems["ldb"]
                    e.dma_start(out=KH[i % 2][:],
                                in_=kfm[h * DH:(h + 1) * DH, b * S:(b + 1) * S]).then_inc(sm_, 16)
                    e.dma_start(out=QH[i % 2][:],
                                in_=qfm[h * DH:(h + 1) * DH, b * S + qb * 512: b * S + (qb + 1) * 512]).then_inc(sm_, 16)
                    if i == 65:
                        e.wait_ge(sems["tt"], tt_av[63])
                        for v in range(16):
                            e.dma_start(out=VB[:, v * D:(v + 1) * D], in_=vscr[16 + v]).then_inc(sems["ldg"], 16)

            def f_tensor(e):
                e.transpose(ps[7][0:1, 0:128], mtmp[:, 1:2], ident_s[:]).then_inc(sems["tt"], 1)
                e.wait_ge(sems["vv"], vv_m1[0])
                e.matmul(ps[7][:, 256:257], negr_s[:, 0:128], sc[0:1, 0:1],
                         start=True, stop=True).then_inc(sems["tt"], 1)
                for i in range(NIT + 2):
                    if i < NIT:
                        sm_ = sems["lda"] if i % 2 == 0 else sems["ldb"]
                        e.wait_ge(sm_, (la0 if i % 2 == 0 else lb0) + 32 * (i // 2 + 1))
                        for kc in range(16):
                            if kc % 4 == 0 and i * 16 + kc >= 4:
                                e.wait_ge(sems["vv"], vv_evac[i * 16 + kc - 1])
                            e.matmul(ps[kc % 4][:], KH[i % 2][:, kc * 128:(kc + 1) * 128],
                                     QH[i % 2][:], start=True, stop=True).then_inc(sems["tt"], 1)
                            if kc == 3 and i >= 1:
                                e.wait_ge(sems["vv"], vv_m1[i])
                                e.wait_ge(sems["vv"], vv_negm[i - 1])
                                e.matmul(ps[7][:, 256:257], negr_s[:, 0:128], sc[0:1, 0:1],
                                         start=True, stop=True).then_inc(sems["tt"], 1)
                    if 2 <= i <= NIT + 1:
                        p2 = i - 2
                        e.wait_ge(sems["vv"], vv_zw[p2])
                        if p2 >= 1:
                            e.wait_ge(sems["vv"], vv_rzb[p2 - 1])
                        e.matmul(ps[6][0:1, 0:512], ones_c[:], zw[:, 0:512],
                                 start=True, stop=True).then_inc(sems["tt"], 1)
                    if 1 <= i <= NIT:
                        p = i - 1
                        b, qb, h = hdr(p)
                        e.wait_ge(sems["ldg"], g0 + 16 * 16 * (b + 1))
                        if p >= 2:
                            e.wait_ge(sems["vv"], vv_em[p - 2])
                        for kc in range(16):
                            if kc % 4 == 0:
                                e.wait_ge(sems["ss"], ss0 + 4 * p + kc // 4 + 1)
                            mm = e.matmul(ps[4 + p % 2][0:64, :], VB[:, kc * D + h * DH: kc * D + h * DH + DH],
                                          ESd[p % 2][:, kc * 512:(kc + 1) * 512],
                                          start=(kc == 0), stop=(kc == 15))
                        mm.then_inc(sems["tt"], 1)
                    if 2 <= i <= NIT + 1:
                        p2 = i - 2
                        e.wait_ge(sems["vv"], vv_rzrow[p2])
                        e.matmul(ps[6][0:64, :], ones_s[:, 0:64], rzrow[:],
                                 start=True, stop=True).then_inc(sems["tt"], 1)
                    if i + 1 <= NIT - 1:
                        e.wait_ge(sems["vv"], vv_maxred[i])
                        e.wait_ge(sems["vv"], vv_m1[i])
                        e.transpose(ps[7][0:1, 0:128], mtmp[:, i % 2:i % 2 + 1], ident_s[:]).then_inc(sems["tt"], 1)

            def f_vector(e):
                for i in range(NIT + 3):
                    if i < NIT:
                        b, qb, h = hdr(i)
                        e.wait_ge(sems["tt"], tt_Mtp[i])
                        VF.step(e, e.tensor_copy(mrow[:], ps[7][0:1, 0:128]))
                        if i >= 1:
                            e.wait_ge(sems["tt"], tt_negbc[i - 1])
                        VF.step(e, e.reduce_max(sc[0:1, 0:1], mrow[:], axis=AX))
                        e.nop().then_inc(sems["vv"], 1)
                        if i >= 2:
                            e.wait_ge(sems["tt"], tt_av[i - 2])
                            e.wait_ge(sems["gg"], gg0 + i - 1)
                        for kc in range(4):
                            e.wait_ge(sems["tt"], tt_scores[(i, kc)])
                            VF.step(e, e.tensor_scalar(ESd[i % 2][:, kc * 512:(kc + 1) * 512], ps[kc % 4][:],
                                                       1.0, -3e38, AL.mult, AL.max,
                                                       accum_out=mt16[:, kc:kc + 1]))
                            e.nop().then_inc(sems["vv"], 1)
                        e.wait_ge(sems["tt"], tt_negbc[i])
                        e.wait_ge(sems["ss"], ss0 + 4 * i)
                        VF.step(e, e.tensor_copy(negm[:], ps[7][:, 256:257]))
                        e.nop().then_inc(sems["vv"], 1)
                    if 3 <= i <= NIT + 2:
                        p3 = i - 3
                        b3, qb3, h3 = hdr(p3)
                        e.wait_ge(sems["tt"], tt_rzbc[p3])
                        VF.step(e, e.tensor_copy(rzb[:], ps[6][0:64, :]))
                        e.nop().then_inc(sems["vv"], 1)
                        e.wait_ge(sems["tt"], tt_av[p3])
                        c = h3 // 2
                        gc = (b3 * 4 + qb3) * 8 + c
                        if gc >= 2:
                            e.wait_ge(sems["spg"], sg0 + 16 * (gc - 1))
                        VF.step(e, e.tensor_tensor(ast[c % 2][(h3 % 2) * 64:(h3 % 2) * 64 + 64, :],
                                                   ps[4 + p3 % 2][0:64, :], rzb[:], AL.mult))
                        e.nop().then_inc(sems["vv"], 1)
                    if 2 <= i <= NIT + 1:
                        p = i - 2
                        e.wait_ge(sems["gg"], gg0 + p + 1)
                        for lv in range(4):
                            VF.step(e, e.tensor_tensor(zw[:, lv * 512:(lv + 1) * 512],
                                                       zw[:, (2 * lv) * 512:(2 * lv + 1) * 512],
                                                       zw[:, (2 * lv + 1) * 512:(2 * lv + 2) * 512], AL.add))
                        for lv in range(2):
                            VF.step(e, e.tensor_tensor(zw[:, lv * 512:(lv + 1) * 512],
                                                       zw[:, (2 * lv) * 512:(2 * lv + 1) * 512],
                                                       zw[:, (2 * lv + 1) * 512:(2 * lv + 2) * 512], AL.add))
                        VF.step(e, e.tensor_tensor(zw[:, 0:512], zw[:, 0:512], zw[:, 512:1024], AL.add))
                        e.nop().then_inc(sems["vv"], 1)
                    if i < NIT:
                        for kc in range(4, 16):
                            e.wait_ge(sems["tt"], tt_scores[(i, kc)])
                            VF.step(e, e.tensor_scalar(ESd[i % 2][:, kc * 512:(kc + 1) * 512], ps[kc % 4][:],
                                                       1.0, -3e38, AL.mult, AL.max,
                                                       accum_out=mt16[:, kc:kc + 1]))
                            e.nop().then_inc(sems["vv"], 1)
                        VF.step(e, e.reduce_max(mtmp[:, i % 2:i % 2 + 1], mt16[:], axis=AX))
                        e.nop().then_inc(sems["vv"], 1)
                    if 2 <= i <= NIT + 1:
                        p2 = i - 2
                        e.wait_ge(sems["tt"], tt_zmm[p2])
                        VF.step(e, e.reciprocal(rzrow[:], ps[6][0:1, 0:512]))
                        e.nop().then_inc(sems["vv"], 1)

            def f_scalar(e):
                for p in range(NIT):
                    e.wait_ge(sems["vv"], vv_negm[p])
                    for q4 in range(4):
                        e.wait_ge(sems["vv"], vv_evac[p * 16 + q4 * 4 + 3])
                        SF.step(e, e.activation(ESd[p % 2][:, q4 * 2048:(q4 + 1) * 2048],
                                                ESd[p % 2][:, q4 * 2048:(q4 + 1) * 2048],
                                                ACT.Exp, bias=negm[:], scale=1.0))
                        e.nop().then_inc(sems["ss"], 1)

            def f_gpsimd(e):
                gfn = GFN

                def spill(sp):
                    b, qb, h = hdr(sp)
                    e.wait_ge(sems["vv"], vv_em[sp])
                    gc = (b * 4 + qb) * 8 + h // 2
                    if gc >= 1:
                        e.wait_ge(sems["spg"], sg0 + 16 * gc)
                    c = h // 2
                    e.dma_start(out=attn_fm[b, qb, c], in_=ast[c % 2][:]).then_inc(sems["spg"], 16)
                for p in range(NIT):
                    b, qb, h = hdr(p)
                    if p >= 1:
                        e.wait_ge(sems["tt"], tt_zmm[p - 1])
                        e.wait_ge(sems["vv"], vv_zw[p - 1])
                    for lv in range(8):
                        if lv % 2 == 0:
                            e.wait_ge(sems["ss"], ss0 + 4 * p + lv // 2 + 1)
                        gfn[0] += 1
                        e.tensor_tensor(zw[:, lv * 512:(lv + 1) * 512],
                                        ESd[p % 2][:, (2 * lv) * 512:(2 * lv + 1) * 512],
                                        ESd[p % 2][:, (2 * lv + 1) * 512:(2 * lv + 2) * 512],
                                        AL.add).then_inc(sems["gf"], 1)
                        e.wait_ge(sems["gf"], gfn[0])
                    e.nop().then_inc(sems["gg"], 1)
                    if p >= 2 and hdr(p - 2)[2] % 2 == 1:
                        spill(p - 2)
                for sp in (NIT - 2, NIT - 1):
                    if hdr(sp)[2] % 2 == 1:
                        spill(sp)
                e.wait_ge(sems["spg"], sg0 + 16 * (NIT // 2))
            return {"sync": f_sync, "tensor": f_tensor, "scalar": f_scalar,
                    "vector": f_vector, "gpsimd": f_gpsimd}

        def ph_wo():
            vv0, ss0, tt0 = C["vv"], C["ss"], C["tt"]
            la0, lb0, g0, sa0, sb0 = C["lda"], C["ldb"], C["ldg"], C["spa"], C["spb"]
            C["ldg"] += 16
            C["lda"] += 32 * (NT // 2)
            C["ldb"] += 32 * (NT // 2)
            C["tt"] += 1 + 2 * NT
            C["vv"] += 1 + 5 * NT
            C["ss"] += 4 * NT
            C["spa"] += 16 * NT
            C["spb"] += 16 * NT

            def f_sync(e):
                dst = bass.AP(woS, 0, [[KC * D, 128], [D, KC], [1, D]])
                srcv = bass.AP(wo.tensor, wo.offset, [[D, 128], [128 * D, KC], [1, D]])
                e.dma_start(out=dst, in_=srcv).then_inc(sems["ldg"], 16)
                for t in range(NT):
                    if t >= 2:
                        e.wait_ge(sems["tt"], tt0 + 1 + 2 * (t - 2) + 2)
                        e.wait_ge(sems["vv"], vv0 + 1 + 5 * (t - 2) + 3)
                    sm_ = sems["lda"] if t % 2 == 0 else sems["ldb"]
                    tb_, qb_, j_ = t // 16, (t % 16) // 4, t % 4
                    afsrc = bass.AP(attn_fm.tensor,
                                    attn_fm.offset + ((tb_ * 4 + qb_) * KC) * 128 * 512 + j_ * 128,
                                    [[512, 128], [128 * 512, KC], [1, 128]])
                    e.dma_start(out=afm[t % 2][:], in_=afsrc).then_inc(sm_, 16)
                    e.dma_start(out=x0r[t % 2][:], in_=x0[t * 128:(t + 1) * 128, :]).then_inc(sm_, 16)

            def f_tensor(e):
                e.wait_ge(sems["ldg"], g0 + 16)
                mm = e.matmul(ps[2][:, 0:512], ones_s[:, 0:128], bo_s[:, 0:512], start=True, stop=True)
                mm = e.matmul(ps[3][:, 0:512], ones_s[:, 0:128], bo_s[:, 512:1024], start=True, stop=True)
                mm.then_inc(sems["tt"], 1)
                for t in range(NT):
                    sm_ = sems["lda"] if t % 2 == 0 else sems["ldb"]
                    e.wait_ge(sm_, (la0 if t % 2 == 0 else lb0) + 32 * (t // 2 + 1))
                    if t >= 1:
                        e.wait_ge(sems["vv"], vv0 + 1 + 5 * (t - 1) + 1)
                    for db in range(2):
                        p = ps[4 + db]
                        for c in range(KC):
                            mm = e.matmul(p[:], afm[t % 2][:, c * 128:(c + 1) * 128],
                                          woS[:, c * D + db * 512: c * D + (db + 1) * 512],
                                          start=(c == 0), stop=(c == KC - 1))
                        mm.then_inc(sems["tt"], 1)

            def f_vector(e):
                e.wait_ge(sems["tt"], tt0 + 1)
                VF.step(e, e.tensor_copy(bobc[:, 0:512], ps[2][:, 0:512]))
                VF.step(e, e.tensor_copy(bobc[:, 512:1024], ps[3][:, 0:512]))
                e.nop().then_inc(sems["vv"], 1)
                for t in range(NT):
                    e.wait_ge(sems["tt"], tt0 + 1 + 2 * t + 1)
                    if t >= 1:
                        e.wait_ge(sems["spa"], sa0 + 16 * t)
                        e.wait_ge(sems["ss"], ss0 + 4 * (t - 1) + 3)
                    VF.step(e, e.scalar_tensor_tensor(x1t[:, 0:512], ps[4][:], 1.0, bobc[:, 0:512],
                                                      AL.mult, AL.add, accum_out=stats[:, 0:1]))
                    e.wait_ge(sems["tt"], tt0 + 1 + 2 * t + 2)
                    VF.step(e, e.scalar_tensor_tensor(x1t[:, 512:1024], ps[5][:], 1.0, bobc[:, 512:1024],
                                                      AL.mult, AL.add, accum_out=stats[:, 12:13]))
                    VF.step(e, e.tensor_tensor(stats[:, 0:1], stats[:, 0:1], stats[:, 12:13], AL.add))
                    e.nop().then_inc(sems["vv"], 1)
                    VF.step(e, e.tensor_scalar(stats[:, 1:2], stats[:, 0:1], 1.0 / D, None, AL.mult, AL.bypass))
                    e.wait_ge(sems["ss"], ss0 + 4 * t + 1)
                    VF.step(e, e.tensor_scalar(stats[:, 3:4], stats[:, 2:3], 1.0 / D, None, AL.mult, AL.bypass))
                    VF.step(e, e.tensor_tensor(stats[:, 4:5], stats[:, 1:2], stats[:, 1:2], AL.mult))
                    VF.step(e, e.tensor_tensor(stats[:, 5:6], stats[:, 3:4], stats[:, 4:5], AL.subtract))
                    e.nop().then_inc(sems["vv"], 1)
                    e.wait_ge(sems["ss"], ss0 + 4 * t + 2)
                    VF.step(e, e.reciprocal(stats[:, 7:8], stats[:, 6:7]))
                    VF.step(e, e.tensor_scalar(stats[:, 8:9], stats[:, 5:6], EPS, None, AL.add, AL.bypass))
                    for _ in range(2):
                        VF.step(e, e.tensor_tensor(stats[:, 9:10], stats[:, 7:8], stats[:, 7:8], AL.mult))
                        VF.step(e, e.tensor_tensor(stats[:, 9:10], stats[:, 9:10], stats[:, 8:9], AL.mult))
                        VF.step(e, e.tensor_scalar(stats[:, 9:10], stats[:, 9:10], -0.5, 1.5, AL.mult, AL.add))
                        VF.step(e, e.tensor_tensor(stats[:, 7:8], stats[:, 7:8], stats[:, 9:10], AL.mult))
                    VF.step(e, e.tensor_scalar(x1t[:], x1t[:], stats[:, 1:2], stats[:, 7:8], AL.subtract, AL.mult))
                    sm_ = sems["lda"] if t % 2 == 0 else sems["ldb"]
                    e.wait_ge(sm_, (la0 if t % 2 == 0 else lb0) + 32 * (t // 2 + 1))
                    VF.step(e, e.scalar_tensor_tensor(x1t[:], x1t[:], 1.0, x0r[t % 2][:],
                                                      AL.mult, AL.add, accum_out=stats[:, 10:11]))
                    e.nop().then_inc(sems["vv"], 1)
                    VF.step(e, e.tensor_scalar(stats[:, 11:12], stats[:, 10:11], 1.0 / D, None, AL.mult, AL.bypass))
                    e.wait_ge(sems["ss"], ss0 + 4 * t + 3)
                    VF.step(e, e.tensor_scalar(stats[:, 13:14], stats[:, 12:13], 1.0 / D, None, AL.mult, AL.bypass))
                    VF.step(e, e.tensor_tensor(stats[:, 14:15], stats[:, 11:12], stats[:, 11:12], AL.mult))
                    VF.step(e, e.tensor_tensor(stats[:, 5:6], stats[:, 13:14], stats[:, 14:15], AL.subtract))
                    e.nop().then_inc(sems["vv"], 1)
                    e.wait_ge(sems["ss"], ss0 + 4 * t + 4)
                    VF.step(e, e.reciprocal(stats[:, 7:8], stats[:, 6:7]))
                    VF.step(e, e.tensor_scalar(stats[:, 8:9], stats[:, 5:6], EPS, None, AL.add, AL.bypass))
                    for _ in range(2):
                        VF.step(e, e.tensor_tensor(stats[:, 9:10], stats[:, 7:8], stats[:, 7:8], AL.mult))
                        VF.step(e, e.tensor_tensor(stats[:, 9:10], stats[:, 9:10], stats[:, 8:9], AL.mult))
                        VF.step(e, e.tensor_scalar(stats[:, 9:10], stats[:, 9:10], -0.5, 1.5, AL.mult, AL.add))
                        VF.step(e, e.tensor_tensor(stats[:, 7:8], stats[:, 7:8], stats[:, 9:10], AL.mult))
                    if t >= 1:
                        e.wait_ge(sems["spb"], sb0 + 16 * t)
                    VF.step(e, e.tensor_scalar(h2t[:], x1t[:], stats[:, 11:12], stats[:, 7:8], AL.subtract, AL.mult))
                    VF.step(e, e.reduce_max(mtmp[:, 0:1], h2t[:], axis=AX, apply_absolute_value=True))
                    VF.step(e, e.tensor_tensor(macc[:, 0:1], macc[:, 0:1], mtmp[:, 0:1], AL.max))
                    e.nop().then_inc(sems["vv"], 1)

            def f_scalar(e):
                for t in range(NT):
                    e.wait_ge(sems["vv"], vv0 + 1 + 5 * t + 1)
                    SF.step(e, e.activation(sqscr[:], x1t[:], ACT.Square, accum_out=stats[:, 2:3]))
                    e.nop().then_inc(sems["ss"], 1)
                    e.wait_ge(sems["vv"], vv0 + 1 + 5 * t + 2)
                    SF.step(e, e.activation(stats[:, 6:7], stats[:, 5:6], ACT.Sqrt, bias=epsv[:]))
                    e.nop().then_inc(sems["ss"], 1)
                    e.wait_ge(sems["vv"], vv0 + 1 + 5 * t + 3)
                    SF.step(e, e.activation(sqscr[:], x1t[:], ACT.Square, accum_out=stats[:, 12:13]))
                    e.nop().then_inc(sems["ss"], 1)
                    e.wait_ge(sems["vv"], vv0 + 1 + 5 * t + 4)
                    SF.step(e, e.activation(stats[:, 6:7], stats[:, 5:6], ACT.Sqrt, bias=epsv[:]))
                    e.nop().then_inc(sems["ss"], 1)

            def f_gpsimd(e):
                for t in range(NT):
                    e.wait_ge(sems["vv"], vv0 + 1 + 5 * t + 3)
                    e.dma_start(out=x1scr[t * 128:(t + 1) * 128, :], in_=x1t[:]).then_inc(sems["spa"], 16)
                    e.wait_ge(sems["vv"], vv0 + 1 + 5 * t + 5)
                    e.dma_start(out=h2scr[t * 128:(t + 1) * 128, :], in_=h2t[:]).then_inc(sems["spb"], 16)
                e.wait_ge(sems["spa"], sa0 + 16 * NT)
                e.wait_ge(sems["spb"], sb0 + 16 * NT)
            return {"sync": f_sync, "tensor": f_tensor, "vector": f_vector,
                    "scalar": f_scalar, "gpsimd": f_gpsimd}



        def ph_ar(idx, cols, slots, chain, nchain, bcast_cols):
            vv0, tt0, gg0, cc0 = C["vv"], C["tt"], C["gg"], C["cc"]
            sg0 = C["spg"]
            C["tt"] += 2
            C["vv"] += 3
            C["gg"] += 1
            C["spg"] += 32
            C["cc"] += 1

            def f_tensor(e):
                for i, col in enumerate(cols):
                    mm = e.transpose(ps[2][0:1, i * 128:(i + 1) * 128], macc[:, col:col + 1], ident_s[:])
                mm.then_inc(sems["tt"], 1)
                e.wait_ge(sems["vv"], vv0 + 2)
                e.matmul(ps[3][:, 0:bcast_cols], ones_s[:, 0:128], sc[0:1, 0:bcast_cols],
                         start=True, stop=True).then_inc(sems["tt"], 1)

            def f_vector(e):
                e.wait_ge(sems["tt"], tt0 + 1)
                VF.step(e, e.memset(arv_s[:], 0.0))
                for i, col in enumerate(cols):
                    VF.step(e, e.tensor_copy(mrow[:], ps[2][0:1, i * 128:(i + 1) * 128]))
                    VF.step(e, e.reduce_max(sc[0:1, 8 + i:9 + i], mrow[:], axis=AX))
                    VF.step(e, e.tensor_scalar(arv_s[0:1, slots[i]:slots[i] + 8],
                                               slot_s[0:1, slots[i]:slots[i] + 8],
                                               sc[0:1, 8 + i:9 + i], None, AL.mult, AL.bypass))
                e.nop().then_inc(sems["vv"], 1)
                e.wait_ge(sems["gg"], gg0 + 1)
                chain(e)
                e.nop().then_inc(sems["vv"], 1)
                e.wait_ge(sems["tt"], tt0 + 2)
                VF.step(e, e.tensor_copy(bc[:, 0:bcast_cols], ps[3][:, 0:bcast_cols]))
                e.nop().then_inc(sems["vv"], 1)

            def f_gpsimd(e):
                e.wait_ge(sems["vv"], vv0 + 1)
                e.dma_start(out=ccin[idx].ap(), in_=arv_s[:]).then_inc(sems["spg"], 16)
                e.wait_ge(sems["spg"], sg0 + 16)
                e.collective_compute(
                    "AllReduce", AL.add,
                    replica_groups=[list(range(n_cores))],
                    ins=[ccin[idx].ap().opt()],
                    outs=[ccout[idx].ap().opt()],
                ).then_inc(sems["cc"], 1)
                e.wait_ge(sems["cc"], cc0 + 1)
                e.dma_start(out=arr_s[:], in_=ccout[idx].ap()).then_inc(sems["spg"], 16)
                e.wait_ge(sems["spg"], sg0 + 32)
                e.nop().then_inc(sems["gg"], 1)
            return {"tensor": f_tensor, "vector": f_vector, "gpsimd": f_gpsimd}

        def ph_x():
            vv0, tt0 = C["vv"], C["tt"]
            la0, lb0 = C["lda"], C["ldb"]
            C["lda"] += 16 * (NT // 2)
            C["ldb"] += 16 * (NT // 2)
            C["vv"] += 2 * NT
            C["tt"] += 2 * NT

            def f_sync(e):
                for t in range(NT):
                    if t >= 2:
                        e.wait_ge(sems["vv"], vv0 + 2 * (t - 2) + 2)
                    sm_ = sems["lda"] if t % 2 == 0 else sems["ldb"]
                    e.dma_start(out=h2in[t % 2][:], in_=h2scr[t * 128:(t + 1) * 128, :]).then_inc(sm_, 16)

            def f_vector(e):
                for t in range(NT):
                    sm_ = sems["lda"] if t % 2 == 0 else sems["ldb"]
                    e.wait_ge(sm_, (la0 if t % 2 == 0 else lb0) + 16 * (t // 2 + 1))
                    VF.step(e, e.tensor_scalar(h2in[t % 2][:], h2in[t % 2][:], bc[:, 0:1], MAGIC, AL.mult, AL.add))
                    VF.step(e, e.tensor_scalar(xh[:], h2in[t % 2][:], MAGIC, None, AL.subtract, AL.bypass))
                    VF.step(e, e.tensor_scalar(h2in[t % 2][:], h2in[t % 2][:], MAGIC, None, AL.subtract, AL.bypass))
                    VF.step(e, e.scalar_tensor_tensor(xl[:], h2in[t % 2][:], 1.0, xh[:],
                                                      AL.mult, AL.subtract))
                    e.nop().then_inc(sems["vv"], 1)
                    for half, (dstt, _) in enumerate(((xfmh, 0), (xfml, 0))):
                        e.wait_ge(sems["tt"], tt0 + 2 * t + half + 1)
                        dst = bass.AP(dstt, t * 128, [[KC * TOK, 128], [TOK, KC], [1, 128]])
                        VF.step(e, e.tensor_copy(dst, ps[half][:].bitcast(mybir.dt.bfloat16)))
                    e.nop().then_inc(sems["vv"], 1)

            def f_tensor(e):
                for t in range(NT):
                    e.wait_ge(sems["vv"], vv0 + 2 * t + 1)
                    if t >= 1:
                        e.wait_ge(sems["vv"], vv0 + 2 * t)
                    for half, src in enumerate((xh, xl)):
                        pb = ps[half][:].bitcast(mybir.dt.bfloat16)
                        for c in range(KC):
                            mm = e.transpose(pb[:, c * 128:(c + 1) * 128],
                                             src[:, c * 128:(c + 1) * 128], ident_b[:])
                        mm.then_inc(sems["tt"], 1)
            return {"sync": f_sync, "vector": f_vector, "tensor": f_tensor}

        def ph_upgate():
            vv0, ss0, tt0 = C["vv"], C["ss"], C["tt"]
            la0, lb0, sa0, sb0 = C["lda"], C["ldb"], C["spa"], C["spb"]
            mats = [("u", wupT, u0scr, 1), ("g", wgT, g0scr, 2)]
            NI = 2 * IT
            C["lda"] += 32 * (NI // 2)
            C["ldb"] += 32 * (NI // 2)
            C["tt"] += NI * TB
            C["ss"] += NI * TB
            C["vv"] += NI * (TB * 2 + 1)
            C["spa"] += 16 * (NI // 2)
            C["spb"] += 16 * (NI // 2)

            def f_sync(e):
                for mi, (kind, wsrc, dscr, mcol) in enumerate(mats):
                    for r in range(IT):
                        i = mi * IT + r
                        if i >= 2:
                            e.wait_ge(sems["tt"], tt0 + TB * (i - 1))
                        sm_ = sems["lda"] if i % 2 == 0 else sems["ldb"]
                        for j, dstw in ((0, wT7h), (1, wT7l)):
                            e.dma_start(out=dstw[i % 2][:],
                                        in_=bass.AP(wsrc[j].tensor, wsrc[j].offset + r * KC * 128 * 128,
                                                    [[128, 128], [128 * 128, KC], [1, 128]])).then_inc(sm_, 16)

            def f_tensor(e):
                for mi, (kind, wsrc, dscr, mcol) in enumerate(mats):
                    for r in range(IT):
                        i = mi * IT + r
                        sm_ = sems["lda"] if i % 2 == 0 else sems["ldb"]
                        e.wait_ge(sm_, (la0 if i % 2 == 0 else lb0) + 32 * (i // 2 + 1))
                        for tb in range(TB):
                            g = i * TB + tb
                            if g >= 2:
                                e.wait_ge(sems["ss"], ss0 + g - 1)
                            p = ps[g % 2]
                            for c in range(KC):
                                xs_h = xfmh[:, c * TOK + tb * 512: c * TOK + (tb + 1) * 512]
                                xs_l = xfml[:, c * TOK + tb * 512: c * TOK + (tb + 1) * 512]
                                e.matmul(p[:], wT7h[i % 2][:, c * 128:(c + 1) * 128], xs_h,
                                         start=(c == 0), stop=False)
                                e.matmul(p[:], wT7l[i % 2][:, c * 128:(c + 1) * 128], xs_h,
                                         start=False, stop=False)
                                mm = e.matmul(p[:], wT7h[i % 2][:, c * 128:(c + 1) * 128], xs_l,
                                              start=False, stop=(c == KC - 1))
                            mm.then_inc(sems["tt"], 1)

            def f_scalar(e):
                for i in range(NI):
                    if i >= 2:
                        pit = i - 2
                        sm_ = sems["spa"] if pit % 2 == 0 else sems["spb"]
                        e.wait_ge(sm_, (sa0 if pit % 2 == 0 else sb0) + 16 * (pit // 2 + 1))
                    for tb in range(TB):
                        g = i * TB + tb
                        e.wait_ge(sems["tt"], tt0 + g + 1)
                        SF.step(e, e.activation(u0j[i % 2][:, tb * 512:(tb + 1) * 512], ps[g % 2][:],
                                                ACT.Copy, bias=0.0, scale=1.0))
                        e.nop().then_inc(sems["ss"], 1)

            def f_vector(e):
                for mi, (kind, wsrc, dscr, mcol) in enumerate(mats):
                    for r in range(IT):
                        i = mi * IT + r
                        for tb in range(TB):
                            g = i * TB + tb
                            e.wait_ge(sems["ss"], ss0 + g + 1)
                            VF.step(e, e.reduce_max(mtmp[:, 1:2], u0j[i % 2][:, tb * 512:(tb + 1) * 512],
                                                    axis=AX, apply_absolute_value=True))
                            e.nop().then_inc(sems["vv"], 1)
                            VF.step(e, e.tensor_tensor(macc[:, mcol:mcol + 1], macc[:, mcol:mcol + 1],
                                                       mtmp[:, 1:2], AL.max))
                            e.nop().then_inc(sems["vv"], 1)
                        e.nop().then_inc(sems["vv"], 1)

            def f_gpsimd(e):
                for mi, (kind, wsrc, dscr, mcol) in enumerate(mats):
                    for r in range(IT):
                        i = mi * IT + r
                        e.wait_ge(sems["vv"], vv0 + (TB * 2 + 1) * (i + 1))
                        sm_ = sems["spa"] if i % 2 == 0 else sems["spb"]
                        e.dma_start(out=dscr[r], in_=u0j[i % 2][:]).then_inc(sm_, 16)
                e.wait_ge(sems["spa"], sa0 + 16 * (NI // 2))
                e.wait_ge(sems["spb"], sb0 + 16 * (NI // 2))
            return {"sync": f_sync, "tensor": f_tensor, "scalar": f_scalar,
                    "vector": f_vector, "gpsimd": f_gpsimd}

        def ph_ffnA():
            vv0, ss0 = C["vv"], C["ss"]
            ld0 = (C["lda"], C["ldb"], C["ldg"])
            ldn = ("lda", "ldb", "ldg")
            C["lda"] += 16 * 8
            C["ldb"] += 16 * 8
            C["ldg"] += 16 * 8
            C["ss"] += 2 * IT
            C["vv"] += 2 * IT
            ss_p1g = {r: ss0 + 2 * r + 1 for r in range(IT)}
            ss_sp1 = {r: ss0 + 2 * r + 2 for r in range(IT)}
            vv_gq = {}; vv_dn = {}
            vvc = vv0
            for r in range(IT):
                vvc += 1; vv_gq[r] = vvc
                if r >= 1:
                    vvc += 1; vv_dn[r - 1] = vvc
            vvc += 1; vv_dn[IT - 1] = vvc
            assert vvc == vv0 + 2 * IT

            def f_sync(e):
                for r in range(IT):
                    if r >= 3:
                        e.wait_ge(sems["vv"], vv_dn[r - 3])
                    e.dma_start(out=jb[r % 3][:], in_=g0scr[r]).then_inc(sems[ldn[r % 3]], 16)

            def f_scalar(e):
                for r in range(IT):
                    e.wait_ge(sems[ldn[r % 3]], ld0[r % 3] + 16 * (r // 3 + 1))
                    if r >= 3:
                        e.wait_ge(sems["vv"], vv_dn[r - 3])
                    SF.step(e, e.activation(jb[3 + r % 3][:], jb[r % 3][:], ACT.Copy,
                                            bias=MAGIC, scale=bc[:, 3:4]))
                    e.nop().then_inc(sems["ss"], 1)
                    e.wait_ge(sems["vv"], vv_gq[r])
                    if r >= 2:
                        e.wait_ge(sems["vv"], vv_dn[r - 2])
                    SF.step(e, e.activation(jb[6 + r % 2][:], jb[3 + r % 3][:], ACT.Sigmoid))
                    SF.step(e, e.activation(jb[6 + r % 2][:], jb[6 + r % 2][:], ACT.Copy,
                                            bias=MAGIC, scale=65536.0))
                    e.nop().then_inc(sems["ss"], 1)

            def f_vector(e):
                def tail(p):
                    e.wait_ge(sems["ss"], ss_sp1[p])
                    VF.step(e, e.tensor_scalar(jb[6 + p % 2][:], jb[6 + p % 2][:], MAGIC,
                                               65535.0, AL.subtract, AL.min))
                    VF.step(e, e.tensor_tensor(jb[6 + p % 2][:], jb[6 + p % 2][:],
                                               jb[3 + p % 3][:], AL.mult))
                    VF.step(e, e.reduce_max(mtmp[:, 1:2], jb[6 + p % 2][:], axis=AX,
                                            apply_absolute_value=True))
                    VF.step(e, e.tensor_tensor(macc[:, 3:4], macc[:, 3:4], mtmp[:, 1:2], AL.max))
                    e.nop().then_inc(sems["vv"], 1)
                for r in range(IT):
                    e.wait_ge(sems["ss"], ss_p1g[r])
                    VF.step(e, e.tensor_scalar(jb[3 + r % 3][:], jb[3 + r % 3][:], MAGIC,
                                               bc[:, 4:5], AL.subtract, AL.mult))
                    e.nop().then_inc(sems["vv"], 1)
                    if r >= 1:
                        tail(r - 1)
                tail(IT - 1)
            return {"sync": f_sync, "scalar": f_scalar, "vector": f_vector}

        def ph_ffnB():
            vv0, ss0 = C["vv"], C["ss"]
            sa0, sb0 = C["spa"], C["spb"]
            ld0 = (C["lda"], C["ldb"], C["ldg"])
            ldn = ("lda", "ldb", "ldg")
            C["lda"] += 32 * 8
            C["ldb"] += 32 * 8
            C["ldg"] += 32 * 8
            C["ss"] += 4 * IT
            C["vv"] += 3 * IT
            C["spa"] += 16 * (IT - IT // 2)
            C["spb"] += 16 * (IT // 2)
            ss_p1g = {}; ss_p1u = {}; ss_sp1 = {}; ss_p1t = {}
            ssc = ss0
            for r in range(IT):
                ssc += 1; ss_p1g[r] = ssc
                ssc += 1; ss_p1u[r] = ssc
                ssc += 1; ss_sp1[r] = ssc
                if r >= 1:
                    ssc += 1; ss_p1t[r - 1] = ssc
            ssc += 1; ss_p1t[IT - 1] = ssc
            assert ssc == ss0 + 4 * IT
            vv_gq = {}; vv_t0 = {}; vv_op = {}
            vvc = vv0
            for r in range(IT + 2):
                if r < IT:
                    vvc += 1; vv_gq[r] = vvc
                if 1 <= r < IT + 1:
                    vvc += 1; vv_t0[r - 1] = vvc
                if r >= 2:
                    vvc += 1; vv_op[r - 2] = vvc
            assert vvc == vv0 + 3 * IT
            G3 = [jb[0], jb[1], jbx[0]]
            U3 = [jb[2], jb[3], jbx[1]]

            def f_sync(e):
                for r in range(IT):
                    if r >= 3:
                        e.wait_ge(sems["vv"], vv_op[r - 3])
                        sp_ = sems["spa"] if (r - 3) % 2 == 0 else sems["spb"]
                        e.wait_ge(sp_, (sa0 if (r - 3) % 2 == 0 else sb0) + 16 * ((r - 3) // 2 + 1))
                    sm_ = sems[ldn[r % 3]]
                    e.dma_start(out=G3[r % 3][:], in_=g0scr[r]).then_inc(sm_, 16)
                    e.dma_start(out=U3[r % 3][:], in_=u0scr[r]).then_inc(sm_, 16)

            def f_scalar(e):
                for r in range(IT):
                    e.wait_ge(sems[ldn[r % 3]], ld0[r % 3] + 32 * (r // 3 + 1))
                    if r >= 2:
                        e.wait_ge(sems["vv"], vv_t0[r - 2])
                    SF.step(e, e.activation(jb[4 + r % 2][:], G3[r % 3][:], ACT.Copy,
                                            bias=MAGIC, scale=bc[:, 3:4]))
                    e.nop().then_inc(sems["ss"], 1)
                    SF.step(e, e.activation(U3[r % 3][:], U3[r % 3][:], ACT.Copy,
                                            bias=MAGIC, scale=bc[:, 1:2]))
                    e.nop().then_inc(sems["ss"], 1)
                    e.wait_ge(sems["vv"], vv_gq[r])
                    if r >= 2:
                        e.wait_ge(sems["vv"], vv_op[r - 2])
                    SF.step(e, e.activation(jb[6 + r % 2][:], jb[4 + r % 2][:], ACT.Sigmoid))
                    SF.step(e, e.activation(jb[6 + r % 2][:], jb[6 + r % 2][:], ACT.Copy,
                                            bias=MAGIC, scale=65536.0))
                    e.nop().then_inc(sems["ss"], 1)
                    if r >= 1:
                        e.wait_ge(sems["vv"], vv_t0[r - 1])
                        SF.step(e, e.activation(jb[6 + (r - 1) % 2][:], jb[6 + (r - 1) % 2][:],
                                                ACT.Copy, bias=MAGIC, scale=bc[:, 5:6]))
                        e.nop().then_inc(sems["ss"], 1)
                e.wait_ge(sems["vv"], vv_t0[IT - 1])
                SF.step(e, e.activation(jb[6 + (IT - 1) % 2][:], jb[6 + (IT - 1) % 2][:],
                                        ACT.Copy, bias=MAGIC, scale=bc[:, 5:6]))
                e.nop().then_inc(sems["ss"], 1)

            def f_vector(e):
                def tail1(p):
                    e.wait_ge(sems["ss"], ss_sp1[p])
                    VF.step(e, e.tensor_scalar(jb[6 + p % 2][:], jb[6 + p % 2][:], MAGIC,
                                               65535.0, AL.subtract, AL.min))
                    VF.step(e, e.tensor_tensor(jb[6 + p % 2][:], jb[6 + p % 2][:],
                                               jb[4 + p % 2][:], AL.mult))
                    e.nop().then_inc(sems["vv"], 1)
                    e.wait_ge(sems["ss"], ss_p1u[p])
                    VF.step(e, e.tensor_scalar(U3[p % 3][:], U3[p % 3][:], MAGIC,
                                               bc[:, 2:3], AL.subtract, AL.mult))

                def tail2(p):
                    e.wait_ge(sems["ss"], ss_p1t[p])
                    VF.step(e, e.scalar_tensor_tensor(G3[p % 3][:], jb[6 + p % 2][:], MAGIC,
                                                      U3[p % 3][:], AL.subtract, AL.mult))
                    VF.step(e, e.reduce_max(mtmp[:, 1:2], G3[p % 3][:], axis=AX,
                                            apply_absolute_value=True))
                    VF.step(e, e.tensor_tensor(macc[:, 3:4], macc[:, 3:4], mtmp[:, 1:2], AL.max))
                    e.nop().then_inc(sems["vv"], 1)
                for r in range(IT + 2):
                    if r < IT:
                        e.wait_ge(sems["ss"], ss_p1g[r])
                        VF.step(e, e.tensor_scalar(jb[4 + r % 2][:], jb[4 + r % 2][:], MAGIC,
                                                   bc[:, 4:5], AL.subtract, AL.mult))
                        e.nop().then_inc(sems["vv"], 1)
                    if 1 <= r < IT + 1:
                        tail1(r - 1)
                    if r >= 2:
                        tail2(r - 2)

            def f_gpsimd(e):
                for r in range(IT):
                    e.wait_ge(sems["vv"], vv_op[r])
                    sp_ = sems["spa"] if r % 2 == 0 else sems["spb"]
                    if r >= 2:
                        e.wait_ge(sp_, (sa0 if r % 2 == 0 else sb0) + 16 * (r // 2))
                    e.dma_start(out=opscr[r], in_=G3[r % 3][:]).then_inc(sp_, 16)
                e.wait_ge(sems["spa"], sa0 + 16 * (IT - IT // 2))
                e.wait_ge(sems["spb"], sb0 + 16 * (IT // 2))
            return {"sync": f_sync, "scalar": f_scalar, "vector": f_vector, "gpsimd": f_gpsimd}

        def ph_down():
            vv0, ss0, tt0 = C["vv"], C["ss"], C["tt"]
            la0, lb0, g0, sg0 = C["lda"], C["ldb"], C["ldg"], C["spg"]
            RNG = 8
            C["lda"] += 16 * 12 * TB
            C["ldb"] += 16 * 12 * TB
            C["ldg"] += 16 * (2 + 4 * TB)
            DCTX["wds_ldg"] = g0 + 32
            C["spg"] += 16 * 4 * TB
            tt_mm = {}; tt_tr = {}
            ttc = tt0
            for tb in range(TB):
                for r in range(IT):
                    ttc += 1; tt_mm[(tb, r)] = ttc
                for j in range(4):
                    ttc += 1; tt_tr[(tb, j)] = ttc
            C["tt"] = ttc
            ss_p1o = {}; ss_sq = {}; ss_sqrt = {}
            ssc = ss0
            for tb in range(TB + 1):
                if tb < TB:
                    for r in range(0, 8):
                        ssc += 1; ss_p1o[(tb, r)] = ssc
                if tb >= 1:
                    for j in range(4):
                        ssc += 1; ss_sq[(tb - 1, j)] = ssc
                        ssc += 1; ss_sqrt[(tb - 1, j)] = ssc
                if tb < TB:
                    for r in range(8, 24):
                        ssc += 1; ss_p1o[(tb, r)] = ssc
            C["ss"] = ssc
            vv_evac = {}; vv_slice = {}; vv_trev = {}; vv_var = {}; vv_out = {}
            vvc = vv0
            for tb in range(TB + 1):
                if tb >= 1:
                    for dt in range(KC):
                        vvc += 1; vv_evac[(tb - 1, dt)] = vvc
                if tb < TB:
                    for r in range(0, 8):
                        vvc += 1; vv_slice[(tb, r)] = vvc
                if tb >= 1:
                    for j in range(4):
                        vvc += 1; vv_trev[(tb - 1, j)] = vvc
                        vvc += 1; vv_var[(tb - 1, j)] = vvc
                        vvc += 1; vv_out[(tb - 1, j)] = vvc
                if tb < TB:
                    for r in range(8, 24):
                        vvc += 1; vv_slice[(tb, r)] = vvc
            C["vv"] = vvc

            def prev_slot_user(tb, r):
                if r >= RNG:
                    return (tb, r - RNG)
                if tb >= 1:
                    return (tb - 1, IT - RNG + r)
                return None

            def f_sync(e):
                for j, dstw in ((0, wdSh), (1, wdSl)):
                    dst = bass.AP(dstw, 0, [[KC * IT * 128, 128], [128, KC * IT], [1, 128]])
                    src = bass.AP(wdT[j].tensor, wdT[j].offset, [[128, 128], [128 * 128, KC * IT], [1, 128]])
                    e.dma_start(out=dst, in_=src).then_inc(sems["ldg"], 16)
                for tb in range(TB):
                    for r in range(IT):
                        gi = tb * IT + r
                        if gi >= 2:
                            e.wait_ge(sems["vv"], vv_slice[divmod(gi - 2, IT)])
                        sm_ = sems["lda"] if r % 2 == 0 else sems["ldb"]
                        e.dma_start(out=opin[r % 2][:],
                                    in_=bass.AP(opscr.tensor, opscr.offset + r * 128 * TOK + tb * 512,
                                                [[TOK, 128], [1, 512]])).then_inc(sm_, 16)
                    for j in range(4):
                        tj = tb * 4 + j
                        if tj >= 4:
                            e.wait_ge(sems["vv"], vv_out[divmod(tj - 4, 4)])
                        e.wait_ge(sems["ldg"], g0 + 16 * (2 + tj))
                        e.dma_start(out=x1r[j][:],
                                    in_=x1scr[tb * 512 + j * 128: tb * 512 + (j + 1) * 128, :]).then_inc(sems["ldg"], 16)

            def f_scalar(e):
                def p1o(tb, r):
                    sm_ = sems["lda"] if r % 2 == 0 else sems["ldb"]
                    e.wait_ge(sm_, (la0 if r % 2 == 0 else lb0) + 16 * (tb * 12 + r // 2 + 1))
                    SF.step(e, e.activation(opin[r % 2][:], opin[r % 2][:], ACT.Copy,
                                            bias=MAGIC, scale=bc[:, 7:8]))
                    e.nop().then_inc(sems["ss"], 1)

                def lnact(tbp, j):
                    e.wait_ge(sems["vv"], vv_trev[(tbp, j)])
                    SF.step(e, e.activation(sqd[:], ffnt[:], ACT.Square, accum_out=stats[:, 2:3]))
                    e.nop().then_inc(sems["ss"], 1)
                    e.wait_ge(sems["vv"], vv_var[(tbp, j)])
                    SF.step(e, e.activation(stats[:, 6:7], stats[:, 5:6], ACT.Sqrt, bias=epsv[:]))
                    e.nop().then_inc(sems["ss"], 1)
                for tb in range(TB + 1):
                    if tb < TB:
                        for r in range(0, 8):
                            p1o(tb, r)
                    if tb >= 1:
                        for j in range(4):
                            lnact(tb - 1, j)
                    if tb < TB:
                        for r in range(8, 24):
                            p1o(tb, r)

            def f_tensor(e):
                e.wait_ge(sems["ldg"], DCTX["wds_ldg"])
                for tb in range(TB):
                    for r in range(IT):
                        e.wait_ge(sems["vv"], vv_slice[(tb, r)])
                        sl = r % RNG
                        oh_ = oslh[:, sl * 512:(sl + 1) * 512]
                        ol_ = osll[:, sl * 512:(sl + 1) * 512]
                        for dt in range(KC):
                            if r == 0 and tb >= 1:
                                e.wait_ge(sems["vv"], vv_evac[(tb - 1, dt)])
                                if dt in (4, 5):
                                    e.wait_ge(sems["vv"], vv_trev[(tb - 1, 3)])
                            wh = wdSh[:, (dt * IT + r) * 128:(dt * IT + r + 1) * 128]
                            wl = wdSl[:, (dt * IT + r) * 128:(dt * IT + r + 1) * 128]
                            e.matmul(ps[dt][:], wh, oh_, start=(r == 0), stop=False)
                            e.matmul(ps[dt][:], wl, oh_, start=False, stop=False)
                            mm = e.matmul(ps[dt][:], wh, ol_, start=False, stop=(r == IT - 1))
                        mm.then_inc(sems["tt"], 1)
                    for j in range(4):
                        e.wait_ge(sems["vv"], vv_evac[(tb, KC - 1)])
                        if j >= 1:
                            e.wait_ge(sems["vv"], vv_trev[(tb, j - 1)])
                        for g2 in range(2):
                            for k2 in range(4):
                                dt = g2 * 4 + k2
                                mm = e.transpose(ps[4 + g2][:, k2 * 128:(k2 + 1) * 128],
                                                 ffm[:, dt * 512 + j * 128: dt * 512 + j * 128 + 128],
                                                 ident_s[:])
                        mm.then_inc(sems["tt"], 1)

            def f_vector(e):
                def c_slice(tb, r):
                    e.wait_ge(sems["ss"], ss_p1o[(tb, r)])
                    prev = prev_slot_user(tb, r)
                    if prev is not None:
                        e.wait_ge(sems["tt"], tt_mm[prev])
                    sl = r % RNG
                    q = r % 2
                    VF.step(e, e.tensor_scalar(oslh[:, sl * 512:(sl + 1) * 512], opin[q][:],
                                               MAGIC, None, AL.subtract, AL.bypass))
                    VF.step(e, e.tensor_scalar(opin[q][:], opin[q][:], MAGIC, None,
                                               AL.subtract, AL.bypass))
                    VF.step(e, e.scalar_tensor_tensor(osll[:, sl * 512:(sl + 1) * 512], opin[q][:],
                                                      1.0, oslh[:, sl * 512:(sl + 1) * 512],
                                                      AL.mult, AL.subtract))
                    e.nop().then_inc(sems["vv"], 1)

                def evacs(tbp):
                    e.wait_ge(sems["tt"], tt_mm[(tbp, IT - 1)])
                    if tbp >= 1:
                        e.wait_ge(sems["tt"], tt_tr[(tbp - 1, 3)])
                    for dt in range(KC):
                        VF.step(e, e.tensor_scalar(ffm[:, dt * 512:(dt + 1) * 512], ps[dt][:],
                                                   bc[:, 8:9], None, AL.mult, AL.bypass))
                        e.nop().then_inc(sems["vv"], 1)

                def ln_j(tbp, j):
                    tj = tbp * 4 + j
                    e.wait_ge(sems["tt"], tt_tr[(tbp, j)])
                    VF.step(e, e.tensor_copy(ffnt[:, 0:512], ps[4][:]))
                    VF.step(e, e.tensor_copy(ffnt[:, 512:1024], ps[5][:]))
                    e.nop().then_inc(sems["vv"], 1)
                    ln_stats_vec(e, ffnt[:], 0, ss_sq[(tbp, j)], ss_sqrt[(tbp, j)])
                    VF.step(e, e.tensor_scalar(ffnt[:], ffnt[:], stats[:, 1:2], stats[:, 7:8],
                                               AL.subtract, AL.mult))
                    e.wait_ge(sems["ldg"], g0 + 16 * (2 + tj + 1))
                    if tj >= 1:
                        e.wait_ge(sems["spg"], sg0 + 16 * tj)
                    VF.step(e, e.tensor_tensor(outt[:], ffnt[:], x1r[j][:], AL.add))
                    e.nop().then_inc(sems["vv"], 1)
                for tb in range(TB + 1):
                    if tb >= 1:
                        evacs(tb - 1)
                    if tb < TB:
                        for r in range(0, 8):
                            c_slice(tb, r)
                    if tb >= 1:
                        for j in range(4):
                            ln_j(tb - 1, j)
                    if tb < TB:
                        for r in range(8, 24):
                            c_slice(tb, r)

            def f_gpsimd(e):
                for tb in range(TB):
                    for j in range(4):
                        e.wait_ge(sems["vv"], vv_out[(tb, j)])
                        r0 = tb * 512 + j * 128
                        e.dma_start(out=out[r0:r0 + 128, :], in_=outt[:]).then_inc(sems["spg"], 16)
                e.wait_ge(sems["spg"], sg0 + 16 * TB * 4)
            return {"sync": f_sync, "tensor": f_tensor, "vector": f_vector,
                    "scalar": f_scalar, "gpsimd": f_gpsimd}

        add(ph_consts())
        add(ph_ln1())
        add(ph_qkv())
        add(ph_wdq())
        add(ph_attn())
        add(ph_wo())

        def chain1(e):
            VF.step(e, e.reduce_max(sc[0:1, 12:13], arr_s[0:1, 0:8], axis=AX))
            VF.step(e, e.tensor_scalar(sc[0:1, 13:14], sc[0:1, 12:13], CQ, QEPS, AL.mult, AL.max))
            VF.step(e, e.reciprocal(sc[0:1, 0:1], sc[0:1, 13:14]))
        add(ph_ar(0, [0], [0], chain1, 3, 1))

        add(ph_x())
        add(ph_upgate())

        def chain2(e):
            VF.step(e, e.reduce_max(sc[0:1, 12:13], arr_s[0:1, 0:8], axis=AX))
            VF.step(e, e.tensor_tensor(sc[0:1, 12:13], sc[0:1, 12:13], sc[0:1, 13:14], AL.mult))
            VF.step(e, e.tensor_scalar(sc[0:1, 2:3], sc[0:1, 12:13], CQ, QEPS, AL.mult, AL.max))
            VF.step(e, e.reciprocal(sc[0:1, 14:15], sc[0:1, 2:3]))
            VF.step(e, e.tensor_tensor(sc[0:1, 1:2], sc[0:1, 14:15], sc[0:1, 13:14], AL.mult))
            VF.step(e, e.reduce_max(sc[0:1, 12:13], arr_s[0:1, 8:16], axis=AX))
            VF.step(e, e.tensor_tensor(sc[0:1, 12:13], sc[0:1, 12:13], sc[0:1, 13:14], AL.mult))
            VF.step(e, e.tensor_scalar(sc[0:1, 4:5], sc[0:1, 12:13], CQ, QEPS, AL.mult, AL.max))
            VF.step(e, e.reciprocal(sc[0:1, 14:15], sc[0:1, 4:5]))
            VF.step(e, e.tensor_tensor(sc[0:1, 3:4], sc[0:1, 14:15], sc[0:1, 13:14], AL.mult))
        add(ph_ar(1, [1, 2], [0, 8], chain2, 10, 5))

        add(ph_ffnA())

        def chain3(e):
            VF.step(e, e.reduce_max(sc[0:1, 12:13], arr_s[0:1, 0:8], axis=AX))
            VF.step(e, e.tensor_scalar(sc[0:1, 12:13], sc[0:1, 12:13], 1.0 / 65536.0, None, AL.mult, AL.bypass))
            VF.step(e, e.tensor_scalar(sc[0:1, 6:7], sc[0:1, 12:13], CQ, QEPS, AL.mult, AL.max))
            VF.step(e, e.reciprocal(sc[0:1, 14:15], sc[0:1, 6:7]))
            VF.step(e, e.tensor_scalar(sc[0:1, 5:6], sc[0:1, 14:15], 1.0 / 65536.0, None, AL.mult, AL.bypass))
        def chain3z(e):
            chain3(e)
            VF.step(e, e.memset(macc[:, 3:4], 0.0))
        add(ph_ar(2, [3], [0], chain3z, 6, 7))

        add(ph_ffnB())

        def chain4(e):
            VF.step(e, e.reduce_max(sc[0:1, 12:13], arr_s[0:1, 0:8], axis=AX))
            VF.step(e, e.tensor_tensor(sc[0:1, 12:13], sc[0:1, 12:13], sc[0:1, 6:7], AL.mult))
            VF.step(e, e.tensor_scalar(sc[0:1, 8:9], sc[0:1, 12:13], CQ, QEPS, AL.mult, AL.max))
            VF.step(e, e.reciprocal(sc[0:1, 14:15], sc[0:1, 8:9]))
            VF.step(e, e.tensor_tensor(sc[0:1, 7:8], sc[0:1, 14:15], sc[0:1, 6:7], AL.mult))
        add(ph_ar(3, [3], [0], chain4, 5, 9))

        add(ph_down())

        for ph_ in phases:
            for en in ENGS:
                fn = ph_.get(en)
                if fn is not None:
                    getattr(block, en)(fn)
    return nc


def kernel(L0_attn_hidden_states, L0_attn_Wq, L0_attn_Wkv, L0_attn_Wo, L0_attn_bo,
           L0_ffn_w_gate, L0_ffn_w_up, L0_ffn_w_down):
    n = 8
    x = np.ascontiguousarray(np.asarray(L0_attn_hidden_states, dtype=np.float32))
    xs = x.reshape(8, TOK, D)
    nc = build_nc(n)
    base = {
        "wq": np.ascontiguousarray(np.asarray(L0_attn_Wq, dtype=np.float32)),
        "wkv": np.ascontiguousarray(np.asarray(L0_attn_Wkv, dtype=np.float32)),
        "wo": np.ascontiguousarray(np.asarray(L0_attn_Wo, dtype=np.float32)),
        "bo": np.ascontiguousarray(np.asarray(L0_attn_bo, dtype=np.float32)).reshape(1, D),
        "wup": np.ascontiguousarray(np.asarray(L0_ffn_w_up, dtype=np.float32)),
        "wgate": np.ascontiguousarray(np.asarray(L0_ffn_w_gate, dtype=np.float32)),
        "wdown": np.ascontiguousarray(np.asarray(L0_ffn_w_down, dtype=np.float32)),
        "identity": np.eye(128, dtype=np.float32),
        "onesrow": np.ones((1, 128), np.float32),
        "negrow": -np.ones((1, 128), np.float32),
    }
    in_maps = []
    for c in range(n):
        m = dict(base)
        m["x0"] = xs[c]
        sm = np.zeros((1, 16), np.float32)
        sm[0, c] = 1.0
        sm[0, 8 + c] = 1.0
        m["slotm"] = sm
        in_maps.append(m)
    res = run_bass_kernel_spmd(nc, in_maps, core_ids=list(range(n)))
    outp = np.concatenate([np.asarray(r["out"])[None] for r in res.results], axis=0)
    return outp.reshape(16, S, D)
